# version 10
# baseline (speedup 1.0000x reference)
import contextlib
import os
import numpy as np
import concourse.bass as bass
import concourse.mybir as mybir
from concourse.bass_utils import run_bass_kernel_spmd

F32 = mybir.dt.float32
BF16 = mybir.dt.bfloat16
I16 = mybir.dt.int16
U16 = mybir.dt.uint16
I32 = mybir.dt.int32
AF = mybir.ActivationFunctionType
ALU = mybir.AluOpType
AX = mybir.AxisListType

NCORES = 8
D = 1024
T = 2048
NCH = 8
NH = 16
HD = 64
IFF0 = 3072
HFF1 = 4096
NE = 3
NR = 8
NU = 40
NS = 45
QMAX = 32767.0
LN_EPS = 1e-5
SCALE = 0.125

ENGINES = ("tensor", "vector", "scalar", "sync", "gpsimd")
DMA_SEM = {"sync": "dS", "gpsimd": "dG", "scalar": "dC"}


class Bld:

    def __init__(self):
        self.nc = bass.Bass("TRN2", target_bir_lowering=False, debug=(os.environ.get("DBG_PROBE") == "1"))
        self.q = {e: [] for e in ENGINES}
        self.sems = {}
        self.cnt = {}
        self.inputs = []
        self.es = contextlib.ExitStack()
        self.dram = {}
        self.dirty = {}
        self.dma_sems = set()

    def din(self, name, shape, dtype):
        if name not in self.dram:
            self.dram[name] = self.nc.dram_tensor(name, shape, dtype, kind="ExternalInput").ap()
            self.inputs.append(name)
        return self.dram[name]

    def dout(self, name, shape, dtype):
        self.dram[name] = self.nc.dram_tensor(name, shape, dtype, kind="ExternalOutput").ap()
        return self.dram[name]

    def dtmp(self, name, shape, dtype):
        if name not in self.dram:
            self.dram[name] = self.nc.dram_tensor(name, shape, dtype, kind="Internal").ap()
        return self.dram[name]

    def sb(self, name, shape, dtype, stack=None):
        return (stack or self.es).enter_context(self.nc.sbuf_tensor(name, shape, dtype))

    def ps(self, name, shape, dtype=F32, stack=None):
        return (stack or self.es).enter_context(self.nc.psum_tensor(name, shape, dtype))

    def sem(self, name):
        if name not in self.sems:
            self.sems[name] = self.nc.alloc_semaphore(name)
            self.cnt[name] = 0
        return self.sems[name]

    def val(self, name):
        self.sem(name)
        return self.cnt[name]

    def on(self, eng, fn):
        self.q[eng].append(fn)

    DRAIN_ENGINES = ("vector", "scalar", "gpsimd")

    def _predrain(self, eng):
        if eng in self.DRAIN_ENGINES and self.dirty.get(eng):
            self.dirty[eng] = False
            self.on(eng, lambda e: e.drain())

    def op(self, eng, fn, sem=None, k=1):
        self._predrain(eng)
        if eng in self.DRAIN_ENGINES:
            self.dirty[eng] = True
        if sem is None:
            self.on(eng, lambda e, fn=fn: fn(e))
            return None
        h = self.sem(sem)
        self.cnt[sem] += k
        tgt = self.cnt[sem]
        if eng == "tensor":
            self.on(eng, lambda e, fn=fn: fn(e))
            self.on(eng, lambda e, h=h, k=k: e.drain().then_inc(h, k))
        else:
            self.on(eng, lambda e, fn=fn, h=h, k=k: fn(e).then_inc(h, k))
        return tgt

    def dma(self, eng, out, in_, sem=None, **kw):
        self._predrain(eng)
        if sem is None:
            sem = DMA_SEM[eng]
        self.dma_sems.add(sem)
        h = self.sem(sem)
        prev = self.cnt[sem]
        if prev > 0:
            self.on(eng, lambda e, h=h, prev=prev: e.wait_ge(h, prev))
        self.cnt[sem] += 16
        tgt = self.cnt[sem]
        self.on(eng, lambda e, out=out, in_=in_, kw=kw, h=h: e.dma_start(
            out=out, in_=in_, **kw).then_inc(h, 16))
        return tgt

    QUEUE_OWNER = {"dS": "sync", "dG": "gpsimd", "dC": "scalar"}

    def gather(self, eng, out, in_, idx_ap, sem):
        self._predrain(eng)
        self.dma_sems.add(sem)
        h = self.sem(sem)
        prev = self.cnt[sem]
        if prev > 0:
            self.on(eng, lambda e, h=h, prev=prev: e.wait_ge(h, prev))
        self.cnt[sem] += 16
        tgt = self.cnt[sem]
        self.on(eng, lambda e, out=out, in_=in_, idx_ap=idx_ap, h=h: e.indirect_dma_start(
            out=out, out_offset=None, in_=in_,
            in_offset=bass.IndirectOffsetOnAxis(idx_ap, 0)).then_inc(h, 16))
        return tgt

    def wait(self, eng, sem, v=None):
        auto = v is None
        if v is None:
            v = self.cnt.get(sem, 0)
        if v == 0:
            return
        h = self.sem(sem)
        if auto and sem in self.QUEUE_OWNER:
            own = self.QUEUE_OWNER[sem]
            if not hasattr(self, "fenced"):
                self.fenced = {}
            if self.fenced.get(sem, 0) < v:
                self.fenced[sem] = v
                if own != eng:
                    self.on(own, lambda e, h=h, v=v: e.wait_ge(h, v))
        self.on(eng, lambda e, h=h, v=v: e.wait_ge(h, v))

    def wait_all_dma(self, eng):
        for s in ("dS", "dG", "dC"):
            if self.cnt.get(s, 0) > 0:
                self.wait(eng, s)

    def build(self):
        nc = self.nc
        qs = self.q
        with nc.Block() as blk:
            @blk.tensor
            def _(e):
                for f in qs["tensor"]:
                    f(e)

            @blk.vector
            def _(e):
                for f in qs["vector"]:
                    f(e)

            @blk.scalar
            def _(e):
                for f in qs["scalar"]:
                    f(e)

            @blk.sync
            def _(e):
                for f in qs["sync"]:
                    f(e)

            @blk.gpsimd
            def _(e):
                for f in qs["gpsimd"]:
                    f(e)
        self.es.close()
        return nc


def fence_loads(b, engs=("sync", "gpsimd")):
    for eng in engs:
        for s in ("T", "V", "S", "G"):
            if b.cnt.get(s, 0):
                b.wait(eng, s)


class Ctx:
    pass


def precast_w(b: Bld, c, name, src, nel):
    key = "_pc_" + name
    if getattr(b, key, None) is not None:
        return getattr(b, key)
    dst = b.dtmp("wbf_" + name, [nel], BF16)
    scr32 = c.scrb[:].bitcast(F32)
    CH = 128 * 4096
    sem = "pc"
    assert nel % CH == 0, nel
    for i in range(nel // CH):
        b.wait("sync", "V")
        b.dma("sync", scr32[:, 0:4096],
              bass.AP(src.tensor, i * CH, [[4096, 128], [1, 4096]]), sem=sem)
        b.wait("vector", sem, b.val(sem))
        b.op("vector", lambda e: e.tensor_copy(out=c.scrb[:, 8192:12288],
                                               in_=scr32[:, 0:4096]), sem="V")
        vv = b.val("V")
        b.wait("sync", "V", vv)
        b.dma("sync", bass.AP(dst.tensor, i * CH, [[4096, 128], [1, 4096]]),
              c.scrb[:, 8192:12288], sem=sem)
    for eng in ("vector", "sync", "gpsimd"):
        b.wait(eng, sem)
    setattr(b, key, dst)
    return dst


def setup_common(b: Bld):
    c = Ctx()
    c.x = b.sb("x_fm", [128, NCH * T], F32)
    c.scrb = b.sb("scrb", [128, NCH * T], BF16)
    c.lnb = b.sb("lnb", [128, NCH * T], BF16)
    c.mean = b.sb("ln_mean", [128, T], F32)
    c.rstd = b.sb("ln_rstd", [128, T], F32)
    c.tf32 = b.sb("tf32", [128, T], F32)
    c.ident = b.sb("ident", [128, 128], BF16)
    c.identf = b.sb("identf", [128, 128], F32)
    c.onesb_col = b.sb("onesb_col", [128, 128], BF16)
    c.onesb_row = b.sb("onesb_row", [1, 512], BF16)
    c.onesf_row = b.sb("onesf_row", [1, 128], F32)
    c.onesf_col = b.sb("onesf_col", [128, 1], F32)
    c.eps = b.sb("eps_tile", [128, 1], F32)
    c.psA = b.ps("psA", [128, T], F32)
    c.psB = b.ps("psB", [128, T], F32)

    xin = b.din("xio_in", [NCH, 128, T], F32)
    b.dma("sync", c.x[:], xin.transpose([1, 0, 2]))
    b.dma("sync", c.ident[:], b.din("c_ident", [128, 128], BF16))
    b.dma("sync", c.identf[:], b.din("c_identf", [128, 128], F32))
    b.op("vector", lambda e: e.memset(c.onesb_col[:], 1.0))
    b.op("vector", lambda e: e.memset(c.onesb_row[:], 1.0))
    b.op("vector", lambda e: e.memset(c.onesf_row[:], 1.0))
    b.op("vector", lambda e: e.memset(c.onesf_col[:], 1.0))
    b.op("vector", lambda e: e.memset(c.eps[:], LN_EPS), sem="V")
    b.wait("vector", "dS")
    return c


def finish_store(b: Bld, c):
    xout = b.dout("xio_out", [NCH, 128, T], F32)
    b.wait("sync", "V")
    if b.cnt.get("G", 0):
        b.wait("sync", "G")
    if b.cnt.get("S", 0):
        b.wait("sync", "S")
    b.dma("sync", xout.transpose([1, 0, 2]), c.x[:])
    for s in sorted(b.dma_sems):
        b.wait("sync", s, b.cnt[s])



def ln_compute(b: Bld, c, apply_bf16=True):
    if True:
        psum = c.psA[:]
        for ch in range(NCH):
            b.op("vector", lambda e, ch=ch: e.tensor_copy(
                out=c.scrb[:, ch * T:(ch + 1) * T], in_=c.x[:, ch * T:(ch + 1) * T]),
                sem="V" if ch == NCH - 1 else None)
        vv = b.val("V")
        b.wait("tensor", "V", vv)
        for tt in range(4):
            for ch in range(NCH):
                rhs = c.scrb[:, ch * T + tt * 512: ch * T + (tt + 1) * 512]
                b.op("tensor", lambda e, rhs=rhs, tt=tt, st_=(ch == 0), sp_=(ch == NCH - 1): e.matmul(
                    out=psum[:, tt * 512:(tt + 1) * 512], lhsT=c.onesb_col[:], rhs=rhs,
                    start=st_, stop=sp_),
                    sem="T" if (tt == 3 and ch == NCH - 1) else None)
        tv = b.val("T")
        b.wait("vector", "T", tv)
        b.op("vector", lambda e: e.tensor_scalar(
            out=c.mean[:], in0=psum, scalar1=1.0 / D, scalar2=None, op0=ALU.mult), sem="V")
        vv = b.val("V")
        for ch in range(NCH):
            sl = slice(ch * T, (ch + 1) * T)
            b.op("vector", lambda e, sl=sl: e.tensor_tensor(
                out=c.scrb[:, sl], in0=c.scrb[:, sl], in1=c.scrb[:, sl], op=ALU.mult),
                sem="V" if ch == NCH - 1 else None)
        vv2 = b.val("V")
        b.wait("tensor", "V", vv2)
        for tt in range(4):
            for ch in range(NCH):
                rhs = c.scrb[:, ch * T + tt * 512: ch * T + (tt + 1) * 512]
                b.op("tensor", lambda e, rhs=rhs, tt=tt, st_=(ch == 0), sp_=(ch == NCH - 1): e.matmul(
                    out=psum[:, tt * 512:(tt + 1) * 512], lhsT=c.onesb_col[:], rhs=rhs,
                    start=st_, stop=sp_),
                    sem="T" if (tt == 3 and ch == NCH - 1) else None)
        tv = b.val("T")
        b.wait("vector", "T", tv)
        b.op("vector", lambda e: e.tensor_tensor(
            out=c.rstd[:], in0=c.mean[:], in1=c.mean[:], op=ALU.mult))
        b.op("vector", lambda e: e.scalar_tensor_tensor(
            out=c.rstd[:], in0=psum, scalar=1.0 / D, in1=c.rstd[:],
            op0=ALU.mult, op1=ALU.subtract), sem="V")
        vv = b.val("V")
        b.wait("scalar", "V", vv)
        b.op("scalar", lambda e: e.activation(out=c.rstd[:], in_=c.rstd[:], func=AF.Ln, bias=c.eps[:]))
        b.op("scalar", lambda e: e.activation(out=c.rstd[:], in_=c.rstd[:], func=AF.Exp, scale=-0.5),
             sem="S")
        sv = b.val("S")
        b.wait("vector", "S", sv)
        if apply_bf16:
            for ch in range(NCH):
                sl = slice(ch * T, (ch + 1) * T)
                b.op("vector", lambda e, sl=sl: e.tensor_tensor(
                    out=c.tf32[:], in0=c.x[:, sl], in1=c.mean[:], op=ALU.subtract))
                b.op("vector", lambda e, sl=sl: e.tensor_tensor(
                    out=c.lnb[:, sl], in0=c.tf32[:], in1=c.rstd[:], op=ALU.mult),
                    sem="V" if ch == NCH - 1 else None)



def emit_mole_smalls(b: Bld, c, tag, rhs_getter, nin_ch, Rx, Ap, zg_out, st, pA, pB):
    fence_loads(b)
    rt = b.sb(f"{tag}_rt", [128, nin_ch * 24], BF16, st)
    at = b.sb(f"{tag}_at", [128, nin_ch * 24], BF16, st)
    e24 = b.sb(f"{tag}_e24", [24, T], F32, st)
    e24b = b.sb(f"{tag}_e24b", [24, T], BF16, st)
    r24 = b.sb(f"{tag}_r24", [24, T], F32, st)
    mxc = b.sb(f"{tag}_mxc", [128, 16], F32, st)
    scr32 = c.scrb[:].bitcast(F32)
    lm = e24
    mxr16 = scr32[0:16, 0:128]
    mxrow = scr32[0:1, 128:128 + T]
    mxbs = scr32[0:24, 4096:4096 + T]
    lgs = scr32[0:24, 6144:6144 + T]
    sums = b.sb(f"{tag}_ssel", [24, 24], BF16, st)
    mx_dram = b.dtmp(f"scr_mx", [16, 128], F32)
    b.dma("gpsimd", rt[:].rearrange("p (c k) -> p c k", k=24), Rx.rearrange("(c p) k -> p c k", p=128))
    b.dma("gpsimd", at[:].rearrange("p (c k) -> p c k", k=24), Ap.rearrange("(c p) k -> p c k", p=128))
    b.dma("sync", sums[:], b.din("c_sumsel24", [24, 24], BF16))
    plog = pA[0:24, :]
    b.wait("tensor", "dG")
    b.wait("tensor", "dS")
    b.wait("tensor", "V")
    for k in range(nin_ch):
        rhs = rhs_getter(k)
        for tt in range(4):
            b.op("tensor", lambda e, k=k, tt=tt, rhs=rhs: e.matmul(
                out=plog[:, tt * 512:(tt + 1) * 512], lhsT=rt[:, k * 24:(k + 1) * 24],
                rhs=rhs[:, tt * 512:(tt + 1) * 512], start=(k == 0), stop=(k == nin_ch - 1)),
                sem="T" if tt == 3 else None)
    tv = b.val("T")
    b.wait("vector", "T", tv)
    b.op("vector", lambda e: e.tensor_copy(out=lgs, in_=plog), sem="V")
    vv = b.val("V")
    b.wait("tensor", "V", vv)
    for tile in range(16):
        b.op("tensor", lambda e, tile=tile: e.transpose(
            out=pB[0:128, tile * 24:(tile + 1) * 24],
            in_=lgs[0:24, tile * 128:(tile + 1) * 128], identity=c.identf[0:24, 0:24]),
            sem="T" if tile == 15 else None)
    tv = b.val("T")
    b.wait("vector", "T", tv)
    b.op("vector", lambda e: e.reduce_max(
        out=mxc[:], in_=pB[:, 0:384].rearrange("p (a b) -> p a b", b=24), axis=AX.X), sem="V")
    vv = b.val("V")
    b.wait("tensor", "V", vv)
    b.op("tensor", lambda e: e.transpose(
        out=pB[:, 384:400].bitcast(F32)[0:16, 0:128] if False else pB[0:16, 384:512],
        in_=mxc[:], identity=c.identf[:]), sem="T")
    tv = b.val("T")
    b.wait("vector", "T", tv)
    b.op("vector", lambda e: e.tensor_copy(out=mxr16, in_=pB[0:16, 384:512]), sem="V")
    b.wait("sync", "V")
    b.dma("sync", mx_dram, mxr16)
    b.wait("sync", "dS")
    b.dma("sync", mxrow, mx_dram.rearrange("a b -> (a b)").unsqueeze(0))
    b.wait("tensor", "dS")
    for tt in range(4):
        b.op("tensor", lambda e, tt=tt: e.matmul(
            out=pB[0:24, tt * 512:(tt + 1) * 512], lhsT=c.onesf_row[:, 0:24],
            rhs=mxrow[0:1, tt * 512:(tt + 1) * 512], start=True, stop=True),
            sem="T" if tt == 3 else None)
    tv = b.val("T")
    b.wait("vector", "T", tv)
    b.op("vector", lambda e: e.tensor_copy(out=mxbs, in_=pB[0:24, :]))
    b.op("vector", lambda e: e.tensor_tensor(out=lm[:], in0=lgs, in1=mxbs, op=ALU.subtract),
         sem="V")
    vv = b.val("V")
    b.wait("scalar", "V", vv)
    b.op("scalar", lambda e: e.activation(out=e24[:], in_=lm[:], func=AF.Exp), sem="S")
    sv = b.val("S")
    b.wait("vector", "S", sv)
    b.op("vector", lambda e: e.tensor_copy(out=e24b[:], in_=e24[:]), sem="V")
    vv = b.val("V")
    b.wait("tensor", "V", vv)
    b.wait("tensor", "S", sv)
    for tt in range(4):
        b.op("tensor", lambda e, tt=tt: e.matmul(
            out=plog[:, tt * 512:(tt + 1) * 512], lhsT=sums[:],
            rhs=e24b[:, tt * 512:(tt + 1) * 512], start=True, stop=True),
            sem="T" if tt == 3 else None)
    tv = b.val("T")
    b.wait("scalar", "T", tv)
    b.op("scalar", lambda e: e.activation(out=r24[:], in_=plog, func=AF.Ln))
    b.op("scalar", lambda e: e.activation(out=r24[:], in_=r24[:], func=AF.Exp, scale=-1.0),
         sem="S")
    sv = b.val("S")
    b.wait("vector", "S", sv)
    b.op("vector", lambda e: e.scalar_tensor_tensor(
        out=e24[:], in0=e24[:], scalar=0.125, in1=r24[:], op0=ALU.mult, op1=ALU.mult), sem="V")
    vv = b.val("V")
    pz = pB[0:24, :]
    b.wait("tensor", "S")
    b.wait("tensor", "V")
    for k in range(nin_ch):
        rhs = rhs_getter(k)
        for tt in range(4):
            b.op("tensor", lambda e, k=k, tt=tt, rhs=rhs: e.matmul(
                out=pz[:, tt * 512:(tt + 1) * 512], lhsT=at[:, k * 24:(k + 1) * 24],
                rhs=rhs[:, tt * 512:(tt + 1) * 512], start=(k == 0), stop=(k == nin_ch - 1)),
                sem="T" if tt == 3 else None)
    tv = b.val("T")
    b.wait("vector", "T", tv)
    b.op("vector", lambda e: e.tensor_tensor(out=zg_out[:], in0=pz, in1=e24[:], op=ALU.mult),
         sem="V")
    return b.val("V")


def emit_mole_main(b: Bld, c, tag, rhs_getter, nin_ch, nout_ch, Wt, Bp, bcol, zg, st,
                   out_cb, per_k_sem=False):
    fence_loads(b)
    bb = b.sb(f"{tag}_bb", [24, nout_ch * 128], BF16, st)
    wt0 = b.sb(f"{tag}_wt", [128, nin_ch * 128], BF16, st)
    wbufs = [wt0[:], c.scrb[:, 0:nin_ch * 128]]
    pws = b._mole_pws
    b._dbg_wt0 = wt0
    b.dma("gpsimd", bb[:], Bp)
    b.wait("tensor", "dG")
    gv_prev = None
    lim = int(os.environ.get("MOLE_NOUT", "999"))
    nout_eff = min(nout_ch, lim)
    wtgt = {}

    def wissue(hh):
        eng = ("gpsimd", "scalar")[hh % 2]
        b.wait(eng, "T")
        b.wait(eng, "V")
        wtgt[hh] = b.dma(eng, wbufs[hh % 2], Wt(hh), sem=f"wl{hh % 2}_{tag}")

    for hh in range(nout_eff):
        with contextlib.ExitStack() as st2:
            pw = pws[hh % 2][:]
            if hh not in wtgt:
                wissue(hh)
            if hh + 1 < nout_eff:
                wissue(hh + 1)
            b.wait("tensor", f"wl{hh % 2}_{tag}", wtgt[hh])
            if b.cnt.get("S", 0):
                b.wait("tensor", "S")
            if b.cnt.get("V", 0):
                b.wait("tensor", "V")
            nolora = os.environ.get("DBG_NOLORA") == "1"
            if hh >= 1 and os.environ.get("DBG_SKIP") == "mm":
                continue
            for k in range(nin_ch):
                rhs = rhs_getter(k)
                for tt in range(4):
                    b.op("tensor", lambda e, k=k, tt=tt, rhs=rhs, pw=pw, wb=wbufs[hh % 2], sp=(nolora and k == nin_ch - 1): e.matmul(
                        out=pw[:, tt * 512:(tt + 1) * 512], lhsT=wb[:, k * 128:(k + 1) * 128],
                        rhs=rhs[:, tt * 512:(tt + 1) * 512], start=(k == 0), stop=sp),
                        sem="T" if (tt == 3 and (per_k_sem or (nolora and k == nin_ch - 1))) else None)
            if not nolora:
                for tt in range(4):
                    b.op("tensor", lambda e, hh=hh, tt=tt, pw=pw: e.matmul(
                        out=pw[:, tt * 512:(tt + 1) * 512], lhsT=bb[:, hh * 128:(hh + 1) * 128],
                        rhs=zg[:, tt * 512:(tt + 1) * 512], start=False, stop=True),
                        sem="T" if tt == 3 else None)
            gv_prev = b.val("T")
            if os.environ.get("DBG_PROBE") == "1" and hh == 0:
                b.wait("gpsimd", "T", gv_prev)
                b.on("gpsimd", lambda e, p=pw: e.print(p[0:1, 0:6]))
            out_cb(hh, pw, st2)


import os
L1FFN_STAGE = int(os.environ.get("L1FFN_STAGE", "9"))


def emit_l1ffn(b: Bld, c, tag):
    W1t = b.din("L1_ffn_W1t", [HFF1 // 128, 128, D], F32)
    W1bf = precast_w(b, c, "l1W1", W1t, HFF1 * D)
    W2bf = precast_w(b, c, "l1W2", b.din("L1_ffn_W2t", [NCH, 128, HFF1], F32),
                     HFF1 * D)
    w1ap = lambda hh: bass.AP(W1bf.tensor, hh * 128 * D, [[D, 128], [1, D]])
    w2ap = lambda hh: bass.AP(W2bf.tensor, hh * 128 * HFF1,
                              [[HFF1, 128], [1, HFF1]])
    A1 = b.din("L1_ffn_A1p", [D, 24], F32)
    B1 = b.din("L1_ffn_B1p", [24, HFF1], F32)
    R1 = b.din("L1_ffn_R1x", [D, 24], F32)
    b1 = b.din("L1_ffn_b1c", [128, HFF1 // 128], F32)
    A2 = b.din("L1_ffn_A2p", [HFF1, 24], F32)
    B2 = b.din("L1_ffn_B2p", [24, D], F32)
    R2 = b.din("L1_ffn_R2x", [HFF1, 24], F32)
    b2 = b.din("L1_ffn_b2c", [128, NCH], F32)
    h1d = b.dtmp(f"scr_h1", [HFF1 // 128, 128, T], BF16)

    ln_compute(b, c, apply_bf16=True)
    if os.environ.get("DBG_TAP") == "ln":
        for ch in range(NCH):
            sl = slice(ch * T, (ch + 1) * T)
            b.op("vector", lambda e, sl=sl: e.tensor_copy(out=c.x[:, sl], in_=c.lnb[:, sl]),
                 sem="V")
        return
    if L1FFN_STAGE < 1:
        return

    NH1 = HFF1 // 128
    with contextlib.ExitStack() as st:
        zg1 = b.sb(f"{tag}_zg1", [24, T], BF16, st)
        b1t = b.sb(f"{tag}_b1t", [128, NH1], F32, st)
        hst = b.sb(f"{tag}_hst", [128, T], BF16, st)
        b.dma("sync", b1t[:], b1)

        def ln_chunk(k):
            return c.lnb[:, k * T:(k + 1) * T]

        b._mole_pws = [c.psA, c.psB]
        vv_zg = emit_mole_smalls(b, c, tag + "m1", ln_chunk, NCH, R1, A1, zg1, st,
                                 c.psA, c.psB)
        if os.environ.get("DBG_TAP") == "zg":
            b.op("vector", lambda e: e.tensor_copy(out=c.x[0:24, 0:T], in_=zg1[:]), sem="V")
            return
        if L1FFN_STAGE < 2:
            return

        dma_prev = [None]

        def out1(hh, pw, st2):
            if hh >= 1 and os.environ.get("DBG_SKIP") == "act":
                return
            if os.environ.get("DBG_TAP") == "wt0" and hh == 0:
                b.wait("vector", "wl_" + tag + "m1")
                b.op("vector", lambda e, w=b._dbg_wt0: e.tensor_copy(out=c.x[:, 0:1024],
                     in_=w[:]), sem="V")
            if os.environ.get("DBG_TAP") == "ln2" and hh == 0:
                b.op("vector", lambda e: e.tensor_copy(out=c.x[:, 0:T],
                     in_=c.lnb[:, 0:T]), sem="V")
            if os.environ.get("DBG_TAP") == "pw0" and hh == 0:
                b.wait("vector", "T")
                b.op("vector", lambda e: e.tensor_copy(out=c.x[:, 0:T], in_=pw), sem="V")
            b.wait("scalar", "T")
            if dma_prev[0] is not None:
                b.wait("scalar", "h1s", dma_prev[0])
            b.op("scalar", lambda e, hh=hh, pw=pw: e.activation(
                out=hst[:], in_=pw, func=AF.Relu, bias=b1t[:, hh:hh + 1]), sem="S")
            if os.environ.get("DBG_TAP") == "hst" and hh == 0:
                b.wait("vector", "S")
                b.op("vector", lambda e: e.tensor_copy(out=c.x[:, 0:T], in_=hst[:]), sem="V")
            if hh >= 1 and os.environ.get("DBG_SKIP") == "store":
                return
            b.wait("sync", "S")
            dma_prev[0] = b.dma("sync", h1d[hh], hst[:], sem="h1s")
            if os.environ.get("DBG_PROBE") == "1" and hh == 0:
                b.wait("gpsimd", "h1s", b.val("h1s"))
                b.on("gpsimd", lambda e: e.print(h1d[0][0:1, 0:6]))

        emit_mole_main(b, c, tag + "m1", ln_chunk, NCH, NH1, w1ap, B1, b1, zg1, st, out1)
        b.wait("vector", "dS")
    if os.environ.get("DBG_TAP") in ("wt0", "pw0", "ln2", "hst"):
        return
    if os.environ.get("DBG_TAP") == "h1":
        b.wait("gpsimd", "h1s", b.val("h1s"))
        for ch in range(min(NCH, int(os.environ.get("MOLE_NOUT", "8")))):
            slot = c.scrb[:, 14336:16384]
            b.dma("gpsimd", slot, b.dram["scr_h1"][ch])
            b.wait("vector", "dG")
            sl = slice(ch * T, (ch + 1) * T)
            b.op("vector", lambda e, sl=sl, slot=slot: e.tensor_copy(out=c.x[:, sl], in_=slot),
                 sem="V")
            b.wait("gpsimd", "V")
        return
    if L1FFN_STAGE < 3:
        return

    b.wait("sync", "h1s")
    with contextlib.ExitStack() as st:
        zg2 = b.sb(f"{tag}_zg2", [24, T], BF16, st)
        b2t = b.sb(f"{tag}_b2t2", [128, NCH], F32, st)
        h1a = b.sb(f"{tag}_h1a", [128, T], BF16, st)
        h1b = b.sb(f"{tag}_h1b", [128, T], BF16, st)
        b.dma("sync", b2t[:], b2)
        hbufs = [h1a, h1b]
        loaded = {}

        def h_chunk(k):
            buf = hbufs[k % 2]
            if loaded.get(k) is None:
                loaded[k] = b.dma("sync", buf[:], h1d[k])
                b.wait("tensor", "dS", loaded[k])
            return buf[:]

        class Reload:
            def __init__(self):
                self.tgt = {}
                self.prev_k = None

            def issue(self, k):
                eng = ("sync", "gpsimd")[k % 2]
                b.wait(eng, "T")
                self.tgt[k] = b.dma(eng, hbufs[k % 2][:], h1d[k],
                                    sem=f"h1l{k % 2}")

            def __call__(self, k):
                if k == 0 and self.prev_k is not None:
                    self.tgt.clear()
                if k not in self.tgt:
                    self.issue(k)
                if k + 1 < NH1 and (k + 1) not in self.tgt:
                    self.issue(k + 1)
                self.prev_k = k
                b.wait("tensor", f"h1l{k % 2}", self.tgt[k])
                return hbufs[k % 2][:]

        rl = Reload()
        b._mole_pws = [c.psA, c.psB]
        emit_mole_smalls(b, c, tag + "m2", rl, NH1, R2, A2, zg2, st,
                         c.psA, c.psB)
        if L1FFN_STAGE < 4:
            return

        def out2(hh, pw, st2):
            b.wait("vector", "T")
            b.op("vector", lambda e, hh=hh: e.tensor_scalar(
                out=c.tf32[:], in0=pw[:], scalar1=b2t[:, hh:hh + 1], scalar2=0.5,
                op0=ALU.add, op1=ALU.mult))
            sl = slice(hh * T, (hh + 1) * T)
            b.op("vector", lambda e, sl=sl: e.tensor_tensor(
                out=c.x[:, sl], in0=c.x[:, sl], in1=c.tf32[:], op=ALU.add), sem="V")

        emit_mole_main(b, c, tag + "m2", rl, NH1, NCH, w2ap, B2, b2, zg2, st, out2,
                       per_k_sem=True)
        b.wait("sync", "V")





MAGIC = 12582912.0

NC_GROUP = [list(range(NCORES))]


def part_absmax_to_scalar(b, c, col, pB, dst):
    vv = b.val("V")
    b.wait("tensor", "V", vv)
    b.op("tensor", lambda e: e.transpose(out=pB[0:1, 0:128], in_=col, identity=c.identf[:]),
         sem="T")
    tv = b.val("T")
    b.wait("vector", "T", tv)
    b.op("vector", lambda e: e.reduce_max(out=dst, in_=pB[0:1, 0:128], axis=AX.X), sem="V")


def collective_max(b, c, src11, n, tag, out_sb):
    cin = b.dtmp(f"ccin_{tag}", [1, n], F32)
    cout = b.dtmp(f"ccout_{tag}", [1, n], F32)
    b.wait("gpsimd", "V")
    b.dma("gpsimd", cin, src11)
    b.wait("gpsimd", "dG")
    csem = f"cc_{tag}"
    h = b.sem(csem)
    b.cnt[csem] += 1
    grp = NC_GROUP if os.environ.get("SIM1") != "1" else [[0]]
    b.on("gpsimd", lambda e, h=h: e.collective_compute(
        "AllReduce", ALU.max, replica_groups=grp, ins=[cin], outs=[cout]).then_inc(h, 1))
    b.on("gpsimd", lambda e, h=h: e.wait_ge(h, 1))
    b.dma("gpsimd", out_sb, cout)
    b.wait("vector", "dG")


def make_scale_tiles(b, c, mx11, pB, s_col, rs_col, qmax=32767.0):
    srow = b._l0_srow
    b.op("vector", lambda e: e.tensor_scalar(out=srow[0:1, 0:1], in0=mx11, scalar1=1.0 / qmax,
                                             scalar2=1e-12, op0=ALU.mult, op1=ALU.max))
    b.op("vector", lambda e: e.reciprocal(out=srow[0:1, 1:2], in_=srow[0:1, 0:1]), sem="V")
    vv = b.val("V")
    b.wait("tensor", "V", vv)
    b.op("tensor", lambda e: e.matmul(out=pB[0:128, 0:2], lhsT=c.onesf_row[:],
                                      rhs=srow[0:1, 0:2], start=True, stop=True), sem="T")
    tv = b.val("T")
    b.wait("vector", "T", tv)
    b.op("vector", lambda e: e.tensor_copy(out=s_col, in_=pB[0:128, 0:1]))
    b.op("vector", lambda e: e.tensor_copy(out=rs_col, in_=pB[0:128, 1:2]), sem="V")


def emit_dequant(b, c, tag, w_given, wT, O, I, stash, stash_mode):
    nb = I // 32
    not_ = O // 128
    nic = I // 128
    with contextlib.ExitStack() as st:
        fence_loads(b)
        wbuf = b.sb(f"{tag}_dw", [128, max(I, O)], F32, st)
        sacc = b.sb(f"{tag}_sacc", [128, not_ * nb], F32, st)
        racc = b.sb(f"{tag}_racc", [128, not_ * nb], F32, st)
        str16 = b.sb(f"{tag}_str", [128, 128], F32, st)
        sT_d = b.dtmp(f"scr_sT_{tag}", [nb, O], F32)
        rT_d = b.dtmp(f"scr_rT_{tag}", [nb, O], F32)
        for ot in range(not_):
            b.wait("sync", "V")
            ldv = b.dma("sync", wbuf[:, 0:I], w_given[ot * 128:(ot + 1) * 128, :])
            b.wait("vector", "dS", ldv)
            sl = slice(ot * nb, (ot + 1) * nb)
            b.op("vector", lambda e, sl=sl: e.reduce_max(
                out=sacc[:, sl], in_=wbuf[:, 0:I].rearrange("p (a b) -> p a b", b=32),
                axis=AX.X, apply_absolute_value=True))
            b.op("vector", lambda e, sl=sl: e.tensor_scalar(
                out=sacc[:, sl], in0=sacc[:, sl], scalar1=1.0 / 7.0, scalar2=1e-12,
                op0=ALU.mult, op1=ALU.max))
            b.op("vector", lambda e, sl=sl: e.reciprocal(out=racc[:, sl], in_=sacc[:, sl]),
                 sem="V")
        for which, acc, dstd in (("s", sacc, sT_d), ("r", racc, rT_d)):
            for ot in range(not_):
                vv = b.val("V")
                b.wait("tensor", "V", vv)
                b.op("tensor", lambda e, ot=ot, acc=acc: e.transpose(
                    out=c.psB[0:nb, 0:128] if nb <= 128 else None,
                    in_=acc[:, ot * nb:(ot + 1) * nb], identity=c.identf[:]), sem="T")
                tv = b.val("T")
                b.wait("vector", "T", tv)
                b.op("vector", lambda e: e.tensor_copy(out=str16[0:nb, 0:128],
                                                       in_=c.psB[0:nb, 0:128]), sem="V")
                vv = b.val("V")
                b.wait("sync", "V", vv)
                sv = b.dma("sync", dstd[:, ot * 128:(ot + 1) * 128], str16[0:nb, 0:128])
                b.wait("vector", "dS", sv)
        scr32 = c.scrb[:].bitcast(F32)
        sexp = scr32[:, 0:O]
        rexp = scr32[:, 4096:4096 + O]
        wq = c.lnb[:, 0:O]
        b.wait("gpsimd", "dS")
        for ci in range(nic):
            b.wait("sync", "V")
            b.wait("gpsimd", "V")
            ldv = b.dma("sync", wbuf[:, 0:O], wT[ci * 128:(ci + 1) * 128, :])
            l2 = b.dma("gpsimd", sexp, bass.AP(sT_d.tensor, 4 * ci * O, [[O, 4], [0, 32], [1, O]]))
            l3 = b.dma("gpsimd", rexp, bass.AP(rT_d.tensor, 4 * ci * O, [[O, 4], [0, 32], [1, O]]))
            b.wait("vector", "dS", ldv)
            b.wait("vector", "dG", l3)
            b.op("vector", lambda e: e.tensor_tensor(out=wbuf[:, 0:O], in0=wbuf[:, 0:O],
                                                     in1=rexp, op=ALU.mult))
            b.op("vector", lambda e: e.tensor_scalar(out=wbuf[:, 0:O], in0=wbuf[:, 0:O],
                                                     scalar1=MAGIC, scalar2=MAGIC,
                                                     op0=ALU.add, op1=ALU.subtract))
            b.op("vector", lambda e: e.tensor_tensor(out=wq, in0=wbuf[:, 0:O],
                                                     in1=sexp, op=ALU.mult), sem="V")
            vv = b.val("V")
            b.wait("sync", "V", vv)
            if stash_mode == "tiled8":
                sv = b.dma("sync", bass.AP(stash.tensor, ci * 16384,
                                           [[128, 128], [nic * 16384, not_], [1, 128]]),
                           wq.rearrange("p (a j) -> p a j", j=128))
            else:
                sv = b.dma("sync", stash[ci], wq)
            b.wait("vector", "dS", sv)


def emit_l0ffn(b: Bld, c, tag):
    wg_g = b.din("L0_ffn_w_gate", [IFF0, D], F32)
    wu_g = b.din("L0_ffn_w_up", [IFF0, D], F32)
    wd_g = b.din("L0_ffn_w_down", [D, IFF0], F32)
    wg_t = b.din("L0_ffn_w_gateT", [D, IFF0], F32)
    wu_t = b.din("L0_ffn_w_upT", [D, IFF0], F32)
    wd_t = b.din("L0_ffn_w_downT", [IFF0, D], F32)
    stash_u = b.dtmp("scr_wu", [24 * 8 * 128 * 128], BF16)
    stash_g = b.dtmp("scr_wg", [24 * 8 * 128 * 128], BF16)
    stash_d = b.dtmp("scr_wd", [24, 128, 1024], BF16)
    up_d = b.dtmp("scr_up", [24, 128, T], F32)
    gate_d = b.dtmp("scr_gate", [24, 128, T], F32)
    gq_d = b.dtmp("scr_gq", [24, 128, T], I16)
    upq_d = b.dtmp("scr_upq", [24, 128, T], I16)
    g2q_d = b.dtmp("scr_g2q", [24, 128, T], I16)
    b._l0_srow = getattr(b, "_l0_srow", None) or b.sb("l0_srow", [1, 8], F32)

    if not getattr(b, "_l0_dequant_done", False):
        b._l0_dequant_done = True
        emit_dequant(b, c, tag + "du", wu_g, wu_t, IFF0, D, stash_u, "tiled8")
        emit_dequant(b, c, tag + "dg", wg_g, wg_t, IFF0, D, stash_g, "tiled8")
        emit_dequant(b, c, tag + "dd", wd_g, wd_t, D, IFF0, stash_d, "rows")

    ln_compute(b, c, apply_bf16=False)

    with contextlib.ExitStack() as st:
        fence_loads(b)
        macc = b.sb(f"{tag}_macc", [128, 1], F32, st)
        mcol = b.sb(f"{tag}_mcol", [128, 8], F32, st)
        scl = b.sb(f"{tag}_scl", [128, 8], F32, st)
        scl2 = b.sb(f"{tag}_scl2", [128, 4], F32, st)
        stA = contextlib.ExitStack()
        tfb = b.sb(f"{tag}_tfb", [128, T], F32, stA)
        xqi = b.sb(f"{tag}_xqi", [128, T], I16, stA)
        mrow = b._l0_srow

        for ch in range(NCH):
            sl = slice(ch * T, (ch + 1) * T)
            b.op("vector", lambda e, sl=sl: e.tensor_tensor(out=c.tf32[:], in0=c.x[:, sl],
                                                            in1=c.mean[:], op=ALU.subtract))
            b.op("vector", lambda e: e.tensor_tensor(out=tfb[:], in0=c.tf32[:],
                                                     in1=c.rstd[:], op=ALU.mult))
            b.op("vector", lambda e, ch=ch: e.reduce_max(out=mcol[:, ch:ch + 1], in_=tfb[:],
                                                         axis=AX.X, apply_absolute_value=True))
        b.op("vector", lambda e: e.reduce_max(out=macc[:], in_=mcol[:], axis=AX.X), sem="V")
        part_absmax_to_scalar(b, c, macc[:], c.psB, mrow[0:1, 0:1])
        collective_max(b, c, mrow[0:1, 0:1], 1, tag + "c1", mrow[0:1, 0:1])
        make_scale_tiles(b, c, mrow[0:1, 0:1], c.psB, scl[:, 0:1], scl[:, 1:2])

        for ch in range(NCH):
            sl = slice(ch * T, (ch + 1) * T)
            b.op("vector", lambda e, sl=sl: e.tensor_tensor(out=c.tf32[:], in0=c.x[:, sl],
                                                            in1=c.mean[:], op=ALU.subtract))
            b.op("vector", lambda e: e.tensor_tensor(out=tfb[:], in0=c.tf32[:],
                                                     in1=c.rstd[:], op=ALU.mult))
            b.op("vector", lambda e: e.tensor_scalar(out=c.tf32[:], in0=tfb[:],
                                                     scalar1=scl[:, 1:2], scalar2=MAGIC,
                                                     op0=ALU.mult, op1=ALU.add))
            b.op("vector", lambda e: e.tensor_scalar(out=xqi[:], in0=c.tf32[:],
                                                     scalar1=MAGIC, scalar2=32767.0,
                                                     op0=ALU.subtract, op1=ALU.min))
            b.op("vector", lambda e, sl=sl: e.tensor_scalar(out=c.lnb[:, sl], in0=xqi[:],
                                                            scalar1=scl[:, 0:1], scalar2=None,
                                                            op0=ALU.mult), sem="V")

        wtu = [b.sb(f"{tag}_wtu{i}", [128, 1024], BF16, stA) for i in range(2)]
        wtg = [b.sb(f"{tag}_wtg{i}", [128, 1024], BF16, stA) for i in range(2)]
        stg = [b.sb(f"{tag}_stg{i}", [128, 512], F32, stA) for i in range(4)]
        mxu = b.sb(f"{tag}_mxu", [128, 1], F32, stA)
        mxg = b.sb(f"{tag}_mxg", [128, 1], F32, stA)
        b.op("vector", lambda e: e.memset(mxu[:], 0.0))
        b.op("vector", lambda e: e.memset(mxg[:], 0.0), sem="V")
        tprev = {}
        vdone = {}
        sdone = {}
        b.wait("tensor", "V")
        for it in range(24):
            for wi, (wbufs, stsh) in enumerate(((wtu, stash_u), (wtg, stash_g))):
                wb = wbufs[it % 2]
                wsem = f"{tag}_w{wi}"
                if it >= 2 and tprev.get((it - 2, wi)) is not None:
                    b.wait("gpsimd", "T", tprev[(it - 2, wi)])
                b.dma("gpsimd", wb[:].rearrange("p (k j) -> p k j", j=128),
                      bass.AP(stsh.tensor, it * 8 * 16384, [[128, 128], [16384, 8], [1, 128]]),
                      sem=wsem)
                b.wait("tensor", wsem, b.val(wsem))
            for tt in range(4):
                for wi, wbufs in enumerate((wtu, wtg)):
                    wb = wbufs[it % 2]
                    ps = c.psA[:, (2 * tt + wi) % 4 * 512:((2 * tt + wi) % 4 + 1) * 512]
                    key = (2 * tt + wi) % 4
                    if vdone.get(key) is not None:
                        b.wait("tensor", "V", vdone[key])
                    for k in range(NCH):
                        b.op("tensor", lambda e, k=k, tt=tt, wb=wb, ps=ps: e.matmul(
                            out=ps, lhsT=wb[:, k * 128:(k + 1) * 128],
                            rhs=c.lnb[:, k * T + tt * 512:k * T + (tt + 1) * 512],
                            start=(k == 0), stop=(k == NCH - 1)),
                            sem="T" if k == NCH - 1 else None)
                    tv = b.val("T")
                    tprev[(it, wi)] = tv
                    sb_ = stg[key]
                    b.wait("vector", "T", tv)
                    if sdone.get(key) is not None:
                        b.wait("vector", "dS", sdone[key])
                    b.op("vector", lambda e, ps=ps, sb_=sb_: e.tensor_copy(out=sb_[:], in_=ps))
                    mx = (mxu, mxg)[wi]
                    b.op("vector", lambda e, ps=ps: e.reduce_max(
                        out=mcol[:, 0:1], in_=ps, axis=AX.X, apply_absolute_value=True))
                    b.op("vector", lambda e, mx=mx: e.tensor_tensor(
                        out=mx[:], in0=mx[:], in1=mcol[:, 0:1], op=ALU.max), sem="V")
                    vv = b.val("V")
                    vdone[key] = vv
                    b.wait("sync", "V", vv)
                    dst = (up_d, gate_d)[wi]
                    sdone[key] = b.dma("sync", dst[it][:, tt * 512:(tt + 1) * 512], sb_[:])
        part_absmax_to_scalar(b, c, mxu[:], c.psB, mrow[0:1, 2:3])
        part_absmax_to_scalar(b, c, mxg[:], c.psB, mrow[0:1, 3:4])
        collective_max(b, c, mrow[0:1, 2:4], 2, tag + "c2", mrow[0:1, 2:4])
        make_scale_tiles(b, c, mrow[0:1, 2:3], c.psB, scl[:, 2:3], scl[:, 3:4])
        make_scale_tiles(b, c, mrow[0:1, 3:4], c.psB, scl[:, 4:5], scl[:, 5:6])

        fence_loads(b)
        b.wait("vector", "dS")
        b.wait("vector", "dG")
        stA.close()
        gb = [b.sb(f"{tag}_gb{i}", [128, 512], F32, st) for i in range(2)]
        qb = [b.sb(f"{tag}_qb{i}", [128, 512], I16, st) for i in range(2)]
        t1 = b.sb(f"{tag}_t1", [128, 512], F32, st)
        t2 = b.sb(f"{tag}_t2", [128, 512], F32, st)
        t2u = b.sb(f"{tag}_t2u", [128, 512], U16, st)
        m4 = b.sb(f"{tag}_m4", [128, 1], F32, st)
        b.op("vector", lambda e: e.memset(m4[:], 0.0), sem="V")
        t1g = b.sb(f"{tag}_t1g", [128, 512], F32, st)
        t2g = b.sb(f"{tag}_t2g", [128, 512], F32, st)
        t2ug = b.sb(f"{tag}_t2ug", [128, 512], U16, st)
        m4g = b.sb(f"{tag}_m4g", [128, 1], F32, st)
        mcolg = b.sb(f"{tag}_mcg", [128, 1], F32, st)
        m4g11 = b.sb(f"{tag}_m4g11", [1, 1], F32, st)
        mcg11 = b.sb(f"{tag}_mcg11", [1, 1], F32, st)
        for _ws in (f"{tag}_w0", f"{tag}_w1"):
            if b.cnt.get(_ws, 0):
                b.wait("gpsimd", _ws)
                b.wait("vector", _ws)
                b.wait("scalar", _ws)
                b.wait("sync", _ws)
        b.wait("gpsimd", "dS")
        b.wait("scalar", "dS")
        b.op("gpsimd", lambda e: e.memset(m4g11[:], 0.0), sem="G")
        ldv = {}
        vdn = {}
        sdn = {}
        b.wait("sync", "dS")
        for i in range(96):
            it, tt = divmod(i, 4)
            bi = i % 2
            eng = ("vector", "gpsimd")[bi]
            sq = ("V", "G")[bi]
            ssem = ("dC", "dG")[bi]
            seng = ("scalar", "gpsimd")[bi]
            tt1 = (t1, t1g)[bi]
            tt2 = (t2, t2g)[bi]
            ttu = (t2u, t2ug)[bi]
            mm4 = (m4, m4g)[bi]
            mmc = (mcol[:, 0:1], mcolg[:])[bi]
            if vdn.get(i - 2) is not None:
                b.wait("sync", sq, vdn[i - 2])
            ldv[i] = b.dma("sync", gb[bi][:], gate_d[it][:, tt * 512:(tt + 1) * 512], sem="gl")
            b.wait(eng, "gl", ldv[i])
            if sdn.get(i - 2) is not None:
                b.wait(eng, ssem, sdn[i - 2])
            b.op(eng, lambda e, bi=bi, tt2=tt2: e.tensor_scalar(
                out=tt2[:], in0=gb[bi][:], scalar1=scl[:, 5:6], scalar2=MAGIC,
                op0=ALU.mult, op1=ALU.add))
            b.op(eng, lambda e, bi=bi, tt2=tt2: e.tensor_scalar(
                out=qb[bi][:], in0=tt2[:], scalar1=MAGIC, scalar2=32767.0,
                op0=ALU.subtract, op1=ALU.min))
            b.op(eng, lambda e, bi=bi, tt1=tt1: e.tensor_scalar(
                out=tt1[:], in0=qb[bi][:], scalar1=scl[:, 4:5], scalar2=None, op0=ALU.mult),
                sem=sq)
            vv = b.val(sq)
            b.wait("scalar", sq, vv)
            b.op("scalar", lambda e, tt1=tt1, tt2=tt2: e.activation(
                out=tt2[:], in_=tt1[:], func=AF.Sigmoid), sem="S")
            sv = b.val("S")
            b.wait(eng, "S", sv)
            b.op(eng, lambda e, tt2=tt2: e.tensor_scalar(
                out=tt2[:], in0=tt2[:], scalar1=65536.0, scalar2=MAGIC,
                op0=ALU.mult, op1=ALU.add))
            b.op(eng, lambda e, tt2=tt2, ttu=ttu: e.tensor_scalar(
                out=ttu[:], in0=tt2[:], scalar1=MAGIC, scalar2=65535.0,
                op0=ALU.subtract, op1=ALU.min))
            b.op(eng, lambda e, tt2=tt2, ttu=ttu: e.tensor_scalar(
                out=tt2[:], in0=ttu[:], scalar1=1.0, scalar2=None, op0=ALU.mult))
            b.op(eng, lambda e, tt1=tt1, tt2=tt2: e.tensor_tensor(
                out=tt1[:], in0=tt1[:], in1=tt2[:], op=ALU.mult))
            if bi == 0:
                b.op(eng, lambda e, tt1=tt1, mmc=mmc: e.reduce_max(
                    out=mmc, in_=tt1[:], axis=AX.X, apply_absolute_value=True))
                b.op(eng, lambda e, mm4=mm4, mmc=mmc: e.tensor_tensor(
                    out=mm4[:], in0=mm4[:], in1=mmc, op=ALU.max), sem=sq)
            else:
                b.op(eng, lambda e, tt1=tt1: e.reduce_max(
                    out=mcg11[:], in_=tt1[:], axis=AX.XYZWC,
                    apply_absolute_value=True))
                b.op(eng, lambda e: e.tensor_scalar(
                    out=m4g11[:], in0=m4g11[:], scalar1=mcg11[:], scalar2=None,
                    op0=ALU.max), sem=sq)
            vdn[i] = b.val(sq)
            b.wait(seng, sq, vdn[i])
            sdn[i] = b.dma(seng, gq_d[it][:, tt * 512:(tt + 1) * 512], qb[bi][:])
        b.wait("vector", "G")
        b.op("vector", lambda e: e.tensor_tensor(out=m4[0:1, :], in0=m4[0:1, :],
                                                 in1=m4g11[:], op=ALU.max), sem="V")
        part_absmax_to_scalar(b, c, m4[:], c.psB, mrow[0:1, 4:5])
        b.op("vector", lambda e: e.tensor_scalar(out=mrow[0:1, 4:5], in0=mrow[0:1, 4:5],
                                                 scalar1=1.0 / 65536.0, scalar2=None,
                                                 op0=ALU.mult), sem="V")
        collective_max(b, c, mrow[0:1, 4:5], 1, tag + "c3", mrow[0:1, 4:5])
        make_scale_tiles(b, c, mrow[0:1, 4:5], c.psB, scl[:, 6:7], scl[:, 7:8])
        b.op("vector", lambda e: e.tensor_scalar(out=scl[:, 7:8], in0=scl[:, 7:8],
                                                 scalar1=1.0 / 65536.0, scalar2=None,
                                                 op0=ALU.mult), sem="V")

        ub = gb
        qgb = [b.sb(f"{tag}_qg{i}", [128, 512], I16, st) for i in range(2)]
        q2b = [b.sb(f"{tag}_q2b{i}", [128, 512], I16, st) for i in range(2)]
        t3 = b.sb(f"{tag}_t3", [128, 512], F32, st)
        m5 = b.sb(f"{tag}_m5", [128, 1], F32, st)
        b.op("vector", lambda e: e.memset(m5[:], 0.0), sem="V")
        t3g = b.sb(f"{tag}_t3g", [128, 512], F32, st)
        b.op("gpsimd", lambda e: e.memset(m4g11[:], 0.0), sem="G")
        ldv = {}
        ldg = {}
        vdn = {}
        sdn = {}
        sdn2 = {}
        b.wait("sync", "gl")
        b.wait("sync", "V")
        b.wait("sync", "G")
        b.wait("sync", "dG")
        b.wait("sync", "dC")
        b.wait("gpsimd", "dG")
        b.wait("gpsimd", "V")
        for i in range(96):
            it, tt = divmod(i, 4)
            bi = i % 2
            eng = ("vector", "gpsimd")[bi]
            sq = ("V", "G")[bi]
            ssem = ("dC", "dG")[bi]
            seng = ("scalar", "gpsimd")[bi]
            tt1 = (t1, t1g)[bi]
            tt2 = (t2, t2g)[bi]
            tt3 = (t3, t3g)[bi]
            ttu = (t2u, t2ug)[bi]
            mm5 = m5
            mmc = (mcol[:, 0:1], mcolg[:])[bi]
            if vdn.get(i - 2) is not None:
                b.wait("sync", sq, vdn[i - 2])
            if sdn2.get(i - 2) is not None:
                b.wait("sync", ssem, sdn2[i - 2])
            ldv[i] = b.dma("sync", ub[bi][:], up_d[it][:, tt * 512:(tt + 1) * 512], sem="gl")
            ldg[i] = b.dma("sync", qgb[bi][:], gq_d[it][:, tt * 512:(tt + 1) * 512], sem="gq")
            b.wait(eng, "gl", ldv[i])
            b.wait(eng, "gq", ldg[i])
            if sdn.get(i - 2) is not None:
                b.wait(eng, ssem, sdn2[i - 2])
            b.op(eng, lambda e, bi=bi, tt3=tt3: e.tensor_scalar(
                out=tt3[:], in0=ub[bi][:], scalar1=scl[:, 3:4], scalar2=MAGIC,
                op0=ALU.mult, op1=ALU.add))
            b.op(eng, lambda e, bi=bi, tt3=tt3: e.tensor_scalar(
                out=q2b[bi][:], in0=tt3[:], scalar1=MAGIC, scalar2=32767.0,
                op0=ALU.subtract, op1=ALU.min))
            b.op(eng, lambda e, bi=bi, tt1=tt1: e.tensor_scalar(
                out=tt1[:], in0=qgb[bi][:], scalar1=scl[:, 4:5], scalar2=None, op0=ALU.mult),
                sem=sq)
            vv = b.val(sq)
            b.wait("scalar", sq, vv)
            b.op("scalar", lambda e, tt1=tt1, tt2=tt2: e.activation(
                out=tt2[:], in_=tt1[:], func=AF.Sigmoid), sem="S")
            sv = b.val("S")
            b.wait(eng, "S", sv)
            b.op(eng, lambda e, tt2=tt2: e.tensor_scalar(
                out=tt2[:], in0=tt2[:], scalar1=65536.0, scalar2=MAGIC,
                op0=ALU.mult, op1=ALU.add))
            b.op(eng, lambda e, tt2=tt2, ttu=ttu: e.tensor_scalar(
                out=ttu[:], in0=tt2[:], scalar1=MAGIC, scalar2=65535.0,
                op0=ALU.subtract, op1=ALU.min))
            b.op(eng, lambda e, tt2=tt2, ttu=ttu: e.tensor_scalar(
                out=tt2[:], in0=ttu[:], scalar1=1.0, scalar2=None, op0=ALU.mult))
            b.op(eng, lambda e, tt1=tt1, tt2=tt2: e.tensor_tensor(
                out=tt1[:], in0=tt1[:], in1=tt2[:], op=ALU.mult))
            b.op(eng, lambda e, tt1=tt1, tt2=tt2: e.tensor_scalar(
                out=tt2[:], in0=tt1[:], scalar1=scl[:, 7:8], scalar2=MAGIC,
                op0=ALU.mult, op1=ALU.add))
            b.op(eng, lambda e, bi=bi, tt2=tt2: e.tensor_scalar(
                out=qgb[bi][:], in0=tt2[:], scalar1=MAGIC, scalar2=32767.0,
                op0=ALU.subtract, op1=ALU.min))
            b.op(eng, lambda e, bi=bi, tt3=tt3: e.tensor_scalar(
                out=tt3[:], in0=qgb[bi][:], scalar1=1.0, scalar2=None, op0=ALU.mult))
            b.op(eng, lambda e, bi=bi, tt2=tt2: e.tensor_scalar(
                out=tt2[:], in0=q2b[bi][:], scalar1=1.0, scalar2=None, op0=ALU.mult))
            b.op(eng, lambda e, tt2=tt2, tt3=tt3: e.tensor_tensor(
                out=tt3[:], in0=tt3[:], in1=tt2[:], op=ALU.mult))
            if bi == 0:
                b.op(eng, lambda e, tt3=tt3, mmc=mmc: e.reduce_max(
                    out=mmc, in_=tt3[:], axis=AX.X, apply_absolute_value=True))
                b.op(eng, lambda e, mm5=mm5, mmc=mmc: e.tensor_tensor(
                    out=mm5[:], in0=mm5[:], in1=mmc, op=ALU.max), sem=sq)
            else:
                b.op(eng, lambda e, tt3=tt3: e.reduce_max(
                    out=mcg11[:], in_=tt3[:], axis=AX.XYZWC,
                    apply_absolute_value=True))
                b.op(eng, lambda e: e.tensor_scalar(
                    out=m4g11[:], in0=m4g11[:], scalar1=mcg11[:], scalar2=None,
                    op0=ALU.max), sem=sq)
            vdn[i] = b.val(sq)
            b.wait(seng, sq, vdn[i])
            sdn[i] = b.dma(seng, upq_d[it][:, tt * 512:(tt + 1) * 512], q2b[bi][:])
            sdn2[i] = b.dma(seng, g2q_d[it][:, tt * 512:(tt + 1) * 512], qgb[bi][:])
        b.wait("vector", "G")
        b.op("vector", lambda e: e.tensor_tensor(out=m5[0:1, :], in0=m5[0:1, :],
                                                 in1=m4g11[:], op=ALU.max), sem="V")
        part_absmax_to_scalar(b, c, m5[:], c.psB, mrow[0:1, 5:6])
        b.op("vector", lambda e: e.tensor_scalar(out=mrow[0:1, 6:7], in0=scl[0:1, 6:7],
                                                 scalar1=mrow[0:1, 5:6], scalar2=None,
                                                 op0=ALU.mult))
        b.op("vector", lambda e: e.tensor_scalar(out=mrow[0:1, 5:6], in0=mrow[0:1, 6:7],
                                                 scalar1=scl[0:1, 2:3], scalar2=None,
                                                 op0=ALU.mult), sem="V")
        collective_max(b, c, mrow[0:1, 5:6], 1, tag + "c4", mrow[0:1, 5:6])
        make_scale_tiles(b, c, mrow[0:1, 5:6], c.psB, scl2[:, 0:1], scl2[:, 1:2])
        b.op("vector", lambda e: e.tensor_scalar(out=scl2[:, 2:3], in0=scl[:, 6:7],
                                                 scalar1=scl[:, 2:3], scalar2=None,
                                                 op0=ALU.mult))
        b.op("vector", lambda e: e.tensor_tensor(out=scl2[:, 2:3], in0=scl2[:, 2:3],
                                                 in1=scl2[:, 1:2], op=ALU.mult), sem="V")

        wdb = [b.sb(f"{tag}_wdb{i}", [128, 1024], BF16, st) for i in range(2)]
        ob = [b.sb(f"{tag}_ob{i}", [128, 512], BF16, st) for i in range(2)]
        b.wait("sync", "gl")
        b.wait("sync", "V")
        b.wait("sync", "G")
        b.wait("sync", "dG")
        b.wait("sync", "dC")
        b.wait("gpsimd", "dG")
        b.wait("gpsimd", "dC")
        b.wait("gpsimd", "V")
        b.wait("gpsimd", "G")
        ldv = {}
        ldg = {}
        ldw = {}
        odone = {}
        for tt in range(4):
            evv = b.val("V")
            b.wait("tensor", "V", evv)
            for it in range(24):
                i = tt * 24 + it
                bi = i % 2
                b.wait("sync", "V")
                b.wait("gpsimd", "V")
                ldv[i] = b.dma("sync", q2b[bi][:], upq_d[it][:, tt * 512:(tt + 1) * 512],
                               sem="gl")
                ldg[i] = b.dma("gpsimd", qgb[bi][:], g2q_d[it][:, tt * 512:(tt + 1) * 512],
                               sem="gq2")
                if odone.get(i - 2) is not None:
                    b.wait("sync", "T", odone[i - 2])
                ldw[i] = b.dma("sync", wdb[bi][:], stash_d[it], sem=f"{tag}_wd")
                b.wait("tensor", f"{tag}_wd", ldw[i])
                b.wait("vector", "gl", ldv[i])
                b.wait("vector", "gq2", ldg[i])
                if odone.get(i - 2) is not None:
                    b.wait("vector", "T", odone[i - 2])
                b.op("vector", lambda e, bi=bi: e.tensor_scalar(
                    out=t1[:], in0=qgb[bi][:], scalar1=1.0, scalar2=None, op0=ALU.mult))
                b.op("vector", lambda e, bi=bi: e.tensor_scalar(
                    out=t2[:], in0=q2b[bi][:], scalar1=1.0, scalar2=None, op0=ALU.mult))
                b.op("vector", lambda e: e.tensor_tensor(out=t1[:], in0=t1[:], in1=t2[:],
                                                         op=ALU.mult))
                b.op("vector", lambda e: e.tensor_scalar(
                    out=t2[:], in0=t1[:], scalar1=scl2[:, 2:3], scalar2=MAGIC,
                    op0=ALU.mult, op1=ALU.add))
                b.op("vector", lambda e: e.tensor_scalar(
                    out=t2u[:].bitcast(I16), in0=t2[:], scalar1=MAGIC, scalar2=32767.0,
                    op0=ALU.subtract, op1=ALU.min))
                b.op("vector", lambda e, bi=bi: e.tensor_scalar(
                    out=ob[bi][:], in0=t2u[:].bitcast(I16), scalar1=scl2[:, 0:1],
                    scalar2=None, op0=ALU.mult), sem="V")
                vv = b.val("V")
                b.wait("tensor", "V", vv)
                for hh in range(8):
                    pd = c.psA[:, hh % 4 * 512:(hh % 4 + 1) * 512] if hh < 4 else                         c.psB[:, (hh - 4) * 512:(hh - 3) * 512]
                    b.op("tensor", lambda e, hh=hh, it=it, bi=bi, pd=pd: e.matmul(
                        out=pd, lhsT=wdb[bi][:, hh * 128:(hh + 1) * 128],
                        rhs=ob[bi][:], start=(it == 0), stop=(it == 23)),
                        sem="T" if hh == 7 else None)
                odone[i] = b.val("T")
            tv = b.val("T")
            b.wait("vector", "T", tv)
            for hh in range(8):
                pd = c.psA[:, hh % 4 * 512:(hh % 4 + 1) * 512] if hh < 4 else                     c.psB[:, (hh - 4) * 512:(hh - 3) * 512]
                xs = c.x[:, hh * T + tt * 512: hh * T + (tt + 1) * 512]
                b.op("vector", lambda e, pd=pd, xs=xs: e.scalar_tensor_tensor(
                    out=xs, in0=pd, scalar=0.5, in1=xs, op0=ALU.mult, op1=ALU.add),
                    sem="V")


def stage_l0ffn(inputs):
    d = {}
    for n in ("w_gate", "w_up", "w_down"):
        w = np.ascontiguousarray(np.asarray(inputs[f"L0_ffn_{n}"], dtype=np.float32))
        d[f"L0_ffn_{n}"] = w
        d[f"L0_ffn_{n}T"] = np.ascontiguousarray(w.T)
    return d





def emit_l1attn(b: Bld, c, tag):
    Wqt = precast_w(b, c, "l1aWq", b.din("L1_attn_Wqt", [8, 128, D], F32), D * D)
    Wkt = precast_w(b, c, "l1aWk", b.din("L1_attn_Wkt", [8, 128, D], F32), D * D)
    Wvr = precast_w(b, c, "l1aWv", b.din("L1_attn_Wv", [D, D], F32), D * D)
    Wor = precast_w(b, c, "l1aWo", b.din("L1_attn_Wo", [D, D], F32), D * D)
    bqc = b.din("L1_attn_bqc", [128, 8], F32)
    bkc = b.din("L1_attn_bkc", [128, 8], F32)
    bvR = b.din("L1_attn_bvR", [1, D], F32)
    boR = b.din("L1_attn_boR", [1, D], F32)
    sidx = b.din("L1_attn_sidx", [48, 1], I32)
    lntm_d = b.dtmp("scr_lntm", [T, D], BF16)
    v_d = b.dtmp("scr_v", [16, 128, D], BF16)
    r_d = b.dtmp("scr_rank", [256 * 128], F32)
    if os.environ.get("DBG_BIS"):
        b._bislog = b.dtmp("scr_bislog", [28 * 80], F32)

    ln_compute(b, c, apply_bf16=True)
    scr32 = c.scrb[:].bitcast(F32)

    with contextlib.ExitStack() as st:
        fence_loads(b)
        c.iota_row = b.sb("iota_row", [128, T], F32, st)
        c.iota_col1 = b.sb("iota_col1", [128, 1], F32, st)
        c.tri = b.sb("tri_c", [128, 128], BF16, st)
        b.dma("sync", c.iota_row[:], b.din("c_iota_row", [128, T], F32))
        b.dma("sync", c.iota_col1[:], b.din("c_iota_col1", [128, 1], F32))
        b.dma("sync", c.tri[:], b.din("c_tri", [128, 128], BF16))
        b.wait("vector", "dS")
        lnb32 = c.lnb[:].bitcast(F32)
        mvrow = b.sb(f"{tag}_mvrow", [1, D], F32, st)
        mvb = scr32[0:48, 4096:4096 + D]
        
        m_all = b.sb(f"{tag}_mall", [128, 256], F32, st)
        lo = b.sb(f"{tag}_lo", [1, 16], F32, st)
        hi = b.sb(f"{tag}_hi", [1, 16], F32, st)
        mid = b.sb(f"{tag}_mid", [1, 16], F32, st)
        sel = b.sb(f"{tag}_sel", [1, 16], I32, st)
        cnt16 = b.sb(f"{tag}_cnt", [1, 16], F32, st)
        seln = b.sb(f"{tag}_seln", [1, 16], I32, st)
        red16 = b.sb(f"{tag}_red16", [128, 16], F32, st)
        red16b = b.sb(f"{tag}_red16b", [128, 16], BF16, st)
        rowx = None
        rowy = None
        cw_sb = None
        mask01 = b.sb(f"{tag}_mask", [128, 256], BF16, st)
        tr128 = b.sb(f"{tag}_tr128", [128, 128], F32, st)
        idx48 = b.sb(f"{tag}_idx", [48, 1], I32, st)
        sid48 = b.sb(f"{tag}_sid", [48, 1], I32, st)
        wmvbo_b = c.scrb[0:1, 14848:14848 + D]
        mvcol = b.sb(f"{tag}_mvcol", [128, 8], BF16, st)
        bvb = b.sb(f"{tag}_bvb", [1, D], BF16, st)
        wsem = f"{tag}_wl"
        stage = c.scrb[:, 0:T]

        for ch in range(NCH):
            sl = slice(ch * T, (ch + 1) * T)
            b.wait("vector", "dS")
            b.op("vector", lambda e, sl=sl: e.tensor_tensor(out=c.tf32[:], in0=c.x[:, sl],
                                                            in1=c.mean[:], op=ALU.subtract))
            b.op("vector", lambda e: e.tensor_tensor(out=c.tf32[:], in0=c.tf32[:],
                                                     in1=c.rstd[:], op=ALU.mult), sem="V")
            vv = b.val("V")
            b.wait("tensor", "V", vv)
            for g in range(4):
                for j in range(4):
                    tl = g * 4 + j
                    b.op("tensor", lambda e, tl=tl, j=j: e.transpose(
                        out=c.psB[:, j * 512:j * 512 + 128],
                        in_=c.tf32[:, tl * 128:(tl + 1) * 128], identity=c.identf[:]),
                        sem="T" if j == 3 else None)
                tv = b.val("T")
                b.wait("vector", "T", tv)
                for j in range(4):
                    tl = g * 4 + j
                    b.op("vector", lambda e, tl=tl, j=j: e.tensor_copy(
                        out=stage[:, tl * 128:(tl + 1) * 128],
                        in_=c.psB[:, j * 512:j * 512 + 128]),
                        sem="V" if j == 3 else None)
                vv2 = b.val("V")
                b.wait("tensor", "V", vv2)
            vv2 = b.val("V")
            b.wait("sync", "V", vv2)
            b.dma("sync", bass.AP(lntm_d.tensor, ch * 128,
                                  [[D, 128], [128 * D, 16], [1, 128]]),
                  stage[:].rearrange("p (a j) -> p a j", j=128))
        b.wait("vector", "dS")

        wb2 = [b.sb(f"{tag}_wb{i}", [128, 1024], BF16, st) for i in range(2)]
        vst = b.sb(f"{tag}_vst", [128, 1024], BF16, st)
        ksf = b.sb(f"{tag}_ksf", [128, 8 * 48], BF16, st)
        qch = b.sb(f"{tag}_qch", [128, T], BF16, st)
        bqt = b.sb(f"{tag}_bqt", [128, 8], F32, st)
        bkt = b.sb(f"{tag}_bkt", [128, 8], F32, st)
        b.dma("sync", bqt[:], bqc)
        b.dma("sync", bkt[:], bkc)
        ln48 = c.scrb[0:48, 4096:4096 + D]
        b.dma("sync", sid48[:], sidx)
        b.wait("vector", "dS")
        b.op("vector", lambda e: e.tensor_scalar(out=sid48[:], in0=sid48[:],
                                                 scalar1=float(T - 1), scalar2=None,
                                                 op0=ALU.min), sem="V")
        vv = b.val("V")
        b.wait("gpsimd", "V", vv)
        gx_t = b.gather("gpsimd", ln48, lntm_d, sid48[:], "gx")
        b.wait("vector", "gx", gx_t)
        ln48f = scr32[0:48, 0:D]
        b.op("vector", lambda e: e.tensor_copy(out=ln48f, in_=ln48), sem="V")
        vv = b.val("V")
        b.wait("tensor", "V", vv)
        for k in range(8):
            b.op("tensor", lambda e, k=k: e.transpose(
                out=c.psB[:, k * 48:(k + 1) * 48], in_=ln48f[0:48, k * 128:(k + 1) * 128],
                identity=c.identf[0:48, 0:48]), sem="T" if k == 7 else None)
        tv = b.val("T")
        b.wait("vector", "T", tv)
        ln48b = b.sb(f"{tag}_l48b", [128, 384], BF16, st)
        b.op("vector", lambda e: e.tensor_copy(out=ln48b[:], in_=c.psB[:, 0:384]),
             sem="V")

        scrK = c.scrb
        wsem = f"{tag}_wl"
        tprev = None
        for cc in range(8):
            if tprev is not None:
                b.wait("sync", "T", tprev)
            b.wait("sync", "V")
            b.dma("sync", wb2[cc % 2][:],
                  bass.AP(Wkt.tensor, cc * 128 * D, [[D, 128], [1, D]]), sem=wsem)
            b.wait("tensor", wsem, b.val(wsem))
            b.wait("tensor", "V")
            for tt in range(4):
                for k in range(8):
                    b.op("tensor", lambda e, k=k, tt=tt, wb=wb2[cc % 2]: e.matmul(
                        out=c.psA[:, tt * 512:(tt + 1) * 512],
                        lhsT=wb[:, k * 128:(k + 1) * 128],
                        rhs=c.lnb[:, k * T + tt * 512:k * T + (tt + 1) * 512],
                        start=(k == 0), stop=(k == 7)),
                        sem="T" if k == 7 else None)
            for k in range(8):
                b.op("tensor", lambda e, k=k, wb=wb2[cc % 2]: e.matmul(
                    out=c.psB[:, 1024:1072], lhsT=wb[:, k * 128:(k + 1) * 128],
                    rhs=ln48b[:, k * 48:(k + 1) * 48],
                    start=(k == 0), stop=(k == 7)), sem="T" if k == 7 else None)
            tprev = b.val("T")
            b.wait("vector", "T", tprev)
            for tt in range(4):
                b.op("vector", lambda e, cc=cc, tt=tt: e.tensor_scalar(
                    out=scrK[:, cc * T + tt * 512:cc * T + (tt + 1) * 512],
                    in0=c.psA[:, tt * 512:(tt + 1) * 512], scalar1=bkt[:, cc:cc + 1],
                    scalar2=None, op0=ALU.add))
            b.op("vector", lambda e, cc=cc: e.tensor_scalar(
                out=ksf[:, cc * 48:(cc + 1) * 48], in0=c.psB[:, 1024:1072],
                scalar1=bkt[:, cc:cc + 1], scalar2=None, op0=ALU.add), sem="V")

        b.dma("gpsimd", bvb[:], bvR)
        b.wait("tensor", "dG")
        sprev = {}
        for tg in range(4):
            b.wait("tensor", "V")
            for tsub in range(4):
                for half in range(2):
                    ps = (c.psA, c.psB)[tsub // 2][:, (tsub % 2) * 1024 + half * 512:
                                                   (tsub % 2) * 1024 + (half + 1) * 512]
                    b.op("tensor", lambda e, ps=ps, half=half: e.matmul(
                        out=ps, lhsT=c.onesb_row[0:1, 0:128],
                        rhs=bvb[0:1, half * 512:(half + 1) * 512],
                        start=True, stop=False))
            wop = None
            b.wait("sync", "T")
            for k in range(8):
                if wop is not None and k >= 2:
                    b.wait("sync", "T", wop)
                b.dma("sync", wb2[k % 2][:],
                      bass.AP(Wvr.tensor, k * 128 * D, [[D, 128], [1, D]]), sem=wsem)
                b.wait("tensor", wsem, b.val(wsem))
                for tsub in range(4):
                    ttile = tg * 4 + tsub
                    for half in range(2):
                        ps = (c.psA, c.psB)[tsub // 2][:, (tsub % 2) * 1024 + half * 512:
                                                       (tsub % 2) * 1024 + (half + 1) * 512]
                        b.op("tensor", lambda e, k=k, half=half, ttile=ttile, ps=ps: e.matmul(
                            out=ps,
                            lhsT=c.lnb[:, k * T + ttile * 128:k * T + (ttile + 1) * 128],
                            rhs=wb2[k % 2][:, half * 512:(half + 1) * 512],
                            start=False, stop=(k == 7)),
                            sem="T" if (tsub == 3 and half == 1) else None)
                wop = b.val("T")
            tv = b.val("T")
            b.wait("vector", "T", tv)
            for tsub in range(4):
                ttile = tg * 4 + tsub
                if sprev.get(ttile - 1) is not None:
                    b.wait("vector", "vs", sprev[ttile - 1])
                b.wait("vector", "T")
                ps0 = (c.psA, c.psB)[tsub // 2]
                b.op("vector", lambda e, tsub=tsub, ps0=ps0: e.tensor_copy(
                    out=vst[:], in_=ps0[:, (tsub % 2) * 1024:(tsub % 2) * 1024 + 1024]),
                    sem="V")
                vv = b.val("V")
                b.wait("sync", "V", vv)
                sprev[ttile] = b.dma("sync", v_d[ttile], vst[:], sem="vs")
        b.wait("gpsimd", "T")
        b.wait("gpsimd", "V")
        b.wait("gpsimd", "vs")
        mprev = None
        for ttile in range(16):
            if mprev is not None:
                b.wait("gpsimd", "T", mprev)
            lv = b.dma("gpsimd", wb2[ttile % 2][:], v_d[ttile], sem="mvl")
            b.wait("tensor", "mvl", b.val("mvl"))
            if ttile == 0:
                b.wait("tensor", "V")
            for half in range(2):
                b.op("tensor", lambda e, half=half, ttile=ttile: e.matmul(
                    out=c.psA[0:1, half * 512:(half + 1) * 512],
                    lhsT=c.onesb_col[:, 0:1],
                    rhs=wb2[ttile % 2][:, half * 512:(half + 1) * 512],
                    start=(ttile == 0), stop=(ttile == 15)),
                    sem="T" if half == 1 else None)
            mprev = b.val("T")
        tv = b.val("T")
        b.wait("vector", "T", tv)
        b.op("vector", lambda e: e.tensor_scalar(out=mvrow[:], in0=c.psA[0:1, 0:1024],
                                                 scalar1=1.0 / T, scalar2=None,
                                                 op0=ALU.mult), sem="V")
        vv = b.val("V")
        b.wait("tensor", "V", vv)
        for k in range(8):
            b.op("tensor", lambda e, k=k: e.transpose(
                out=c.psB[:, 1024 + k:1024 + k + 1], in_=mvrow[0:1, k * 128:(k + 1) * 128],
                identity=c.identf[0:1, 0:1]), sem="T" if k == 7 else None)
        tv = b.val("T")
        b.wait("vector", "T", tv)
        b.op("vector", lambda e: e.tensor_copy(out=mvcol[:], in_=c.psB[:, 1024:1032]), sem="V")

        tprev = None
        for cc in range(8):
            if tprev is not None:
                b.wait("sync", "T", tprev)
            b.wait("sync", "V")
            b.dma("sync", wb2[0][:],
                  bass.AP(Wqt.tensor, cc * 128 * D, [[D, 128], [1, D]]), sem=wsem)
            b.wait("tensor", wsem, b.val(wsem))
            b.wait("tensor", "V")
            for tt in range(4):
                for k in range(8):
                    b.op("tensor", lambda e, k=k, tt=tt: e.matmul(
                        out=c.psA[:, tt * 512:(tt + 1) * 512],
                        lhsT=wb2[0][:, k * 128:(k + 1) * 128],
                        rhs=c.lnb[:, k * T + tt * 512:k * T + (tt + 1) * 512],
                        start=(k == 0), stop=(k == 7)),
                        sem="T" if k == 7 else None)
            tprev = b.val("T")
            b.wait("vector", "T", tprev)
            for tt in range(4):
                b.op("vector", lambda e, cc=cc, tt=tt: e.tensor_scalar(
                    out=qch[:, tt * 512:(tt + 1) * 512],
                    in0=c.psA[:, tt * 512:(tt + 1) * 512], scalar1=bqt[:, cc:cc + 1],
                    scalar2=None, op0=ALU.add), sem="V" if tt == 3 else None)
            vv = b.val("V")
            b.wait("tensor", "V", vv)
            for hp in range(2):
                hh = cc * 2 + hp
                pbase = hp * 64
                for ltg in range(4):
                    for j in range(4):
                        lt = ltg * 4 + j
                        b.op("tensor", lambda e, lt=lt, j=j, pbase=pbase, cc=cc: e.matmul(
                            out=c.psB[:, j * 512:j * 512 + 48],
                            lhsT=qch[pbase:pbase + 64, lt * 128:(lt + 1) * 128],
                            rhs=ksf[pbase:pbase + 64, cc * 48:cc * 48 + 48],
                            start=True, stop=True, tile_position=(pbase, 0)),
                            sem="T" if j == 3 else None)
                    tv = b.val("T")
                    b.wait("vector", "T", tv)
                    for j in range(4):
                        lt = ltg * 4 + j
                        col = hh * 16 + lt
                        b.op("vector", lambda e, j=j: e.reduce_max(
                            out=c.tf32[:, 0:1], in_=c.psB[:, j * 512:j * 512 + 45],
                            axis=AX.X))
                        b.op("vector", lambda e, j=j: e.reduce_sum(
                            out=c.tf32[:, 1:2], in_=c.psB[:, j * 512:j * 512 + 45],
                            axis=AX.X))
                        b.op("vector", lambda e, col=col: e.scalar_tensor_tensor(
                            out=m_all[:, col:col + 1], in0=c.tf32[:, 1:2],
                            scalar=-1.0 / 45.0, in1=c.tf32[:, 0:1],
                            op0=ALU.mult, op1=ALU.add), sem="V" if j == 3 else None)
                    vv = b.val("V")
                    b.wait("tensor", "V", vv)

        b.op("vector", lambda e: e.reduce_max(
            out=red16[:], in_=m_all[:].rearrange("p (h l) -> p h l", l=16), axis=AX.X))
        b.op("vector", lambda e: e.tensor_copy(out=tr128[:, 0:16], in_=red16[:]), sem="V")
        vv = b.val("V")
        b.wait("tensor", "V", vv)
        b.op("tensor", lambda e: e.transpose(out=c.psB[0:16, 0:128], in_=tr128[:, 0:16],
                                             identity=c.identf[:]), sem="T")
        tv = b.val("T")
        b.wait("vector", "T", tv)
        b.op("vector", lambda e: e.reduce_max(out=c.tf32[0:16, 0:1],
                                              in_=c.psB[0:16, 0:128], axis=AX.X), sem="V")
        vv = b.val("V")
        b.wait("tensor", "V", vv)
        b.op("tensor", lambda e: e.matmul(out=c.psB[0:1, 128:144], lhsT=c.tf32[0:16, 0:1],
                                          rhs=c.identf[0:16, 0:16], start=True, stop=True),
             sem="T")
        tv = b.val("T")
        b.wait("vector", "T", tv)
        b.op("vector", lambda e: e.tensor_scalar(out=hi[:], in0=c.psB[0:1, 128:144],
                                                 scalar1=0.1, scalar2=None, op0=ALU.add))
        b.op("vector", lambda e: e.tensor_scalar(
            out=c.tf32[:, 0:256], in0=m_all[:], scalar1=-1.0, scalar2=None, op0=ALU.mult))
        b.op("vector", lambda e: e.reduce_max(
            out=red16[:], in_=c.tf32[:, 0:256].rearrange("p (h l) -> p h l", l=16),
            axis=AX.X))
        b.op("vector", lambda e: e.tensor_copy(out=tr128[:, 0:16], in_=red16[:]), sem="V")
        vv = b.val("V")
        b.wait("tensor", "V", vv)
        b.op("tensor", lambda e: e.transpose(out=c.psB[0:16, 0:128], in_=tr128[:, 0:16],
                                             identity=c.identf[:]), sem="T")
        tv = b.val("T")
        b.wait("vector", "T", tv)
        b.op("vector", lambda e: e.reduce_max(out=c.tf32[0:16, 0:1], in_=c.psB[0:16, 0:128],
                                              axis=AX.X), sem="V")
        vv = b.val("V")
        b.wait("tensor", "V", vv)
        b.op("tensor", lambda e: e.matmul(out=c.psB[0:1, 128:144], lhsT=c.tf32[0:16, 0:1],
                                          rhs=c.identf[0:16, 0:16], start=True, stop=True),
             sem="T")
        tv = b.val("T")
        b.wait("vector", "T", tv)
        b.op("vector", lambda e: e.tensor_scalar(out=lo[:], in0=c.psB[0:1, 128:144],
                                                 scalar1=-1.0, scalar2=-1.0,
                                                 op0=ALU.mult, op1=ALU.add), sem="V")
        for itn in range(28):
            b.op("vector", lambda e: e.tensor_tensor(out=mid[:], in0=lo[:], in1=hi[:],
                                                     op=ALU.add))
            b.op("vector", lambda e: e.tensor_scalar(out=mid[:], in0=mid[:], scalar1=0.5,
                                                     scalar2=None, op0=ALU.mult), sem="V")
            vv = b.val("V")
            b.wait("tensor", "V", vv)
            b.op("tensor", lambda e: e.matmul(out=c.psB[0:128, 0:16], lhsT=c.onesf_row[:],
                                              rhs=mid[:], start=True, stop=True), sem="T")
            tv = b.val("T")
            b.wait("vector", "T", tv)
            b.op("vector", lambda e: e.tensor_tensor(
                out=mask01[:].rearrange("p (h l) -> p h l", l=16),
                in0=m_all[:].rearrange("p (h l) -> p h l", l=16),
                in1=c.psB[0:128, 0:16].unsqueeze(2).broadcast_to([128, 16, 16]),
                op=ALU.is_gt))
            b.op("vector", lambda e: e.reduce_sum(
                out=red16[:], in_=mask01[:].rearrange("p (h l) -> p h l", l=16),
                axis=AX.X))
            b.op("vector", lambda e: e.tensor_copy(out=red16b[:], in_=red16[:]), sem="V")
            vv = b.val("V")
            b.wait("tensor", "V", vv)
            b.op("tensor", lambda e: e.matmul(out=c.psB[0:1, 16:32], lhsT=c.onesb_col[:, 0:1],
                                              rhs=red16b[:], start=True, stop=True), sem="T")
            tv = b.val("T")
            b.wait("vector", "T", tv)
            b.op("vector", lambda e: e.tensor_scalar(out=sel[:], in0=c.psB[0:1, 16:32],
                                                     scalar1=39.5, scalar2=None,
                                                     op0=ALU.is_gt))
            if os.environ.get("DBG_BIS"):
                b.op("vector", lambda e: e.tensor_copy(out=cnt16[:], in_=c.psB[0:1, 16:32]),
                     sem="V")
                vv = b.val("V")
                b.wait("sync", "V", vv)
                b.dma("sync", bass.AP(b._bislog.tensor, itn * 80, [[1, 16]]), mid[:])
                b.dma("sync", bass.AP(b._bislog.tensor, itn * 80 + 16, [[1, 16]]), cnt16[:])
                b.dma("sync", bass.AP(b._bislog.tensor, itn * 80 + 32, [[1, 16]]), sel[:])
                b.dma("sync", bass.AP(b._bislog.tensor, itn * 80 + 48, [[1, 16]]), lo[:])
                b.dma("sync", bass.AP(b._bislog.tensor, itn * 80 + 64, [[1, 16]]), hi[:])
                b.wait("vector", "dS")
            b.op("vector", lambda e: e.tensor_scalar(out=seln[:], in0=c.psB[0:1, 16:32],
                                                     scalar1=39.5, scalar2=None,
                                                     op0=ALU.is_le))
            b.op("vector", lambda e: e.select(out=lo[:], mask=sel[:], on_true=mid[:],
                                              on_false=lo[:], add_drain=True))
            b.op("vector", lambda e: e.select(out=hi[:], mask=seln[:], on_true=mid[:],
                                              on_false=hi[:], add_drain=True), sem="V")
        vv = b.val("V")
        b.wait("tensor", "V", vv)
        b.op("tensor", lambda e: e.matmul(out=c.psB[0:128, 0:16], lhsT=c.onesf_row[:],
                                          rhs=lo[:], start=True, stop=True), sem="T")
        tv = b.val("T")
        b.wait("vector", "T", tv)
        b.op("vector", lambda e: e.tensor_tensor(
            out=mask01[:].rearrange("p (h l) -> p h l", l=16),
            in0=m_all[:].rearrange("p (h l) -> p h l", l=16),
            in1=c.psB[0:128, 0:16].unsqueeze(2).broadcast_to([128, 16, 16]),
            op=ALU.is_gt), sem="V")
        vv = b.val("V")
        b.wait("tensor", "V", vv)
        b.op("tensor", lambda e: e.matmul(out=c.psA[:, 0:256], lhsT=c.tri[:],
                                          rhs=mask01[:], start=True, stop=True), sem="T")
        tv = b.val("T")
        b.wait("vector", "T", tv)
        cw_sb = lnb32[0:128, 3584:3840]
        b.op("vector", lambda e, cw_sb=cw_sb: e.tensor_copy(out=cw_sb, in_=c.psA[:, 0:256]),
             sem="V")
        rowx = lnb32[0:1, 3072:3328]
        rowy = lnb32[0:1, 3328:3584]
        rowt = lnb32[0:1, 3840:4096]
        vv = b.val("V")
        b.wait("tensor", "V", vv)
        b.op("tensor", lambda e: e.matmul(out=c.psB[0:1, 256:512], lhsT=c.onesb_col[:, 0:1],
                                          rhs=mask01[:], start=True, stop=True), sem="T")
        tv = b.val("T")
        b.wait("vector", "T", tv)
        b.op("vector", lambda e, rowx=rowx: e.tensor_copy(
            out=rowx, in_=c.psB[0:1, 256:512]))
        b.op("vector", lambda e, rowt=rowt: e.tensor_copy(
            out=rowt, in_=c.psB[0:1, 256:512]), sem="V")
        cur, oth = rowx, rowy
        for k in (1, 2, 4, 8):
            b.op("vector", lambda e, cur=cur, oth=oth: e.tensor_copy(out=oth, in_=cur))
            b.op("vector", lambda e, cur=cur, oth=oth, k=k: e.tensor_tensor(
                out=oth.rearrange("p (h l) -> p h l", l=16)[:, :, k:16],
                in0=cur.rearrange("p (h l) -> p h l", l=16)[:, :, k:16],
                in1=cur.rearrange("p (h l) -> p h l", l=16)[:, :, 0:16 - k],
                op=ALU.add))
            cur, oth = oth, cur
        b.op("vector", lambda e, cur=cur, rowt=rowt: e.tensor_tensor(
            out=cur, in0=cur, in1=rowt, op=ALU.subtract), sem="V")
        excl = cur
        vv = b.val("V")
        b.wait("tensor", "V", vv)
        b.op("tensor", lambda e, excl=excl: e.matmul(out=c.psB[0:128, 0:256],
                                                     lhsT=c.onesf_row[:], rhs=excl,
                                                     start=True, stop=True), sem="T")
        tv = b.val("T")
        b.wait("vector", "T", tv)
        b.op("vector", lambda e: e.tensor_tensor(out=cw_sb, in0=cw_sb,
                                                 in1=c.psB[0:128, 0:256], op=ALU.add))
        b.op("vector", lambda e: e.tensor_tensor(out=cw_sb, in0=cw_sb, in1=mask01[:],
                                                 op=ALU.mult), sem="V")
        vv = b.val("V")
        b.wait("tensor", "V", vv)
        for half in range(2):
            b.op("tensor", lambda e, half=half: e.transpose(
                out=c.psB[:, half * 512:half * 512 + 128],
                in_=cw_sb[0:128, half * 128:(half + 1) * 128], identity=c.identf[:]),
                sem="T" if half == 1 else None)
        tv = b.val("T")
        b.wait("vector", "T", tv)
        for half in range(2):
            b.wait("vector", "dS")
            b.op("vector", lambda e, half=half: e.tensor_copy(
                out=tr128[:], in_=c.psB[:, half * 512:half * 512 + 128]), sem="V")
            vv = b.val("V")
            b.wait("sync", "V", vv)
            b.dma("sync", bass.AP(r_d.tensor, half * 128 * 128, [[128, 128], [1, 128]]),
                  tr128[:])
        b.wait("sync", "dS")

        pt_all = b.sb(f"{tag}_ptall", [128, 16 * 16 * 40], BF16, st)
        sums_all = b.sb(f"{tag}_sums", [48, 16], F32, st)
        p40 = c.lnb[0:48, 0:4096].bitcast(F32)
        qred = b.sb(f"{tag}_qred", [128, 48], BF16, st)
        l40 = b.sb(f"{tag}_l40", [48, D], BF16, st)
        mxneg = b.sb(f"{tag}_mxneg", [48, 1], F32, st)
        tprev3 = None
        for hh in range(16):
            rhbc = lnb32[0:48, 3072:3072 + T]
            b.wait("sync", "V")
            b.dma("sync", rhbc, bass.AP(r_d.tensor, hh * T, [[0, 48], [1, T]]), sem="rl")
            b.wait("vector", "rl", b.val("rl"))
            ef = lnb32[0:48, 5120:5120 + T]
            b.op("vector", lambda e, rhbc=rhbc: e.tensor_scalar(out=ef, in0=rhbc,
                                                     scalar1=c.iota_col1[0:48, :],
                                                     scalar2=None, op0=ALU.is_equal))
            b.op("vector", lambda e: e.tensor_tensor(out=ef, in0=ef,
                                                     in1=c.iota_row[0:48, :], op=ALU.mult))
            b.op("vector", lambda e: e.reduce_sum(out=lnb32[0:48, 7168:7169], in_=ef,
                                                  axis=AX.X))
            b.op("vector", lambda e: e.tensor_scalar(
                out=lnb32[0:48, 7168:7169], in0=lnb32[0:48, 7168:7169],
                scalar1=float(T - 1), scalar2=0.0, op0=ALU.min, op1=ALU.max))
            b.op("vector", lambda e: e.tensor_copy(out=idx48[:], in_=lnb32[0:48, 7168:7169]),
                 sem="V")
            vv = b.val("V")
            b.wait("gpsimd", "V", vv)
            gx_t = b.gather("gpsimd", l40[:], lntm_d, idx48[:], "gx")
            b.wait("vector", "gx", gx_t)
            l40f = lnb32[0:48, 2048:2048 + D]
            b.op("vector", lambda e, l40f=l40f: e.tensor_copy(out=l40f, in_=l40[:]), sem="V")
            vv = b.val("V")
            b.wait("tensor", "V", vv)
            for k in range(8):
                b.op("tensor", lambda e, k=k: e.transpose(
                    out=c.psB[:, k * 48:(k + 1) * 48], in_=l40f[0:48, k * 128:(k + 1) * 128],
                    identity=c.identf[0:48, 0:48]), sem="T" if k == 7 else None)
            tv = b.val("T")
            b.wait("vector", "T", tv)
            l40b = c.lnb[0:128, 6144:6144 + 384]
            b.op("vector", lambda e, l40b=l40b: e.tensor_copy(out=l40b, in_=c.psB[:, 0:384]),
                 sem="V")
            pbase = (hh % 2) * 64
            chq = hh // 2
            if hh % 2 == 0:
                if tprev3 is not None:
                    b.wait("sync", "T", tprev3)
                b.wait("sync", "V")
                b.dma("sync", wb2[0][:],
                      bass.AP(Wqt.tensor, chq * 128 * D, [[D, 128], [1, D]]), sem=wsem)
            b.wait("tensor", wsem, b.val(wsem))
            b.wait("tensor", "V")
            for k in range(8):
                b.op("tensor", lambda e, k=k, pbase=pbase, l40b=l40b: e.matmul(
                    out=c.psA[pbase:pbase + 64, 0:48],
                    lhsT=wb2[0][:, k * 128 + pbase:k * 128 + pbase + 64],
                    rhs=l40b[:, k * 48:(k + 1) * 48],
                    start=(k == 0), stop=(k == 7),
                    tile_position=(0, pbase) if pbase else None),
                    sem="T" if k == 7 else None)
            tprev3 = b.val("T")
            b.wait("vector", "T", tprev3)
            b.op("vector", lambda e, pbase=pbase, chq=chq: e.tensor_scalar(
                out=qred[pbase:pbase + 64, :], in0=c.psA[pbase:pbase + 64, 0:48],
                scalar1=bqt[pbase:pbase + 64, chq:chq + 1], scalar2=None, op0=ALU.add),
                sem="V")
            vv = b.val("V")
            b.wait("tensor", "V", vv)
            for j in range(4):
                b.op("tensor", lambda e, hh=hh, j=j, pbase=pbase, chq=chq: e.matmul(
                    out=c.psA[0:40, j * 512:(j + 1) * 512],
                    lhsT=qred[pbase:pbase + 64, 0:40],
                    rhs=c.scrb[pbase:pbase + 64, chq * T + j * 512:chq * T + (j + 1) * 512],
                    start=True, stop=True, tile_position=(pbase, 0)),
                    sem="T" if j == 3 else None)
            tv = b.val("T")
            b.wait("vector", "T", tv)
            b.op("vector", lambda e: e.reduce_max(out=mxneg[0:40, :], in_=c.psA[0:40, 0:T],
                                                  axis=AX.X))
            b.op("vector", lambda e: e.tensor_scalar(out=mxneg[0:40, :], in0=mxneg[0:40, :],
                                                     scalar1=-SCALE, scalar2=None,
                                                     op0=ALU.mult), sem="V")
            vv = b.val("V")
            b.wait("scalar", "V", vv)
            b.op("scalar", lambda e, hh=hh: e.activation(
                out=p40[0:40, 0:T], in_=c.psA[0:40, 0:T], func=AF.Exp, bias=mxneg[0:40, :],
                scale=SCALE, accum_out=sums_all[0:40, hh:hh + 1]), sem="S")
            sv = b.val("S")
            b.wait("tensor", "S", sv)
            b.wait("tensor", "V")
            for j in range(16):
                b.op("tensor", lambda e, j=j: e.transpose(
                    out=c.psB[:, (j % 8) * 48:(j % 8) * 48 + 40],
                    in_=p40[0:40, j * 128:(j + 1) * 128], identity=c.identf[0:40, 0:40]),
                    sem="T" if (j % 8 == 7) else None)
                if j % 8 == 7:
                    tv = b.val("T")
                    b.wait("vector", "T", tv)
                    for j2 in range(8):
                        jj = (j // 8) * 8 + j2
                        b.op("vector", lambda e, hh=hh, jj=jj, j2=j2: e.tensor_copy(
                            out=pt_all[:, hh * 640 + jj * 40:hh * 640 + (jj + 1) * 40],
                            in_=c.psB[:, j2 * 48:j2 * 48 + 40]),
                            sem="V" if j2 == 7 else None)
                    vv = b.val("V")
                    b.wait("tensor", "V", vv)

        vch = [c.scrb[:, 11776 + i * D:11776 + (i + 1) * D] for i in range(2)]
        dfm_g = scr32[0:128, 5120:5888].bitcast(BF16)
        rs40 = b.sb(f"{tag}_rs40", [48, 16], F32, st)
        wob2 = wb2
        b.op("vector", lambda e: e.reciprocal(out=rs40[0:40, :], in_=sums_all[0:40, :]),
             sem="V")
        vv = b.val("V")
        b.wait("tensor", "V", vv)
        for half in range(2):
            b.op("tensor", lambda e, half=half: e.matmul(
                out=c.psB[0:48, half * 512:(half + 1) * 512], lhsT=c.onesf_row[:, 0:48],
                rhs=mvrow[0:1, half * 512:(half + 1) * 512], start=True, stop=True),
                sem="T" if half == 1 else None)
        tv = b.val("T")
        b.wait("vector", "T", tv)
        b.op("vector", lambda e: e.tensor_copy(out=mvb, in_=c.psB[0:48, 0:1024]), sem="V")
        b.wait("sync", "T")
        dlt = scr32[0:48, 0:1024]
        for pss in range(2):
            vprev = None
            for rch in range(16):
                if vprev is not None:
                    b.wait("sync", "T", vprev)
                lv = b.dma("sync", vch[rch % 2], v_d[rch], sem="vl")
                b.wait("tensor", "vl", lv)
                if rch == 0:
                    b.wait("tensor", "V")
                for hb in range(8):
                    hh = pss * 8 + hb
                    ps = (c.psA, c.psB)[hb // 4][0:40, (hb % 4) * 512:(hb % 4) * 512 + 64]
                    b.op("tensor", lambda e, hh=hh, rch=rch, ps=ps: e.matmul(
                        out=ps,
                        lhsT=pt_all[:, hh * 640 + rch * 40:hh * 640 + (rch + 1) * 40],
                        rhs=vch[rch % 2][:, hh * 64:(hh + 1) * 64],
                        start=(rch == 0), stop=(rch == 15)),
                        sem="T" if hb == 7 else None)
                vprev = b.val("T")
            b.wait("vector", "T", vprev)
            for hb in range(8):
                hh = pss * 8 + hb
                ps = (c.psA, c.psB)[hb // 4][0:40, (hb % 4) * 512:(hb % 4) * 512 + 64]
                b.op("vector", lambda e, hh=hh, ps=ps: e.tensor_scalar(
                    out=dlt[0:40, hh * 64:(hh + 1) * 64], in0=ps,
                    scalar1=rs40[0:40, hh:hh + 1], scalar2=None, op0=ALU.mult))
            b.op("vector", lambda e, pss=pss: e.tensor_tensor(
                out=dlt[0:40, pss * 512:(pss + 1) * 512],
                in0=dlt[0:40, pss * 512:(pss + 1) * 512],
                in1=mvb[0:40, pss * 512:(pss + 1) * 512], op=ALU.subtract), sem="V")
            vv = b.val("V")
            b.wait("tensor", "V", vv)
        for kb in range(2):
            for k2 in range(4):
                k = kb * 4 + k2
                b.op("tensor", lambda e, k=k, k2=k2: e.transpose(
                    out=c.psB[0:128, k2 * 512:k2 * 512 + 40],
                    in_=dlt[0:40, k * 128:(k + 1) * 128],
                    identity=c.identf[0:40, 0:40]), sem="T" if k2 == 3 else None)
            tv = b.val("T")
            b.wait("vector", "T", tv)
            for k2 in range(4):
                k = kb * 4 + k2
                b.op("vector", lambda e, k=k, k2=k2: e.tensor_copy(
                    out=dfm_g[:, k * 40:(k + 1) * 40],
                    in_=c.psB[0:128, k2 * 512:k2 * 512 + 40]),
                    sem="V" if k2 == 3 else None)
            vv = b.val("V")
            b.wait("tensor", "V", vv)
        wop = None
        b.wait("sync", "T")
        for k in range(8):
            if wop is not None and k >= 2:
                b.wait("sync", "T", wop)
            b.wait("sync", "V")
            b.dma("sync", wob2[k % 2][:],
                  bass.AP(Wor.tensor, k * 128 * D, [[D, 128], [1, D]]), sem=wsem)
            b.wait("tensor", wsem, b.val(wsem))
            if k == 0:
                b.wait("tensor", "V")
            for sub in range(2):
                pso = (c.psA, c.psB)[sub]
                for half in range(2):
                    b.op("tensor", lambda e, k=k, sub=sub, half=half, pso=pso: e.matmul(
                        out=pso[0:40, half * 512:(half + 1) * 512],
                        lhsT=dfm_g[sub * 64:(sub + 1) * 64, k * 40:(k + 1) * 40],
                        rhs=wob2[k % 2][sub * 64:(sub + 1) * 64, half * 512:(half + 1) * 512],
                        start=True, stop=True,
                        tile_position=(sub * 64, 0) if sub else None),
                        sem="T" if (sub == 1 and half == 1) else None)
            wop = b.val("T")
            b.wait("vector", "T", wop)
            for sub in range(2):
                hh = 2 * k + sub
                pso = (c.psA, c.psB)[sub]
                b.op("vector", lambda e, hh=hh, pso=pso: e.tensor_copy(
                    out=c.lnb[0:40, hh * 1024:(hh + 1) * 1024], in_=pso[0:40, 0:1024]),
                    sem="V" if sub == 1 else None)
            vv = b.val("V")
            b.wait("tensor", "V", vv)

        bob = scr32[0:1, 5888:5888 + D]
        b.wait("sync", "V")
        b.dma("sync", bob, boR)
        b.wait("vector", "dS")
        wop = None
        b.wait("sync", "T")
        for k in range(8):
            if wop is not None:
                b.wait("sync", "T", wop)
            b.dma("sync", wob2[k % 2][:],
                  bass.AP(Wor.tensor, k * 128 * D, [[D, 128], [1, D]]), sem=wsem)
            b.wait("tensor", wsem, b.val(wsem))
            for half in range(2):
                b.op("tensor", lambda e, k=k, half=half: e.matmul(
                    out=c.psA[0:1, half * 512:(half + 1) * 512], lhsT=mvcol[:, k:k + 1],
                    rhs=wob2[k % 2][:, half * 512:(half + 1) * 512],
                    start=(k == 0), stop=(k == 7)),
                    sem="T" if half == 1 else None)
            wop = b.val("T")
        tv = b.val("T")
        b.wait("vector", "T", tv)
        b.op("vector", lambda e: e.tensor_tensor(out=bob, in0=bob,
                                                 in1=c.psA[0:1, 0:D], op=ALU.add))
        b.op("vector", lambda e: e.tensor_copy(out=wmvbo_b, in_=bob), sem="V")

        e_all = c.scrb[0:48, 0:16 * 512]
        for tt in range(4):
            rhbc_z = scr32[0:48, 4608:4608 + 512]
            for hh in range(16):
                vv = b.val("V")
                b.wait("sync", "V", vv)
                b.dma("sync", rhbc_z, bass.AP(r_d.tensor, hh * T + tt * 512,
                                              [[0, 48], [1, 512]]), sem="rl")
                b.wait("vector", "rl", b.val("rl"))
                b.op("vector", lambda e, hh=hh, rhbc_z=rhbc_z: e.tensor_scalar(
                    out=e_all[:, hh * 512:(hh + 1) * 512], in0=rhbc_z,
                    scalar1=c.iota_col1[0:48, :], scalar2=None, op0=ALU.is_equal), sem="V")
            vv = b.val("V")
            b.wait("tensor", "V", vv)
            for dch in range(8):
                ps = (c.psA, c.psB)[dch // 4][:, (dch % 4) * 512:(dch % 4 + 1) * 512]
                b.op("tensor", lambda e, dch=dch, ps=ps: e.matmul(
                    out=ps, lhsT=wmvbo_b[0:1, dch * 128:(dch + 1) * 128],
                    rhs=c.onesb_row[0:1, 0:512],
                    start=True, stop=False))
            for hh in range(16):
                for dch in range(8):
                    ps = (c.psA, c.psB)[dch // 4][:, (dch % 4) * 512:(dch % 4 + 1) * 512]
                    b.op("tensor", lambda e, hh=hh, dch=dch, ps=ps: e.matmul(
                        out=ps, lhsT=c.lnb[0:40, hh * 1024 + dch * 128:hh * 1024 + (dch + 1) * 128],
                        rhs=e_all[0:40, hh * 512:(hh + 1) * 512],
                        start=False, stop=(hh == 15)),
                        sem="T" if (hh == 15 and dch == 7) else None)
            tv = b.val("T")
            b.wait("vector", "T", tv)
            for dch in range(8):
                ps = (c.psA, c.psB)[dch // 4][:, (dch % 4) * 512:(dch % 4 + 1) * 512]
                xs = c.x[:, dch * T + tt * 512:dch * T + (tt + 1) * 512]
                b.op("vector", lambda e, ps=ps, xs=xs: e.tensor_tensor(
                    out=xs, in0=xs, in1=ps, op=ALU.add), sem="V")
            vv = b.val("V")
            b.wait("tensor", "V", vv)




def stage_l1attn(inputs):
    d = {}
    for n in ("Wq", "Wk"):
        d[f"L1_attn_{n}t"] = pack_pm(np.asarray(inputs[f"L1_attn_{n}"], dtype=np.float32))
    d["L1_attn_Wv"] = np.ascontiguousarray(np.asarray(inputs["L1_attn_Wv"], dtype=np.float32))
    d["L1_attn_Wo"] = np.ascontiguousarray(np.asarray(inputs["L1_attn_Wo"], dtype=np.float32))
    for n in ("bq", "bk"):
        d[f"L1_attn_{n}c"] = bias_cols(np.asarray(inputs[f"L1_attn_{n}"], dtype=np.float32))
    d["L1_attn_bvR"] = np.asarray(inputs["L1_attn_bv"], dtype=np.float32).reshape(1, D)
    d["L1_attn_boR"] = np.asarray(inputs["L1_attn_bo"], dtype=np.float32).reshape(1, D)
    si = np.asarray(inputs["L1_attn_sample_idx"], dtype=np.int32)
    d["L1_attn_sidx"] = np.ascontiguousarray(
        np.pad(si, (0, 3), mode="edge").reshape(48, 1))
    return d



SCALE0 = 0.125
PVOFF = int(os.environ.get('PVOFF', '64'))
PVSTEP = int(os.environ.get('PVSTEP', '512'))


def emit_l0attn(b: Bld, c, tag):
    Wqt = precast_w(b, c, "l0Wq", b.din("L0_attn_Wqpm", [8, 128, D], F32), D * D)
    Wkt = precast_w(b, c, "l0Wk", b.din("L0_attn_Wkpm", [8, 128, D], F32), D * D)
    WvT = precast_w(b, c, "l0Wv", b.din("L0_attn_WvT", [D, D], F32), D * D)
    WoT = precast_w(b, c, "l0Wo", b.din("L0_attn_WoT", [D, D], F32), D * D)
    bqc = b.din("L0_attn_bqc", [128, 8], F32)
    bkc = b.din("L0_attn_bkc", [128, 8], F32)
    bvR = b.din("L0_attn_bvR", [1, D], F32)
    boc = b.din("L0_attn_boc", [128, 8], F32)
    distT = b.din("L0_attn_distT", [128, 73], BF16)
    qT_d = b.dtmp(f"scr0_qT_{tag}", [D, T], BF16)
    kT_d = b.dtmp(f"scr0_kT_{tag}", [D, T], BF16)
    qd_d = b.dtmp(f"scr0_qd_{tag}", [16 * T * 512], BF16)
    ao_d = b.dtmp(f"scr0_ao_{tag}", [16, 128, D], BF16)

    ln_compute(b, c, apply_bf16=True)
    scr32 = c.scrb[:].bitcast(F32)
    lnb32 = c.lnb[:].bitcast(F32)

    with contextlib.ExitStack() as st:
        fence_loads(b)
        wsem = f"{tag}_wl"
        wb2 = [b.sb(f"{tag}_wb{i}", [128, 1024], BF16, st) for i in range(2)]
        bqt = b.sb(f"{tag}_bqt", [128, 8], F32, st)
        bkt = b.sb(f"{tag}_bkt", [128, 8], F32, st)
        bot = b.sb(f"{tag}_bot", [128, 8], F32, st)
        distb = b.sb(f"{tag}_dist", [128, 73], BF16, st)
        qd02 = b.sb(f"{tag}_qd02", [128, 512], F32, st)
        mnegall = b.sb(f"{tag}_mnegall", [128, 256], F32, st)
        mall0 = b.sb(f"{tag}_mall0", [1, 256], F32, st)
        qnl = b.sb(f"{tag}_qnl", [1, 256], F32, st)
        knl = b.sb(f"{tag}_knl", [1, 16], F32, st)
        hmF = b.sb(f"{tag}_hmF", [128, 2], F32, st)
        qde = [b.sb(f"{tag}_qde{i}", [128, 512], BF16, st) for i in range(2)]
        bvb = b.sb(f"{tag}_bvb", [1, D], BF16, st)
        acc = b.sb(f"{tag}_acc", [128, 65], F32, st)
        tmp65 = b.sb(f"{tag}_tmp65", [128, 65], F32, st)
        rcol = b.sb(f"{tag}_rcol", [128, 1], F32, st)
        e0c = b.sb(f"{tag}_e0c", [128, 1], F32, st)
        e72c = b.sb(f"{tag}_e72c", [128, 1], F32, st)
        aob = b.sb(f"{tag}_aob", [128, 64], BF16, st)
        b.dma("sync", bqt[:], bqc)
        b.dma("sync", bkt[:], bkc)
        b.dma("sync", bot[:], boc)
        b.dma("sync", distb[:], distT)
        b.dma("gpsimd", bvb[:], bvR)
        b.wait("vector", "dS")
        b.wait("vector", "dG")
        b.op("vector", lambda e: e.memset(hmF[0:64, 0:1], 1.0))
        b.op("vector", lambda e: e.memset(hmF[64:128, 0:1], 0.0))
        b.op("vector", lambda e: e.memset(hmF[0:64, 1:2], 0.0))
        b.op("vector", lambda e: e.memset(hmF[64:128, 1:2], 1.0))
        b.op("vector", lambda e: e.memset(c.tf32[:, 0:256], 0.0), sem="V")

        def proj_dimmajor(Wt, bcol, is_q):
            tprev = None
            tmm = {}
            wtg = {}

            def pissue(cc):
                if cc >= 2 and tmm.get(cc - 2) is not None:
                    b.wait("sync", "T", tmm[cc - 2])
                else:
                    b.wait("sync", "T")
                b.dma("sync", wb2[cc % 2][:],
                      bass.AP(Wt.tensor, cc * 128 * D, [[D, 128], [1, D]]), sem=wsem)
                wtg[cc] = b.val(wsem)

            for cc in range(8):
                if cc not in wtg:
                    pissue(cc)
                if cc + 1 < 8:
                    pissue(cc + 1)
                b.wait("tensor", wsem, wtg[cc])
                b.wait("tensor", "V")
                for tt in range(4):
                    for k in range(8):
                        b.op("tensor", lambda e, k=k, tt=tt, wb=wb2[cc % 2]: e.matmul(
                            out=c.psA[:, tt * 512:(tt + 1) * 512],
                            lhsT=wb[:, k * 128:(k + 1) * 128],
                            rhs=c.lnb[:, k * T + tt * 512:k * T + (tt + 1) * 512],
                            start=(k == 0), stop=(k == 7)),
                            sem="T" if k == 7 else None)
                tprev = b.val("T")
                tmm[cc] = tprev
                b.wait("vector", "T", tprev)
                for tt in range(4):
                    b.op("vector", lambda e, cc=cc, tt=tt, bcol=bcol: e.tensor_scalar(
                        out=c.scrb[:, cc * T + tt * 512:cc * T + (tt + 1) * 512],
                        in0=c.psA[:, tt * 512:(tt + 1) * 512], scalar1=bcol[:, cc:cc + 1],
                        scalar2=None, op0=ALU.add), sem="V" if tt == 3 else None)
                b.op("vector", lambda e, cc=cc: e.tensor_tensor(
                    out=c.tf32[:], in0=c.scrb[:, cc * T:(cc + 1) * T],
                    in1=c.scrb[:, cc * T:(cc + 1) * T], op=ALU.mult), sem="V")
                vv = b.val("V")
                b.wait("tensor", "V", vv)
                for half in range(2):
                    hh = cc * 2 + half
                    for tt in range(4):
                        b.op("tensor", lambda e, half=half, tt=tt: e.matmul(
                            out=c.psB[0:1, tt * 512:(tt + 1) * 512],
                            lhsT=hmF[:, half:half + 1],
                            rhs=c.tf32[:, tt * 512:(tt + 1) * 512],
                            start=True, stop=True), sem="T" if tt == 3 else None)
                    tv = b.val("T")
                    b.wait("vector", "T", tv)
                    if is_q:
                        b.op("vector", lambda e: e.reduce_max(
                            out=c.tf32[0:1, 256:272],
                            in_=c.psB[0:1, 0:T].rearrange("p (a l) -> p a l", l=128),
                            axis=AX.X), sem="V")
                        vv = b.val("V")
                        b.wait("scalar", "V", vv)
                        b.op("scalar", lambda e, hh=hh: e.activation(
                            out=qnl[0:1, hh * 16:(hh + 1) * 16], in_=c.tf32[0:1, 256:272],
                            func=AF.Ln, bias=c.eps[0:1, :]), sem="S")
                    else:
                        b.op("vector", lambda e: e.reduce_max(
                            out=c.tf32[0:1, 256:257], in_=c.psB[0:1, 0:T], axis=AX.X),
                            sem="V")
                        vv = b.val("V")
                        b.wait("scalar", "V", vv)
                        b.op("scalar", lambda e, hh=hh: e.activation(
                            out=knl[0:1, hh:hh + 1], in_=c.tf32[0:1, 256:257],
                            func=AF.Ln, bias=c.eps[0:1, :]), sem="S")
                    sv = b.val("S")
                    b.wait("tensor", "S", sv)
                    b.wait("vector", "S", sv)
            return tprev

        proj_dimmajor(Wqt, bqt, True)

        b.op("vector", lambda e: e.memset(c.tf32[:, 0:256], 0.0), sem="V")
        vv = b.val("V")
        b.wait("tensor", "V", vv)
        prevQ = {}
        for h in range(16):
            cc, pb = h // 2, (h % 2) * 64
            for lt in range(16):
                idx2 = (h * 16 + lt) * 2
                ib = lt % 2
                pcol = ib * 512
                if prevQ.get(ib) is not None:
                    b.wait("tensor", "S", prevQ[ib])
                    b.wait("tensor", "V")
                b.op("tensor", lambda e, cc=cc, pb=pb, lt=lt, pcol=pcol: e.matmul(
                    out=c.psA[:, pcol:pcol + 73],
                    lhsT=c.scrb[pb:pb + 64, cc * T + lt * 128:cc * T + (lt + 1) * 128],
                    rhs=distb[pb:pb + 64, :], start=True, stop=True,
                    tile_position=(pb, 0) if pb else None), sem="T")
                tv = b.val("T")
                b.wait("vector", "T", tv)
                b.op("vector", lambda e, idx2=idx2, pcol=pcol: e.tensor_copy(
                    out=qd02[:, idx2:idx2 + 1], in_=c.psA[:, pcol:pcol + 1]))
                b.op("vector", lambda e, idx2=idx2, pcol=pcol: e.tensor_copy(
                    out=qd02[:, idx2 + 1:idx2 + 2], in_=c.psA[:, pcol + 72:pcol + 73]),
                    sem="V")
                vvq = b.val("V")
                b.wait("scalar", "V", vvq)
                b.wait("gpsimd", "V", vvq)
                b.wait("scalar", f"qd{ib}")
                b.wait("gpsimd", f"qd{ib}")
                b.op("gpsimd", lambda e, ib=ib, idx2=idx2: e.tensor_scalar(
                    out=qde[ib][:, 0:191], in0=c.tf32[:, 0:191],
                    scalar1=qd02[:, idx2:idx2 + 1], scalar2=None, op0=ALU.add))
                b.op("gpsimd", lambda e, ib=ib, idx2=idx2: e.tensor_scalar(
                    out=qde[ib][:, 264:512], in0=c.tf32[:, 0:248],
                    scalar1=qd02[:, idx2 + 1:idx2 + 2], scalar2=None, op0=ALU.add),
                    sem="G")
                gvq = b.val("G")
                b.op("scalar", lambda e, ib=ib, pcol=pcol: e.activation(
                    out=qde[ib][:, 191:264], in_=c.psA[:, pcol:pcol + 73], func=AF.Copy),
                    sem="S")
                prevQ[ib] = b.val("S")
                b.wait("sync", "S", prevQ[ib])
                b.wait("sync", "G", gvq)
                b.dma("sync", bass.AP(qd_d.tensor, h * T * 512 + lt * 128 * 512,
                                      [[512, 128], [1, 512]]), qde[ib][:], sem=f"qd{ib}")
        b.wait("sync", "V")
        for cc in range(8):
            b.dma("sync", bass.AP(qT_d.tensor, cc * 128 * T, [[T, 128], [1, T]]),
                  c.scrb[:, cc * T:(cc + 1) * T], sem="qs")

        b.wait("vector", "qs", b.val("qs"))
        proj_dimmajor(Wkt, bkt, False)
        b.wait("sync", "V")
        for cc in range(8):
            b.dma("sync", bass.AP(kT_d.tensor, cc * 128 * T, [[T, 128], [1, T]]),
                  c.scrb[:, cc * T:(cc + 1) * T], sem="ks")

        b.op("vector", lambda e: e.tensor_tensor(
            out=mall0[:].rearrange("p (h l) -> p h l", l=16),
            in0=qnl[:].rearrange("p (h l) -> p h l", l=16),
            in1=knl[:].unsqueeze(2).broadcast_to([1, 16, 16]), op=ALU.add), sem="V")
        vv = b.val("V")
        b.wait("scalar", "V", vv)
        b.op("scalar", lambda e: e.activation(out=mall0[:], in_=mall0[:], func=AF.Exp,
                                              scale=0.5), sem="S")
        sv = b.val("S")
        b.wait("vector", "S", sv)
        b.op("vector", lambda e: e.tensor_scalar(out=mall0[:], in0=mall0[:],
                                                 scalar1=-SCALE0, scalar2=None,
                                                 op0=ALU.mult), sem="V")
        vv = b.val("V")
        b.wait("tensor", "V", vv)
        b.op("tensor", lambda e: e.matmul(out=c.psB[0:128, 0:256], lhsT=c.onesf_row[:],
                                          rhs=mall0[:], start=True, stop=True), sem="T")
        tv = b.val("T")
        b.wait("vector", "T", tv)
        b.op("vector", lambda e: e.tensor_copy(out=mnegall[:], in_=c.psB[0:128, 0:256]),
             sem="V")

        b.wait("vector", "ks", b.val("ks"))
        sprev = {}
        for tg in range(4):
            b.wait("tensor", "V")
            for tsub in range(4):
                for half in range(2):
                    ps = (c.psA, c.psB)[tsub // 2][:, (tsub % 2) * 1024 + half * 512:
                                                   (tsub % 2) * 1024 + (half + 1) * 512]
                    b.op("tensor", lambda e, ps=ps, half=half: e.matmul(
                        out=ps, lhsT=c.onesb_row[0:1, 0:128],
                        rhs=bvb[0:1, half * 512:(half + 1) * 512],
                        start=True, stop=False))
            wop = None
            b.wait("sync", "T")
            for k in range(8):
                if wop is not None and k >= 2:
                    b.wait("sync", "T", wop)
                b.dma("sync", wb2[k % 2][:],
                      bass.AP(WvT.tensor, k * 128 * D, [[D, 128], [1, D]]), sem=wsem)
                b.wait("tensor", wsem, b.val(wsem))
                for tsub in range(4):
                    ttile = tg * 4 + tsub
                    for half in range(2):
                        ps = (c.psA, c.psB)[tsub // 2][:, (tsub % 2) * 1024 + half * 512:
                                                       (tsub % 2) * 1024 + (half + 1) * 512]
                        b.op("tensor", lambda e, k=k, half=half, ttile=ttile, ps=ps: e.matmul(
                            out=ps,
                            lhsT=c.lnb[:, k * T + ttile * 128:k * T + (ttile + 1) * 128],
                            rhs=wb2[k % 2][:, half * 512:(half + 1) * 512],
                            start=False, stop=(k == 7)),
                            sem="T" if (tsub == 3 and half == 1) else None)
                wop = b.val("T")
            tv = b.val("T")
            b.wait("vector", "T", tv)
            for tsub in range(4):
                ttile = tg * 4 + tsub
                ps0 = (c.psA, c.psB)[tsub // 2]
                b.op("vector", lambda e, tsub=tsub, ttile=ttile, ps0=ps0: e.tensor_copy(
                    out=c.scrb[:, ttile * 1024:(ttile + 1) * 1024],
                    in_=ps0[:, (tsub % 2) * 1024:(tsub % 2) * 1024 + 1024]),
                    sem="V" if tsub == 3 else None)
            vv = b.val("V")
            b.wait("tensor", "V", vv)

        PtB = [c.lnb[:, 0:2048], c.lnb[:, 12288:14336]]
        PbandB = [c.lnb[:, 2048:2432], c.lnb[:, 14336:14720]]
        khb = [c.lnb[0:64, 2560 + i * 2048:4608 + i * 2048] for i in range(2)]
        qhb = [c.lnb[0:64, 6656 + i * 2048:8704 + i * 2048] for i in range(2)]
        vaug = c.lnb[:, 10752:11792]
        B_b = c.lnb[:, 11792:12176]
        aoL = c.lnb[:, 12288:13312]
        aoT_b = c.lnb[:, 13312:14336]
        wo_row = c.lnb[:, 14336:15360]
        b.wait("sync", "qs", b.val("qs"))
        b.wait("sync", "ks", b.val("ks"))
        b.wait("sync", "qd0", b.val("qd0"))
        b.wait("sync", "qd1", b.val("qd1"))
        b.wait("sync", "T")
        tend = {}
        for h in range(16):
            if h >= 2:
                b.wait("sync", "T", tend[h - 2])
            b.wait("sync", "V")
            lk = b.dma("sync", khb[h % 2], bass.AP(kT_d.tensor, h * 64 * T,
                                                   [[T, 64], [1, T]]), sem="kl")
            lq = b.dma("sync", qhb[h % 2], bass.AP(qT_d.tensor, h * 64 * T,
                                                   [[T, 64], [1, T]]), sem="kl")
            b.wait("vector", "T")
            b.op("vector", lambda e: e.memset(vaug, 1.0))
            b.op("vector", lambda e, h=h: e.tensor_copy(
                out=vaug[:].rearrange("p (a b) -> p a b", b=65)[:, :, 0:64],
                in_=c.scrb[:].rearrange("p (a b) -> p a b", b=1024)[:, :, h * 64:(h + 1) * 64]),
                sem="V")
            vv_vaug = b.val("V")
            b.wait("tensor", "kl", lq)
            b.wait("tensor", "V", vv_vaug)
            for lt in range(16):
                idx = h * 16 + lt
                band = [rt for rt in (lt - 1, lt, lt + 1) if 0 <= rt <= 15]
                left = list(range(0, lt - 1))
                right = list(range(lt + 2, 16))
                band0, W = band[0], len(band) * 128
                Pt = PtB[0]
                Pband = PbandB[0]
                b.wait("sync", "V")
                b.dma("sync", B_b[:, 0:W],
                      bass.AP(qd_d.tensor,
                              h * T * 512 + lt * 128 * 512 + band0 * 128 - lt * 128 + 255,
                              [[511, 128], [1, W]]), sem="bb")
                b.wait("tensor", "S")
                b.wait("tensor", "V")
                for rt in left + right:
                    b.op("tensor", lambda e, rt=rt, lt=lt, h=h: e.matmul(
                        out=c.psA[:, rt * 128:(rt + 1) * 128],
                        lhsT=khb[h % 2][0:64, rt * 128:(rt + 1) * 128],
                        rhs=qhb[h % 2][0:64, lt * 128:(lt + 1) * 128],
                        start=True, stop=True))
                b.op("tensor", lambda e, lt=lt, h=h, band0=band0, W=W: e.matmul(
                    out=c.psB[:, 1536:1536 + W],
                    lhsT=qhb[h % 2][0:64, lt * 128:(lt + 1) * 128],
                    rhs=khb[h % 2][0:64, band0 * 128:band0 * 128 + W],
                    start=True, stop=True), sem="T")
                tv = b.val("T")
                b.wait("scalar", "T", tv)
                segs = [s for s in (left, right) if s]
                for i, seg in enumerate(segs):
                    n = len(seg)
                    b.op("scalar", lambda e, seg=seg, n=n, idx=idx: e.activation(
                        out=Pt[:, seg[0] * 128:(seg[0] + n) * 128],
                        in_=c.psA[:, seg[0] * 128:(seg[0] + n) * 128],
                        func=AF.Exp, bias=mnegall[:, idx:idx + 1], scale=SCALE0),
                        sem="S" if i == len(segs) - 1 else None)
                b.wait("vector", "T", tv)
                b.wait("vector", "bb", b.val("bb"))
                b.op("vector", lambda e, W=W: e.tensor_copy(
                    out=c.tf32[:, 256:256 + W], in_=B_b[:, 0:W]))
                b.op("vector", lambda e, W=W: e.tensor_tensor(
                    out=c.tf32[:, 640:640 + W], in0=c.psB[:, 1536:1536 + W],
                    in1=c.tf32[:, 256:256 + W], op=ALU.add), sem="V")
                vv = b.val("V")
                b.wait("scalar", "V", vv)
                b.op("scalar", lambda e, W=W, idx=idx: e.activation(
                    out=Pband[:, 0:W], in_=c.tf32[:, 640:640 + W], func=AF.Exp,
                    bias=mnegall[:, idx:idx + 1], scale=SCALE0), sem="S")
                sv = b.val("S")
                b.wait("tensor", "S", sv)
                for i, rt in enumerate(band):
                    b.op("tensor", lambda e, i=i: e.transpose(
                        out=c.psB[:, i * 512:i * 512 + 64].bitcast(BF16),
                        in_=Pband[:, i * 128:(i + 1) * 128], identity=c.ident[:]),
                        sem="T" if i == len(band) - 1 else None)
                tv = b.val("T")
                b.wait("vector", "T", tv)
                for i, rt in enumerate(band):
                    b.op("vector", lambda e, i=i, rt=rt: e.tensor_copy(
                        out=Pt[:, rt * 128:(rt + 1) * 128],
                        in_=c.psB[:, i * 512:i * 512 + 64].bitcast(BF16)),
                        sem="V" if i == len(band) - 1 else None)
                vv = b.val("V")
                b.wait("tensor", "V", vv)
                regions = [r for r in (left, band, right) if r]
                ridx = {id(left): 0, id(band): 1, id(right): 2}
                nmm = sum(len(r) for r in regions)
                done = 0
                for reg in regions:
                    ri = ridx[id(reg)]
                    for j, rt in enumerate(reg):
                        done += 1
                        b.op("tensor", lambda e, rt=rt, ri=ri, j=j, reg=reg: e.matmul(
                            out=c.psB[0:128, 1536 + ri * 128:1536 + ri * 128 + 65],
                            lhsT=Pt[:, rt * 128:(rt + 1) * 128],
                            rhs=vaug[:, rt * 65:(rt + 1) * 65],
                            start=(j == 0), stop=(j == len(reg) - 1)),
                            sem="T" if done == nmm else None)
                tv = b.val("T")
                b.wait("vector", "T", tv)
                b.wait("scalar", "V")
                idx2 = idx * 2
                acts = []
                if left:
                    acts.append((e0c, idx2))
                if right:
                    acts.append((e72c, idx2 + 1))
                for i, (tgt, col) in enumerate(acts):
                    b.op("scalar", lambda e, tgt=tgt, col=col: e.activation(
                        out=tgt[:], in_=qd02[:, col:col + 1], func=AF.Exp,
                        scale=SCALE0), sem="S" if i == len(acts) - 1 else None)
                sv = b.val("S")
                b.wait("vector", "S", sv)
                b.op("vector", lambda e: e.tensor_copy(
                    out=acc[:], in_=c.psB[:, 1664:1729]))
                if left:
                    b.op("vector", lambda e: e.scalar_tensor_tensor(
                        out=acc[:], in0=c.psB[:, 1536:1601], scalar=e0c[:], in1=acc[:],
                        op0=ALU.mult, op1=ALU.add))
                if right:
                    b.op("vector", lambda e: e.scalar_tensor_tensor(
                        out=acc[:], in0=c.psB[:, 1792:1857], scalar=e72c[:], in1=acc[:],
                        op0=ALU.mult, op1=ALU.add))
                b.op("vector", lambda e: e.reciprocal(out=rcol[:], in_=acc[:, 64:65]))
                b.wait("vector", "ao")
                b.op("vector", lambda e: e.tensor_scalar(
                    out=aob[:], in0=acc[:, 0:64], scalar1=rcol[:], scalar2=None,
                    op0=ALU.mult), sem="V")
                vv = b.val("V")
                b.wait("sync", "V", vv)
                b.dma("sync", bass.AP(ao_d.tensor, lt * 128 * D + h * 64,
                                      [[D, 128], [1, 64]]), aob[:], sem="ao")
            tend[h] = b.val("T")

        b.wait("sync", "ao", b.val("ao"))
        wop = None
        b.wait("sync", "T")
        for lt in range(16):
            b.wait("sync", "V")
            b.wait("sync", "T")
            la = b.dma("sync", aoL, ao_d[lt], sem="aol")
            b.wait("tensor", "aol", la)
            b.wait("tensor", "V")
            for k in range(8):
                b.op("tensor", lambda e, k=k: e.transpose(
                    out=c.psB[:, (k % 4) * 512:(k % 4) * 512 + 64].bitcast(BF16),
                    in_=aoL[:, k * 128:(k + 1) * 128], identity=c.ident[:]),
                    sem="T" if k % 4 == 3 else None)
                if k % 4 == 3:
                    tv = b.val("T")
                    b.wait("vector", "T", tv)
                    for k2 in range(4):
                        kk = (k // 4) * 4 + k2
                        b.op("vector", lambda e, kk=kk, k2=k2: e.tensor_copy(
                            out=aoT_b[:, kk * 128:(kk + 1) * 128],
                            in_=c.psB[:, k2 * 512:k2 * 512 + 64].bitcast(BF16)),
                            sem="V" if k2 == 3 else None)
                    vv = b.val("V")
                    b.wait("tensor", "V", vv)
            for k in range(8):
                if wop is not None and k >= 2:
                    b.wait("sync", "T", wop)
                b.wait("sync", "V")
                b.dma("sync", wb2[k % 2][:],
                      bass.AP(WoT.tensor, k * 128 * D, [[D, 128], [1, D]]), sem=wsem)
                b.wait("tensor", wsem, b.val(wsem))
                for half in range(2):
                    b.op("tensor", lambda e, k=k, half=half: e.matmul(
                        out=c.psA[:, half * 512:(half + 1) * 512],
                        lhsT=aoT_b[:, k * 128:(k + 1) * 128],
                        rhs=wb2[k % 2][:, half * 512:(half + 1) * 512],
                        start=(k == 0), stop=(k == 7)),
                        sem="T" if half == 1 else None)
                wop = b.val("T")
            tv = b.val("T")
            b.wait("vector", "T", tv)
            b.op("vector", lambda e: e.tensor_copy(out=wo_row[:], in_=c.psA[:, 0:1024]),
                 sem="V")
            vv = b.val("V")
            b.wait("tensor", "V", vv)
            for k in range(8):
                b.op("tensor", lambda e, k=k: e.transpose(
                    out=c.psB[:, (k % 4) * 512:(k % 4) * 512 + 64].bitcast(BF16),
                    in_=wo_row[:, k * 128:(k + 1) * 128], identity=c.ident[:]),
                    sem="T" if k % 4 == 3 else None)
                if k % 4 == 3:
                    tv = b.val("T")
                    b.wait("vector", "T", tv)
                    for k2 in range(4):
                        dch = (k // 4) * 4 + k2
                        xs = c.x[:, dch * T + lt * 128:dch * T + (lt + 1) * 128]
                        b.op("vector", lambda e, dch=dch, k2=k2, xs=xs: e.scalar_tensor_tensor(
                            out=xs, in0=c.psB[:, k2 * 512:k2 * 512 + 64].bitcast(BF16),
                            scalar=bot[:, dch:dch + 1], in1=xs,
                            op0=ALU.add, op1=ALU.add),
                            sem="V" if k2 == 3 else None)
                    vv = b.val("V")
                    b.wait("tensor", "V", vv)


def stage_l0attn(inputs):
    import ml_dtypes
    d = {}
    for n in ("Wq", "Wk"):
        w = np.asarray(inputs[f"L0_attn_{n}"], dtype=np.float32)
        d[f"L0_attn_{n}pm"] = pack_pm(np.ascontiguousarray(w.T))
    d["L0_attn_WvT"] = np.ascontiguousarray(
        np.asarray(inputs["L0_attn_Wv"], dtype=np.float32).T)
    d["L0_attn_WoT"] = np.ascontiguousarray(
        np.asarray(inputs["L0_attn_Wo"], dtype=np.float32).T)
    for n in ("bq", "bk", "bo"):
        d[f"L0_attn_{n}c"] = bias_cols(np.asarray(inputs[f"L0_attn_{n}"], dtype=np.float32))
    d["L0_attn_bvR"] = np.asarray(inputs["L0_attn_bv"], dtype=np.float32).reshape(1, D)
    de = np.asarray(inputs["L0_attn_dist_emb"], dtype=np.float32)
    assert de.shape == (73, 64), de.shape
    dT = np.zeros((128, 73), dtype=ml_dtypes.bfloat16)
    dT[0:64] = de.T.astype(ml_dtypes.bfloat16)
    dT[64:128] = de.T.astype(ml_dtypes.bfloat16)
    d["L0_attn_distT"] = dT
    return d



def pack_pm(w):
    I, O = w.shape
    return np.ascontiguousarray(
        w.reshape(I // 128, 128, O // 128, 128).transpose(2, 1, 0, 3).reshape(O // 128, 128, I))


def retile(w):
    I, O = w.shape
    return np.ascontiguousarray(
        w.reshape(I // 128, 128, O // 128, 128).transpose(2, 0, 1, 3))


def bias_cols(v):
    n = v.shape[0] // 128
    return np.ascontiguousarray(v.reshape(n, 128).T)


_CONSTS = None


def host_consts():
    global _CONSTS
    if _CONSTS is None:
        import ml_dtypes
        cs = {}
        cs["c_ident"] = np.eye(128).astype(ml_dtypes.bfloat16)
        cs["c_identf"] = np.eye(128, dtype=np.float32)
        cs["c_iota_row"] = np.broadcast_to(np.arange(T, dtype=np.float32), (128, T)).copy()
        cs["c_iota_col1"] = (np.arange(128, dtype=np.float32) + 1).reshape(128, 1)
        ss = np.zeros((24, 24), np.float32)
        for e in range(NE):
            ss[e * NR, :] = 1.0
        cs["c_sumsel24"] = ss.astype(ml_dtypes.bfloat16)
        cs["c_tri"] = np.triu(np.ones((128, 128), np.float32)).astype(ml_dtypes.bfloat16)
        _CONSTS = cs
    return _CONSTS


def stage_l1ffn(inputs):
    d = {}
    d["L1_ffn_W1t"] = pack_pm(np.asarray(inputs["L1_ffn_W1"], dtype=np.float32))
    d["L1_ffn_A1p"] = np.ascontiguousarray(
        np.asarray(inputs["L1_ffn_A1"]).transpose(1, 0, 2).reshape(D, 24))
    d["L1_ffn_B1p"] = np.asarray(inputs["L1_ffn_B1"]).reshape(24, HFF1)
    d["L1_ffn_R1x"] = np.repeat(np.asarray(inputs["L1_ffn_R1"]), NR, axis=1)
    d["L1_ffn_b1c"] = bias_cols(np.asarray(inputs["L1_ffn_b1"]))
    d["L1_ffn_W2t"] = pack_pm(np.asarray(inputs["L1_ffn_W2"], dtype=np.float32))
    d["L1_ffn_A2p"] = np.ascontiguousarray(
        np.asarray(inputs["L1_ffn_A2"]).transpose(1, 0, 2).reshape(HFF1, 24))
    d["L1_ffn_B2p"] = np.asarray(inputs["L1_ffn_B2"]).reshape(24, D)
    d["L1_ffn_R2x"] = np.repeat(np.asarray(inputs["L1_ffn_R2"]), NR, axis=1)
    d["L1_ffn_b2c"] = bias_cols(np.asarray(inputs["L1_ffn_b2"]))
    return d



_PROGRAMS = {}


def build_single(emitter, tag):
    b = Bld()
    c = setup_common(b)
    emitter(b, c, tag)
    finish_store(b, c)
    nc = b.build()
    return nc, list(b.inputs)


def run_block(key, emitter, tag, x_percore, staged):
    if key not in _PROGRAMS:
        _PROGRAMS[key] = build_single(emitter, tag)
    nc, names = _PROGRAMS[key]
    cs = host_consts()
    in_maps = []
    for ci in range(NCORES):
        m = {}
        for n in names:
            if n == "xio_in":
                m[n] = x_percore[ci]
            elif n in cs:
                m[n] = cs[n]
            else:
                m[n] = staged[n]
        in_maps.append(m)
    res = run_bass_kernel_spmd(nc, in_maps, core_ids=list(range(NCORES)))
    return [r["xio_out"] for r in res.results]


def x_to_percore(x):
    out = []
    for ci in range(NCORES):
        xc = np.ascontiguousarray(np.asarray(x)[ci]).T
        out.append(np.ascontiguousarray(xc.reshape(NCH, 128, T), dtype=np.float32))
    return out


def percore_to_x(lst):
    outs = []
    for ci in range(NCORES):
        xf = lst[ci].reshape(D, T)
        outs.append(xf.T)
    return np.stack(outs).astype(np.float32)


def block_barrier(b):
    for eng in ENGINES:
        for s in ("T", "V", "S", "G"):
            if b.cnt.get(s, 0):
                b.wait(eng, s)
        for s in sorted(b.dma_sems):
            if b.cnt.get(s, 0):
                b.wait(eng, s)


FUSED_SEQ = ((emit_l0ffn, "f0A"), (emit_l0attn, "a0"), (emit_l0ffn, "f0B"),
             (emit_l1ffn, "f1A"), (emit_l1attn, "a1"), (emit_l1ffn, "f1B"))


def build_fused():
    b = Bld()
    c = setup_common(b)
    for i, (em, tg) in enumerate(FUSED_SEQ):
        if i:
            block_barrier(b)
        em(b, c, tg)
    finish_store(b, c)
    nc = b.build()
    return nc, list(b.inputs)


def kernel(**inputs):
    staged = {}
    staged.update(stage_l0ffn(inputs))
    staged.update(stage_l0attn(inputs))
    staged.update(stage_l1ffn(inputs))
    staged.update(stage_l1attn(inputs))
    x = x_to_percore(inputs["L0_attn_hidden_states"])
    if os.environ.get("UNFUSED") == "1":
        x = run_block("l0ffn", emit_l0ffn, "f0", x, staged)
        x = run_block("l0attn", emit_l0attn, "a0", x, staged)
        x = run_block("l0ffn", emit_l0ffn, "f0", x, staged)
        x = run_block("l1ffn", emit_l1ffn, "f1", x, staged)
        x = run_block("l1attn", emit_l1attn, "a1", x, staged)
        x = run_block("l1ffn", emit_l1ffn, "f1", x, staged)
        return percore_to_x(x)
    if "fused" not in _PROGRAMS:
        _PROGRAMS["fused"] = build_fused()
    nc, names = _PROGRAMS["fused"]
    cs = host_consts()
    in_maps = []
    for ci in range(NCORES):
        m = {}
        for n in names:
            if n == "xio_in":
                m[n] = x[ci]
            elif n in cs:
                m[n] = cs[n]
            else:
                m[n] = staged[n]
        in_maps.append(m)
    res = run_bass_kernel_spmd(nc, in_maps, core_ids=list(range(NCORES)))
    return percore_to_x([r["xio_out"] for r in res.results])


# version 11 (final)
# speedup vs baseline: 1.3051x; 1.3051x over previous
import contextlib
import os
import numpy as np
import concourse.bass as bass
import concourse.mybir as mybir
from concourse.bass_utils import run_bass_kernel_spmd

F32 = mybir.dt.float32
BF16 = mybir.dt.bfloat16
I16 = mybir.dt.int16
U16 = mybir.dt.uint16
I32 = mybir.dt.int32
AF = mybir.ActivationFunctionType
ALU = mybir.AluOpType
AX = mybir.AxisListType

NCORES = 8
D = 1024
T = 2048
NCH = 8
NH = 16
HD = 64
IFF0 = 3072
HFF1 = 4096
NE = 3
NR = 8
NU = 40
NS = 45
QMAX = 32767.0
LN_EPS = 1e-5
SCALE = 0.125

ENGINES = ("tensor", "vector", "scalar", "sync", "gpsimd")
DMA_SEM = {"sync": "dS", "gpsimd": "dG", "scalar": "dC"}


class Bld:

    def __init__(self):
        self.nc = bass.Bass("TRN2", target_bir_lowering=False, debug=(os.environ.get("DBG_PROBE") == "1"))
        self.q = {e: [] for e in ENGINES}
        self.sems = {}
        self.cnt = {}
        self.inputs = []
        self.es = contextlib.ExitStack()
        self.dram = {}
        self.dirty = {}
        self.dma_sems = set()

    def din(self, name, shape, dtype):
        if name not in self.dram:
            self.dram[name] = self.nc.dram_tensor(name, shape, dtype, kind="ExternalInput").ap()
            self.inputs.append(name)
        return self.dram[name]

    def dout(self, name, shape, dtype):
        self.dram[name] = self.nc.dram_tensor(name, shape, dtype, kind="ExternalOutput").ap()
        return self.dram[name]

    def dtmp(self, name, shape, dtype):
        if name not in self.dram:
            self.dram[name] = self.nc.dram_tensor(name, shape, dtype, kind="Internal").ap()
        return self.dram[name]

    def sb(self, name, shape, dtype, stack=None):
        return (stack or self.es).enter_context(self.nc.sbuf_tensor(name, shape, dtype))

    def ps(self, name, shape, dtype=F32, stack=None):
        return (stack or self.es).enter_context(self.nc.psum_tensor(name, shape, dtype))

    def sem(self, name):
        if name not in self.sems:
            self.sems[name] = self.nc.alloc_semaphore(name)
            self.cnt[name] = 0
        return self.sems[name]

    def val(self, name):
        self.sem(name)
        return self.cnt[name]

    def on(self, eng, fn):
        self.q[eng].append(fn)

    DRAIN_ENGINES = ("vector", "scalar", "gpsimd")

    def _predrain(self, eng):
        if eng in self.DRAIN_ENGINES and self.dirty.get(eng):
            self.dirty[eng] = False
            self.on(eng, lambda e: e.drain())

    def op(self, eng, fn, sem=None, k=1):
        self._predrain(eng)
        if eng in self.DRAIN_ENGINES:
            self.dirty[eng] = True
        if sem is None:
            self.on(eng, lambda e, fn=fn: fn(e))
            return None
        h = self.sem(sem)
        self.cnt[sem] += k
        tgt = self.cnt[sem]
        if eng == "tensor":
            self.on(eng, lambda e, fn=fn: fn(e))
            self.on(eng, lambda e, h=h, k=k: e.drain().then_inc(h, k))
        else:
            self.on(eng, lambda e, fn=fn, h=h, k=k: fn(e).then_inc(h, k))
        return tgt

    def dma(self, eng, out, in_, sem=None, **kw):
        self._predrain(eng)
        if sem is None:
            sem = DMA_SEM[eng]
        self.dma_sems.add(sem)
        h = self.sem(sem)
        prev = self.cnt[sem]
        if prev > 0:
            self.on(eng, lambda e, h=h, prev=prev: e.wait_ge(h, prev))
        self.cnt[sem] += 16
        tgt = self.cnt[sem]
        self.on(eng, lambda e, out=out, in_=in_, kw=kw, h=h: e.dma_start(
            out=out, in_=in_, **kw).then_inc(h, 16))
        return tgt

    QUEUE_OWNER = {"dS": "sync", "dG": "gpsimd", "dC": "scalar"}

    def gather(self, eng, out, in_, idx_ap, sem):
        self._predrain(eng)
        self.dma_sems.add(sem)
        h = self.sem(sem)
        prev = self.cnt[sem]
        if prev > 0:
            self.on(eng, lambda e, h=h, prev=prev: e.wait_ge(h, prev))
        self.cnt[sem] += 16
        tgt = self.cnt[sem]
        self.on(eng, lambda e, out=out, in_=in_, idx_ap=idx_ap, h=h: e.indirect_dma_start(
            out=out, out_offset=None, in_=in_,
            in_offset=bass.IndirectOffsetOnAxis(idx_ap, 0)).then_inc(h, 16))
        return tgt

    def wait(self, eng, sem, v=None):
        auto = v is None
        if v is None:
            v = self.cnt.get(sem, 0)
        if v == 0:
            return
        h = self.sem(sem)
        if auto and sem in self.QUEUE_OWNER:
            own = self.QUEUE_OWNER[sem]
            if not hasattr(self, "fenced"):
                self.fenced = {}
            if self.fenced.get(sem, 0) < v:
                self.fenced[sem] = v
                if own != eng:
                    self.on(own, lambda e, h=h, v=v: e.wait_ge(h, v))
        self.on(eng, lambda e, h=h, v=v: e.wait_ge(h, v))

    def wait_all_dma(self, eng):
        for s in ("dS", "dG", "dC"):
            if self.cnt.get(s, 0) > 0:
                self.wait(eng, s)

    def build(self):
        nc = self.nc
        qs = self.q
        with nc.Block() as blk:
            @blk.tensor
            def _(e):
                for f in qs["tensor"]:
                    f(e)

            @blk.vector
            def _(e):
                for f in qs["vector"]:
                    f(e)

            @blk.scalar
            def _(e):
                for f in qs["scalar"]:
                    f(e)

            @blk.sync
            def _(e):
                for f in qs["sync"]:
                    f(e)

            @blk.gpsimd
            def _(e):
                for f in qs["gpsimd"]:
                    f(e)
        self.es.close()
        return nc


def fence_loads(b, engs=("sync", "gpsimd")):
    for eng in engs:
        for s in ("T", "V", "S", "G"):
            if b.cnt.get(s, 0):
                b.wait(eng, s)


class Ctx:
    pass


def precast_w(b: Bld, c, name, src, nel):
    key = "_pc_" + name
    if getattr(b, key, None) is not None:
        return getattr(b, key)
    dst = b.dtmp("wbf_" + name, [nel], BF16)
    scr32 = c.scrb[:].bitcast(F32)
    CH = 128 * 4096
    sem = "pc"
    assert nel % CH == 0, nel
    for i in range(nel // CH):
        b.wait("sync", "V")
        b.dma("sync", scr32[:, 0:4096],
              bass.AP(src.tensor, i * CH, [[4096, 128], [1, 4096]]), sem=sem)
        b.wait("vector", sem, b.val(sem))
        b.op("vector", lambda e: e.tensor_copy(out=c.scrb[:, 8192:12288],
                                               in_=scr32[:, 0:4096]), sem="V")
        vv = b.val("V")
        b.wait("sync", "V", vv)
        b.dma("sync", bass.AP(dst.tensor, i * CH, [[4096, 128], [1, 4096]]),
              c.scrb[:, 8192:12288], sem=sem)
    for eng in ("vector", "sync", "gpsimd"):
        b.wait(eng, sem)
    setattr(b, key, dst)
    return dst


def setup_common(b: Bld):
    c = Ctx()
    c.x = b.sb("x_fm", [128, NCH * T], F32)
    c.scrb = b.sb("scrb", [128, NCH * T], BF16)
    c.lnb = b.sb("lnb", [128, NCH * T], BF16)
    c.mean = b.sb("ln_mean", [128, T], F32)
    c.rstd = b.sb("ln_rstd", [128, T], F32)
    c.tf32 = b.sb("tf32", [128, T], F32)
    c.ident = b.sb("ident", [128, 128], BF16)
    c.identf = b.sb("identf", [128, 128], F32)
    c.onesb_col = b.sb("onesb_col", [128, 128], BF16)
    c.onesb_row = b.sb("onesb_row", [1, 512], BF16)
    c.onesf_row = b.sb("onesf_row", [1, 128], F32)
    c.onesf_col = b.sb("onesf_col", [128, 1], F32)
    c.eps = b.sb("eps_tile", [128, 1], F32)
    c.psA = b.ps("psA", [128, T], F32)
    c.psB = b.ps("psB", [128, T], F32)

    xin = b.din("xio_in", [NCH, 128, T], F32)
    b.dma("sync", c.x[:], xin.transpose([1, 0, 2]))
    b.dma("sync", c.ident[:], b.din("c_ident", [128, 128], BF16))
    b.dma("sync", c.identf[:], b.din("c_identf", [128, 128], F32))
    b.op("vector", lambda e: e.memset(c.onesb_col[:], 1.0))
    b.op("vector", lambda e: e.memset(c.onesb_row[:], 1.0))
    b.op("vector", lambda e: e.memset(c.onesf_row[:], 1.0))
    b.op("vector", lambda e: e.memset(c.onesf_col[:], 1.0))
    b.op("vector", lambda e: e.memset(c.eps[:], LN_EPS), sem="V")
    b.wait("vector", "dS")
    return c


def finish_store(b: Bld, c):
    xout = b.dout("xio_out", [NCH, 128, T], F32)
    b.wait("sync", "V")
    if b.cnt.get("G", 0):
        b.wait("sync", "G")
    if b.cnt.get("S", 0):
        b.wait("sync", "S")
    b.dma("sync", xout.transpose([1, 0, 2]), c.x[:])
    for s in sorted(b.dma_sems):
        b.wait("sync", s, b.cnt[s])



def ln_compute(b: Bld, c, apply_bf16=True):
    if True:
        psum = c.psA[:]
        for ch in range(NCH):
            b.op("vector", lambda e, ch=ch: e.tensor_copy(
                out=c.scrb[:, ch * T:(ch + 1) * T], in_=c.x[:, ch * T:(ch + 1) * T]),
                sem="V" if ch == NCH - 1 else None)
        vv = b.val("V")
        b.wait("tensor", "V", vv)
        for tt in range(4):
            for ch in range(NCH):
                rhs = c.scrb[:, ch * T + tt * 512: ch * T + (tt + 1) * 512]
                b.op("tensor", lambda e, rhs=rhs, tt=tt, st_=(ch == 0), sp_=(ch == NCH - 1): e.matmul(
                    out=psum[:, tt * 512:(tt + 1) * 512], lhsT=c.onesb_col[:], rhs=rhs,
                    start=st_, stop=sp_),
                    sem="T" if (tt == 3 and ch == NCH - 1) else None)
        tv = b.val("T")
        b.wait("vector", "T", tv)
        b.op("vector", lambda e: e.tensor_scalar(
            out=c.mean[:], in0=psum, scalar1=1.0 / D, scalar2=None, op0=ALU.mult), sem="V")
        vv = b.val("V")
        for ch in range(NCH):
            sl = slice(ch * T, (ch + 1) * T)
            b.op("vector", lambda e, sl=sl: e.tensor_tensor(
                out=c.scrb[:, sl], in0=c.scrb[:, sl], in1=c.scrb[:, sl], op=ALU.mult),
                sem="V" if ch == NCH - 1 else None)
        vv2 = b.val("V")
        b.wait("tensor", "V", vv2)
        for tt in range(4):
            for ch in range(NCH):
                rhs = c.scrb[:, ch * T + tt * 512: ch * T + (tt + 1) * 512]
                b.op("tensor", lambda e, rhs=rhs, tt=tt, st_=(ch == 0), sp_=(ch == NCH - 1): e.matmul(
                    out=psum[:, tt * 512:(tt + 1) * 512], lhsT=c.onesb_col[:], rhs=rhs,
                    start=st_, stop=sp_),
                    sem="T" if (tt == 3 and ch == NCH - 1) else None)
        tv = b.val("T")
        b.wait("vector", "T", tv)
        b.op("vector", lambda e: e.tensor_tensor(
            out=c.rstd[:], in0=c.mean[:], in1=c.mean[:], op=ALU.mult))
        b.op("vector", lambda e: e.scalar_tensor_tensor(
            out=c.rstd[:], in0=psum, scalar=1.0 / D, in1=c.rstd[:],
            op0=ALU.mult, op1=ALU.subtract), sem="V")
        vv = b.val("V")
        b.wait("scalar", "V", vv)
        b.op("scalar", lambda e: e.activation(out=c.rstd[:], in_=c.rstd[:], func=AF.Ln, bias=c.eps[:]))
        b.op("scalar", lambda e: e.activation(out=c.rstd[:], in_=c.rstd[:], func=AF.Exp, scale=-0.5),
             sem="S")
        sv = b.val("S")
        b.wait("vector", "S", sv)
        if apply_bf16:
            for ch in range(NCH):
                sl = slice(ch * T, (ch + 1) * T)
                b.op("vector", lambda e, sl=sl: e.tensor_tensor(
                    out=c.tf32[:], in0=c.x[:, sl], in1=c.mean[:], op=ALU.subtract))
                b.op("vector", lambda e, sl=sl: e.tensor_tensor(
                    out=c.lnb[:, sl], in0=c.tf32[:], in1=c.rstd[:], op=ALU.mult),
                    sem="V" if ch == NCH - 1 else None)



def emit_mole_smalls(b: Bld, c, tag, rhs_getter, nin_ch, Rx, Ap, zg_out, st, pA, pB):
    fence_loads(b)
    rt = b.sb(f"{tag}_rt", [128, nin_ch * 24], BF16, st)
    at = b.sb(f"{tag}_at", [128, nin_ch * 24], BF16, st)
    e24 = b.sb(f"{tag}_e24", [24, T], F32, st)
    e24b = b.sb(f"{tag}_e24b", [24, T], BF16, st)
    r24 = b.sb(f"{tag}_r24", [24, T], F32, st)
    mxc = b.sb(f"{tag}_mxc", [128, 16], F32, st)
    scr32 = c.scrb[:].bitcast(F32)
    lm = e24
    mxr16 = scr32[0:16, 0:128]
    mxrow = scr32[0:1, 128:128 + T]
    mxbs = scr32[0:24, 4096:4096 + T]
    lgs = scr32[0:24, 6144:6144 + T]
    sums = b.sb(f"{tag}_ssel", [24, 24], BF16, st)
    mx_dram = b.dtmp(f"scr_mx", [16, 128], F32)
    b.dma("gpsimd", rt[:].rearrange("p (c k) -> p c k", k=24), Rx.rearrange("(c p) k -> p c k", p=128))
    b.dma("gpsimd", at[:].rearrange("p (c k) -> p c k", k=24), Ap.rearrange("(c p) k -> p c k", p=128))
    b.dma("sync", sums[:], b.din("c_sumsel24", [24, 24], BF16))
    plog = pA[0:24, :]
    b.wait("tensor", "dG")
    b.wait("tensor", "dS")
    b.wait("tensor", "V")
    for k in range(nin_ch):
        rhs = rhs_getter(k)
        for tt in range(4):
            b.op("tensor", lambda e, k=k, tt=tt, rhs=rhs: e.matmul(
                out=plog[:, tt * 512:(tt + 1) * 512], lhsT=rt[:, k * 24:(k + 1) * 24],
                rhs=rhs[:, tt * 512:(tt + 1) * 512], start=(k == 0), stop=(k == nin_ch - 1)),
                sem="T" if tt == 3 else None)
    tv = b.val("T")
    b.wait("vector", "T", tv)
    b.op("vector", lambda e: e.tensor_copy(out=lgs, in_=plog), sem="V")
    vv = b.val("V")
    b.wait("tensor", "V", vv)
    for tile in range(16):
        b.op("tensor", lambda e, tile=tile: e.transpose(
            out=pB[0:128, tile * 24:(tile + 1) * 24],
            in_=lgs[0:24, tile * 128:(tile + 1) * 128], identity=c.identf[0:24, 0:24]),
            sem="T" if tile == 15 else None)
    tv = b.val("T")
    b.wait("vector", "T", tv)
    b.op("vector", lambda e: e.reduce_max(
        out=mxc[:], in_=pB[:, 0:384].rearrange("p (a b) -> p a b", b=24), axis=AX.X), sem="V")
    vv = b.val("V")
    b.wait("tensor", "V", vv)
    b.op("tensor", lambda e: e.transpose(
        out=pB[:, 384:400].bitcast(F32)[0:16, 0:128] if False else pB[0:16, 384:512],
        in_=mxc[:], identity=c.identf[:]), sem="T")
    tv = b.val("T")
    b.wait("vector", "T", tv)
    b.op("vector", lambda e: e.tensor_copy(out=mxr16, in_=pB[0:16, 384:512]), sem="V")
    b.wait("sync", "V")
    b.dma("sync", mx_dram, mxr16)
    b.wait("sync", "dS")
    b.dma("sync", mxrow, mx_dram.rearrange("a b -> (a b)").unsqueeze(0))
    b.wait("tensor", "dS")
    for tt in range(4):
        b.op("tensor", lambda e, tt=tt: e.matmul(
            out=pB[0:24, tt * 512:(tt + 1) * 512], lhsT=c.onesf_row[:, 0:24],
            rhs=mxrow[0:1, tt * 512:(tt + 1) * 512], start=True, stop=True),
            sem="T" if tt == 3 else None)
    tv = b.val("T")
    b.wait("vector", "T", tv)
    b.op("vector", lambda e: e.tensor_copy(out=mxbs, in_=pB[0:24, :]))
    b.op("vector", lambda e: e.tensor_tensor(out=lm[:], in0=lgs, in1=mxbs, op=ALU.subtract),
         sem="V")
    vv = b.val("V")
    b.wait("scalar", "V", vv)
    b.op("scalar", lambda e: e.activation(out=e24[:], in_=lm[:], func=AF.Exp), sem="S")
    sv = b.val("S")
    b.wait("vector", "S", sv)
    b.op("vector", lambda e: e.tensor_copy(out=e24b[:], in_=e24[:]), sem="V")
    vv = b.val("V")
    b.wait("tensor", "V", vv)
    b.wait("tensor", "S", sv)
    for tt in range(4):
        b.op("tensor", lambda e, tt=tt: e.matmul(
            out=plog[:, tt * 512:(tt + 1) * 512], lhsT=sums[:],
            rhs=e24b[:, tt * 512:(tt + 1) * 512], start=True, stop=True),
            sem="T" if tt == 3 else None)
    tv = b.val("T")
    b.wait("scalar", "T", tv)
    b.op("scalar", lambda e: e.activation(out=r24[:], in_=plog, func=AF.Ln))
    b.op("scalar", lambda e: e.activation(out=r24[:], in_=r24[:], func=AF.Exp, scale=-1.0),
         sem="S")
    sv = b.val("S")
    b.wait("vector", "S", sv)
    b.op("vector", lambda e: e.scalar_tensor_tensor(
        out=e24[:], in0=e24[:], scalar=0.125, in1=r24[:], op0=ALU.mult, op1=ALU.mult), sem="V")
    vv = b.val("V")
    pz = pB[0:24, :]
    b.wait("tensor", "S")
    b.wait("tensor", "V")
    for k in range(nin_ch):
        rhs = rhs_getter(k)
        for tt in range(4):
            b.op("tensor", lambda e, k=k, tt=tt, rhs=rhs: e.matmul(
                out=pz[:, tt * 512:(tt + 1) * 512], lhsT=at[:, k * 24:(k + 1) * 24],
                rhs=rhs[:, tt * 512:(tt + 1) * 512], start=(k == 0), stop=(k == nin_ch - 1)),
                sem="T" if tt == 3 else None)
    tv = b.val("T")
    b.wait("vector", "T", tv)
    b.op("vector", lambda e: e.tensor_tensor(out=zg_out[:], in0=pz, in1=e24[:], op=ALU.mult),
         sem="V")
    return b.val("V")


def emit_mole_main(b: Bld, c, tag, rhs_getter, nin_ch, nout_ch, Wt, Bp, bcol, zg, st,
                   out_cb, per_k_sem=False):
    fence_loads(b)
    bb = b.sb(f"{tag}_bb", [24, nout_ch * 128], BF16, st)
    wt0 = b.sb(f"{tag}_wt", [128, nin_ch * 128], BF16, st)
    wbufs = [wt0[:], c.scrb[:, 0:nin_ch * 128]]
    pws = b._mole_pws
    b._dbg_wt0 = wt0
    b.dma("gpsimd", bb[:], Bp)
    b.wait("tensor", "dG")
    gv_prev = None
    lim = int(os.environ.get("MOLE_NOUT", "999"))
    nout_eff = min(nout_ch, lim)
    wtgt = {}

    def wissue(hh):
        eng = ("gpsimd", "scalar")[hh % 2]
        b.wait(eng, "T")
        b.wait(eng, "V")
        wtgt[hh] = b.dma(eng, wbufs[hh % 2], Wt(hh), sem=f"wl{hh % 2}_{tag}")

    for hh in range(nout_eff):
        with contextlib.ExitStack() as st2:
            pw = pws[hh % 2][:]
            if hh not in wtgt:
                wissue(hh)
            if hh + 1 < nout_eff:
                wissue(hh + 1)
            b.wait("tensor", f"wl{hh % 2}_{tag}", wtgt[hh])
            if b.cnt.get("S", 0):
                b.wait("tensor", "S")
            if b.cnt.get("V", 0):
                b.wait("tensor", "V")
            nolora = os.environ.get("DBG_NOLORA") == "1"
            if hh >= 1 and os.environ.get("DBG_SKIP") == "mm":
                continue
            for k in range(nin_ch):
                rhs = rhs_getter(k)
                for tt in range(4):
                    b.op("tensor", lambda e, k=k, tt=tt, rhs=rhs, pw=pw, wb=wbufs[hh % 2], sp=(nolora and k == nin_ch - 1): e.matmul(
                        out=pw[:, tt * 512:(tt + 1) * 512], lhsT=wb[:, k * 128:(k + 1) * 128],
                        rhs=rhs[:, tt * 512:(tt + 1) * 512], start=(k == 0), stop=sp),
                        sem="T" if (tt == 3 and (per_k_sem or (nolora and k == nin_ch - 1))) else None)
            if not nolora:
                for tt in range(4):
                    b.op("tensor", lambda e, hh=hh, tt=tt, pw=pw: e.matmul(
                        out=pw[:, tt * 512:(tt + 1) * 512], lhsT=bb[:, hh * 128:(hh + 1) * 128],
                        rhs=zg[:, tt * 512:(tt + 1) * 512], start=False, stop=True),
                        sem="T" if tt == 3 else None)
            gv_prev = b.val("T")
            if os.environ.get("DBG_PROBE") == "1" and hh == 0:
                b.wait("gpsimd", "T", gv_prev)
                b.on("gpsimd", lambda e, p=pw: e.print(p[0:1, 0:6]))
            out_cb(hh, pw, st2)


import os
L1FFN_STAGE = int(os.environ.get("L1FFN_STAGE", "9"))


def emit_l1ffn(b: Bld, c, tag):
    W1t = b.din("L1_ffn_W1t", [HFF1 // 128, 128, D], F32)
    W1bf = precast_w(b, c, "l1W1", W1t, HFF1 * D)
    W2bf = precast_w(b, c, "l1W2", b.din("L1_ffn_W2t", [NCH, 128, HFF1], F32),
                     HFF1 * D)
    w1ap = lambda hh: bass.AP(W1bf.tensor, hh * 128 * D, [[D, 128], [1, D]])
    w2ap = lambda hh: bass.AP(W2bf.tensor, hh * 128 * HFF1,
                              [[HFF1, 128], [1, HFF1]])
    A1 = b.din("L1_ffn_A1p", [D, 24], F32)
    B1 = b.din("L1_ffn_B1p", [24, HFF1], F32)
    R1 = b.din("L1_ffn_R1x", [D, 24], F32)
    b1 = b.din("L1_ffn_b1c", [128, HFF1 // 128], F32)
    A2 = b.din("L1_ffn_A2p", [HFF1, 24], F32)
    B2 = b.din("L1_ffn_B2p", [24, D], F32)
    R2 = b.din("L1_ffn_R2x", [HFF1, 24], F32)
    b2 = b.din("L1_ffn_b2c", [128, NCH], F32)
    h1d = b.dtmp(f"scr_h1", [HFF1 // 128, 128, T], BF16)

    ln_compute(b, c, apply_bf16=True)
    if os.environ.get("DBG_TAP") == "ln":
        for ch in range(NCH):
            sl = slice(ch * T, (ch + 1) * T)
            b.op("vector", lambda e, sl=sl: e.tensor_copy(out=c.x[:, sl], in_=c.lnb[:, sl]),
                 sem="V")
        return
    if L1FFN_STAGE < 1:
        return

    NH1 = HFF1 // 128
    with contextlib.ExitStack() as st:
        zg1 = b.sb(f"{tag}_zg1", [24, T], BF16, st)
        b1t = b.sb(f"{tag}_b1t", [128, NH1], F32, st)
        hst = b.sb(f"{tag}_hst", [128, T], BF16, st)
        b.dma("sync", b1t[:], b1)

        def ln_chunk(k):
            return c.lnb[:, k * T:(k + 1) * T]

        b._mole_pws = [c.psA, c.psB]
        vv_zg = emit_mole_smalls(b, c, tag + "m1", ln_chunk, NCH, R1, A1, zg1, st,
                                 c.psA, c.psB)
        if os.environ.get("DBG_TAP") == "zg":
            b.op("vector", lambda e: e.tensor_copy(out=c.x[0:24, 0:T], in_=zg1[:]), sem="V")
            return
        if L1FFN_STAGE < 2:
            return

        dma_prev = [None]

        def out1(hh, pw, st2):
            if hh >= 1 and os.environ.get("DBG_SKIP") == "act":
                return
            if os.environ.get("DBG_TAP") == "wt0" and hh == 0:
                b.wait("vector", "wl_" + tag + "m1")
                b.op("vector", lambda e, w=b._dbg_wt0: e.tensor_copy(out=c.x[:, 0:1024],
                     in_=w[:]), sem="V")
            if os.environ.get("DBG_TAP") == "ln2" and hh == 0:
                b.op("vector", lambda e: e.tensor_copy(out=c.x[:, 0:T],
                     in_=c.lnb[:, 0:T]), sem="V")
            if os.environ.get("DBG_TAP") == "pw0" and hh == 0:
                b.wait("vector", "T")
                b.op("vector", lambda e: e.tensor_copy(out=c.x[:, 0:T], in_=pw), sem="V")
            b.wait("scalar", "T")
            if dma_prev[0] is not None:
                b.wait("scalar", "h1s", dma_prev[0])
            b.op("scalar", lambda e, hh=hh, pw=pw: e.activation(
                out=hst[:], in_=pw, func=AF.Relu, bias=b1t[:, hh:hh + 1]), sem="S")
            if os.environ.get("DBG_TAP") == "hst" and hh == 0:
                b.wait("vector", "S")
                b.op("vector", lambda e: e.tensor_copy(out=c.x[:, 0:T], in_=hst[:]), sem="V")
            if hh >= 1 and os.environ.get("DBG_SKIP") == "store":
                return
            b.wait("sync", "S")
            dma_prev[0] = b.dma("sync", h1d[hh], hst[:], sem="h1s")
            if os.environ.get("DBG_PROBE") == "1" and hh == 0:
                b.wait("gpsimd", "h1s", b.val("h1s"))
                b.on("gpsimd", lambda e: e.print(h1d[0][0:1, 0:6]))

        emit_mole_main(b, c, tag + "m1", ln_chunk, NCH, NH1, w1ap, B1, b1, zg1, st, out1)
        b.wait("vector", "dS")
    if os.environ.get("DBG_TAP") in ("wt0", "pw0", "ln2", "hst"):
        return
    if os.environ.get("DBG_TAP") == "h1":
        b.wait("gpsimd", "h1s", b.val("h1s"))
        for ch in range(min(NCH, int(os.environ.get("MOLE_NOUT", "8")))):
            slot = c.scrb[:, 14336:16384]
            b.dma("gpsimd", slot, b.dram["scr_h1"][ch])
            b.wait("vector", "dG")
            sl = slice(ch * T, (ch + 1) * T)
            b.op("vector", lambda e, sl=sl, slot=slot: e.tensor_copy(out=c.x[:, sl], in_=slot),
                 sem="V")
            b.wait("gpsimd", "V")
        return
    if L1FFN_STAGE < 3:
        return

    b.wait("sync", "h1s")
    with contextlib.ExitStack() as st:
        zg2 = b.sb(f"{tag}_zg2", [24, T], BF16, st)
        b2t = b.sb(f"{tag}_b2t2", [128, NCH], F32, st)
        h1a = b.sb(f"{tag}_h1a", [128, T], BF16, st)
        h1b = b.sb(f"{tag}_h1b", [128, T], BF16, st)
        b.dma("sync", b2t[:], b2)
        hbufs = [h1a, h1b]
        loaded = {}

        def h_chunk(k):
            buf = hbufs[k % 2]
            if loaded.get(k) is None:
                loaded[k] = b.dma("sync", buf[:], h1d[k])
                b.wait("tensor", "dS", loaded[k])
            return buf[:]

        hb3 = [h1a[:], h1b[:], c.scrb[:, 4608:6656]]

        class Reload:
            def __init__(self):
                self.tgt = {}
                self.prev_k = None

            def issue(self, k):
                eng = ("sync", "gpsimd", "sync")[k % 3]
                b.wait(eng, "T")
                self.tgt[k] = b.dma(eng, hb3[k % 3], h1d[k],
                                    sem=f"h1l{k % 3}")

            def __call__(self, k):
                if k == 0 and self.prev_k is not None:
                    self.tgt.clear()
                for kk in (k, k + 1, k + 2):
                    if kk < NH1 and kk not in self.tgt:
                        self.issue(kk)
                self.prev_k = k
                b.wait("tensor", f"h1l{k % 3}", self.tgt[k])
                return hb3[k % 3]

        rl = Reload()
        b._mole_pws = [c.psA, c.psB]
        emit_mole_smalls(b, c, tag + "m2", rl, NH1, R2, A2, zg2, st,
                         c.psA, c.psB)
        if L1FFN_STAGE < 4:
            return

        def out2(hh, pw, st2):
            b.wait("vector", "T")
            b.op("vector", lambda e, hh=hh: e.tensor_scalar(
                out=c.tf32[:], in0=pw[:], scalar1=b2t[:, hh:hh + 1], scalar2=0.5,
                op0=ALU.add, op1=ALU.mult))
            sl = slice(hh * T, (hh + 1) * T)
            b.op("vector", lambda e, sl=sl: e.tensor_tensor(
                out=c.x[:, sl], in0=c.x[:, sl], in1=c.tf32[:], op=ALU.add), sem="V")

        emit_mole_main(b, c, tag + "m2", rl, NH1, NCH, w2ap, B2, b2, zg2, st, out2,
                       per_k_sem=True)
        b.wait("sync", "V")





MAGIC = 12582912.0

NC_GROUP = [list(range(NCORES))]


def part_absmax_to_scalar(b, c, col, pB, dst):
    vv = b.val("V")
    b.wait("tensor", "V", vv)
    b.op("tensor", lambda e: e.transpose(out=pB[0:1, 0:128], in_=col, identity=c.identf[:]),
         sem="T")
    tv = b.val("T")
    b.wait("vector", "T", tv)
    b.op("vector", lambda e: e.reduce_max(out=dst, in_=pB[0:1, 0:128], axis=AX.X), sem="V")


def collective_max(b, c, src11, n, tag, out_sb):
    cin = b.dtmp(f"ccin_{tag}", [1, n], F32)
    cout = b.dtmp(f"ccout_{tag}", [1, n], F32)
    b.wait("gpsimd", "V")
    b.dma("gpsimd", cin, src11)
    b.wait("gpsimd", "dG")
    csem = f"cc_{tag}"
    h = b.sem(csem)
    b.cnt[csem] += 1
    grp = NC_GROUP if os.environ.get("SIM1") != "1" else [[0]]
    b.on("gpsimd", lambda e, h=h: e.collective_compute(
        "AllReduce", ALU.max, replica_groups=grp, ins=[cin], outs=[cout]).then_inc(h, 1))
    b.on("gpsimd", lambda e, h=h: e.wait_ge(h, 1))
    b.dma("gpsimd", out_sb, cout)
    b.wait("vector", "dG")


def make_scale_tiles(b, c, mx11, pB, s_col, rs_col, qmax=32767.0):
    srow = b._l0_srow
    b.op("vector", lambda e: e.tensor_scalar(out=srow[0:1, 0:1], in0=mx11, scalar1=1.0 / qmax,
                                             scalar2=1e-12, op0=ALU.mult, op1=ALU.max))
    b.op("vector", lambda e: e.reciprocal(out=srow[0:1, 1:2], in_=srow[0:1, 0:1]), sem="V")
    vv = b.val("V")
    b.wait("tensor", "V", vv)
    b.op("tensor", lambda e: e.matmul(out=pB[0:128, 0:2], lhsT=c.onesf_row[:],
                                      rhs=srow[0:1, 0:2], start=True, stop=True), sem="T")
    tv = b.val("T")
    b.wait("vector", "T", tv)
    b.op("vector", lambda e: e.tensor_copy(out=s_col, in_=pB[0:128, 0:1]))
    b.op("vector", lambda e: e.tensor_copy(out=rs_col, in_=pB[0:128, 1:2]), sem="V")


def emit_dequant(b, c, tag, w_given, wT, O, I, stash, stash_mode):
    nb = I // 32
    not_ = O // 128
    nic = I // 128
    with contextlib.ExitStack() as st:
        fence_loads(b)
        wbuf = b.sb(f"{tag}_dw", [128, max(I, O)], F32, st)
        sacc = b.sb(f"{tag}_sacc", [128, not_ * nb], F32, st)
        racc = b.sb(f"{tag}_racc", [128, not_ * nb], F32, st)
        str16 = b.sb(f"{tag}_str", [128, 128], F32, st)
        sT_d = b.dtmp(f"scr_sT_{tag}", [nb, O], F32)
        rT_d = b.dtmp(f"scr_rT_{tag}", [nb, O], F32)
        for ot in range(not_):
            b.wait("sync", "V")
            ldv = b.dma("sync", wbuf[:, 0:I], w_given[ot * 128:(ot + 1) * 128, :])
            b.wait("vector", "dS", ldv)
            sl = slice(ot * nb, (ot + 1) * nb)
            b.op("vector", lambda e, sl=sl: e.reduce_max(
                out=sacc[:, sl], in_=wbuf[:, 0:I].rearrange("p (a b) -> p a b", b=32),
                axis=AX.X, apply_absolute_value=True))
            b.op("vector", lambda e, sl=sl: e.tensor_scalar(
                out=sacc[:, sl], in0=sacc[:, sl], scalar1=1.0 / 7.0, scalar2=1e-12,
                op0=ALU.mult, op1=ALU.max))
            b.op("vector", lambda e, sl=sl: e.reciprocal(out=racc[:, sl], in_=sacc[:, sl]),
                 sem="V")
        for which, acc, dstd in (("s", sacc, sT_d), ("r", racc, rT_d)):
            for ot in range(not_):
                vv = b.val("V")
                b.wait("tensor", "V", vv)
                b.op("tensor", lambda e, ot=ot, acc=acc: e.transpose(
                    out=c.psB[0:nb, 0:128] if nb <= 128 else None,
                    in_=acc[:, ot * nb:(ot + 1) * nb], identity=c.identf[:]), sem="T")
                tv = b.val("T")
                b.wait("vector", "T", tv)
                b.op("vector", lambda e: e.tensor_copy(out=str16[0:nb, 0:128],
                                                       in_=c.psB[0:nb, 0:128]), sem="V")
                vv = b.val("V")
                b.wait("sync", "V", vv)
                sv = b.dma("sync", dstd[:, ot * 128:(ot + 1) * 128], str16[0:nb, 0:128])
                b.wait("vector", "dS", sv)
        scr32 = c.scrb[:].bitcast(F32)
        sexp = scr32[:, 0:O]
        rexp = scr32[:, 4096:4096 + O]
        wq = c.lnb[:, 0:O]
        b.wait("gpsimd", "dS")
        for ci in range(nic):
            b.wait("sync", "V")
            b.wait("gpsimd", "V")
            ldv = b.dma("sync", wbuf[:, 0:O], wT[ci * 128:(ci + 1) * 128, :])
            l2 = b.dma("gpsimd", sexp, bass.AP(sT_d.tensor, 4 * ci * O, [[O, 4], [0, 32], [1, O]]))
            l3 = b.dma("gpsimd", rexp, bass.AP(rT_d.tensor, 4 * ci * O, [[O, 4], [0, 32], [1, O]]))
            b.wait("vector", "dS", ldv)
            b.wait("vector", "dG", l3)
            b.op("vector", lambda e: e.tensor_tensor(out=wbuf[:, 0:O], in0=wbuf[:, 0:O],
                                                     in1=rexp, op=ALU.mult))
            b.op("vector", lambda e: e.tensor_scalar(out=wbuf[:, 0:O], in0=wbuf[:, 0:O],
                                                     scalar1=MAGIC, scalar2=MAGIC,
                                                     op0=ALU.add, op1=ALU.subtract))
            b.op("vector", lambda e: e.tensor_tensor(out=wq, in0=wbuf[:, 0:O],
                                                     in1=sexp, op=ALU.mult), sem="V")
            vv = b.val("V")
            b.wait("sync", "V", vv)
            if stash_mode == "tiled8":
                sv = b.dma("sync", bass.AP(stash.tensor, ci * 16384,
                                           [[128, 128], [nic * 16384, not_], [1, 128]]),
                           wq.rearrange("p (a j) -> p a j", j=128))
            else:
                sv = b.dma("sync", stash[ci], wq)
            b.wait("vector", "dS", sv)


def emit_l0ffn(b: Bld, c, tag):
    wg_g = b.din("L0_ffn_w_gate", [IFF0, D], F32)
    wu_g = b.din("L0_ffn_w_up", [IFF0, D], F32)
    wd_g = b.din("L0_ffn_w_down", [D, IFF0], F32)
    wg_t = b.din("L0_ffn_w_gateT", [D, IFF0], F32)
    wu_t = b.din("L0_ffn_w_upT", [D, IFF0], F32)
    wd_t = b.din("L0_ffn_w_downT", [IFF0, D], F32)
    stash_u = b.dtmp("scr_wu", [24 * 8 * 128 * 128], BF16)
    stash_g = b.dtmp("scr_wg", [24 * 8 * 128 * 128], BF16)
    stash_d = b.dtmp("scr_wd", [24, 128, 1024], BF16)
    up_d = b.dtmp("scr_up", [24, 128, T], F32)
    gate_d = b.dtmp("scr_gate", [24, 128, T], F32)
    gq_d = b.dtmp("scr_gq", [24, 128, T], I16)
    upq_d = b.dtmp("scr_upq", [24, 128, T], I16)
    g2q_d = b.dtmp("scr_g2q", [24, 128, T], I16)
    b._l0_srow = getattr(b, "_l0_srow", None) or b.sb("l0_srow", [1, 8], F32)

    if not getattr(b, "_l0_dequant_done", False):
        b._l0_dequant_done = True
        emit_dequant(b, c, tag + "du", wu_g, wu_t, IFF0, D, stash_u, "tiled8")
        emit_dequant(b, c, tag + "dg", wg_g, wg_t, IFF0, D, stash_g, "tiled8")
        emit_dequant(b, c, tag + "dd", wd_g, wd_t, D, IFF0, stash_d, "rows")

    ln_compute(b, c, apply_bf16=False)

    with contextlib.ExitStack() as st:
        fence_loads(b)
        macc = b.sb(f"{tag}_macc", [128, 1], F32, st)
        mcol = b.sb(f"{tag}_mcol", [128, 8], F32, st)
        scl = b.sb(f"{tag}_scl", [128, 8], F32, st)
        scl2 = b.sb(f"{tag}_scl2", [128, 4], F32, st)
        stA = contextlib.ExitStack()
        tfb = b.sb(f"{tag}_tfb", [128, T], F32, stA)
        xqi = b.sb(f"{tag}_xqi", [128, T], I16, stA)
        mrow = b._l0_srow

        for ch in range(NCH):
            sl = slice(ch * T, (ch + 1) * T)
            b.op("vector", lambda e, sl=sl: e.tensor_tensor(out=c.tf32[:], in0=c.x[:, sl],
                                                            in1=c.mean[:], op=ALU.subtract))
            b.op("vector", lambda e: e.tensor_tensor(out=tfb[:], in0=c.tf32[:],
                                                     in1=c.rstd[:], op=ALU.mult))
            b.op("vector", lambda e, ch=ch: e.reduce_max(out=mcol[:, ch:ch + 1], in_=tfb[:],
                                                         axis=AX.X, apply_absolute_value=True))
        b.op("vector", lambda e: e.reduce_max(out=macc[:], in_=mcol[:], axis=AX.X), sem="V")
        part_absmax_to_scalar(b, c, macc[:], c.psB, mrow[0:1, 0:1])
        collective_max(b, c, mrow[0:1, 0:1], 1, tag + "c1", mrow[0:1, 0:1])
        make_scale_tiles(b, c, mrow[0:1, 0:1], c.psB, scl[:, 0:1], scl[:, 1:2])

        for ch in range(NCH):
            sl = slice(ch * T, (ch + 1) * T)
            b.op("vector", lambda e, sl=sl: e.tensor_tensor(out=c.tf32[:], in0=c.x[:, sl],
                                                            in1=c.mean[:], op=ALU.subtract))
            b.op("vector", lambda e: e.tensor_tensor(out=tfb[:], in0=c.tf32[:],
                                                     in1=c.rstd[:], op=ALU.mult))
            b.op("vector", lambda e: e.tensor_scalar(out=c.tf32[:], in0=tfb[:],
                                                     scalar1=scl[:, 1:2], scalar2=MAGIC,
                                                     op0=ALU.mult, op1=ALU.add))
            b.op("vector", lambda e: e.tensor_scalar(out=xqi[:], in0=c.tf32[:],
                                                     scalar1=MAGIC, scalar2=32767.0,
                                                     op0=ALU.subtract, op1=ALU.min))
            b.op("vector", lambda e, sl=sl: e.tensor_scalar(out=c.lnb[:, sl], in0=xqi[:],
                                                            scalar1=scl[:, 0:1], scalar2=None,
                                                            op0=ALU.mult), sem="V")

        wtu = [b.sb(f"{tag}_wtu{i}", [128, 1024], BF16, stA) for i in range(2)]
        wtg = [b.sb(f"{tag}_wtg{i}", [128, 1024], BF16, stA) for i in range(2)]
        stg = [b.sb(f"{tag}_stg{i}", [128, 512], F32, stA) for i in range(4)]
        mxu = b.sb(f"{tag}_mxu", [128, 1], F32, stA)
        mxg = b.sb(f"{tag}_mxg", [128, 1], F32, stA)
        b.op("vector", lambda e: e.memset(mxu[:], 0.0))
        b.op("vector", lambda e: e.memset(mxg[:], 0.0), sem="V")
        tprev = {}
        vdone = {}
        sdone = {}
        b.wait("tensor", "V")
        for it in range(24):
            for wi, (wbufs, stsh) in enumerate(((wtu, stash_u), (wtg, stash_g))):
                wb = wbufs[it % 2]
                wsem = f"{tag}_w{wi}"
                if it >= 2 and tprev.get((it - 2, wi)) is not None:
                    b.wait("gpsimd", "T", tprev[(it - 2, wi)])
                b.dma("gpsimd", wb[:].rearrange("p (k j) -> p k j", j=128),
                      bass.AP(stsh.tensor, it * 8 * 16384, [[128, 128], [16384, 8], [1, 128]]),
                      sem=wsem)
                b.wait("tensor", wsem, b.val(wsem))
            for tt in range(4):
                for wi, wbufs in enumerate((wtu, wtg)):
                    wb = wbufs[it % 2]
                    ps = c.psA[:, (2 * tt + wi) % 4 * 512:((2 * tt + wi) % 4 + 1) * 512]
                    key = (2 * tt + wi) % 4
                    if vdone.get(key) is not None:
                        b.wait("tensor", "V", vdone[key])
                    for k in range(NCH):
                        b.op("tensor", lambda e, k=k, tt=tt, wb=wb, ps=ps: e.matmul(
                            out=ps, lhsT=wb[:, k * 128:(k + 1) * 128],
                            rhs=c.lnb[:, k * T + tt * 512:k * T + (tt + 1) * 512],
                            start=(k == 0), stop=(k == NCH - 1)),
                            sem="T" if k == NCH - 1 else None)
                    tv = b.val("T")
                    tprev[(it, wi)] = tv
                    sb_ = stg[key]
                    b.wait("vector", "T", tv)
                    if sdone.get(key) is not None:
                        b.wait("vector", "dS", sdone[key])
                    b.op("vector", lambda e, ps=ps, sb_=sb_: e.tensor_copy(out=sb_[:], in_=ps))
                    mx = (mxu, mxg)[wi]
                    b.op("vector", lambda e, ps=ps: e.reduce_max(
                        out=mcol[:, 0:1], in_=ps, axis=AX.X, apply_absolute_value=True))
                    b.op("vector", lambda e, mx=mx: e.tensor_tensor(
                        out=mx[:], in0=mx[:], in1=mcol[:, 0:1], op=ALU.max), sem="V")
                    vv = b.val("V")
                    vdone[key] = vv
                    b.wait("sync", "V", vv)
                    dst = (up_d, gate_d)[wi]
                    sdone[key] = b.dma("sync", dst[it][:, tt * 512:(tt + 1) * 512], sb_[:])
        part_absmax_to_scalar(b, c, mxu[:], c.psB, mrow[0:1, 2:3])
        part_absmax_to_scalar(b, c, mxg[:], c.psB, mrow[0:1, 3:4])
        collective_max(b, c, mrow[0:1, 2:4], 2, tag + "c2", mrow[0:1, 2:4])
        make_scale_tiles(b, c, mrow[0:1, 2:3], c.psB, scl[:, 2:3], scl[:, 3:4])
        make_scale_tiles(b, c, mrow[0:1, 3:4], c.psB, scl[:, 4:5], scl[:, 5:6])

        fence_loads(b)
        b.wait("vector", "dS")
        b.wait("vector", "dG")
        stA.close()
        gb = [b.sb(f"{tag}_gb{i}", [128, 512], F32, st) for i in range(2)]
        qb = [b.sb(f"{tag}_qb{i}", [128, 512], I16, st) for i in range(2)]
        t1 = b.sb(f"{tag}_t1", [128, 512], F32, st)
        t2 = b.sb(f"{tag}_t2", [128, 512], F32, st)
        t2u = b.sb(f"{tag}_t2u", [128, 512], U16, st)
        m4 = b.sb(f"{tag}_m4", [128, 1], F32, st)
        b.op("vector", lambda e: e.memset(m4[:], 0.0), sem="V")
        ldv = {}
        vdn = {}
        sdn = {}
        b.wait("sync", "dS")
        for i in range(96):
            it, tt = divmod(i, 4)
            bi = i % 2
            if vdn.get(i - 2) is not None:
                b.wait("sync", "V", vdn[i - 2])
            ldv[i] = b.dma("sync", gb[bi][:], gate_d[it][:, tt * 512:(tt + 1) * 512], sem="gl")
            b.wait("vector", "gl", ldv[i])
            if sdn.get(i - 2) is not None:
                b.wait("vector", "dG", sdn[i - 2])
            b.op("vector", lambda e, bi=bi: e.tensor_scalar(
                out=t2[:], in0=gb[bi][:], scalar1=scl[:, 5:6], scalar2=MAGIC,
                op0=ALU.mult, op1=ALU.add))
            b.op("vector", lambda e, bi=bi: e.tensor_scalar(
                out=qb[bi][:], in0=t2[:], scalar1=MAGIC, scalar2=32767.0,
                op0=ALU.subtract, op1=ALU.min))
            b.op("vector", lambda e, bi=bi: e.tensor_scalar(
                out=t1[:], in0=qb[bi][:], scalar1=scl[:, 4:5], scalar2=None, op0=ALU.mult),
                sem="V")
            vv = b.val("V")
            b.wait("scalar", "V", vv)
            b.op("scalar", lambda e: e.activation(out=t2[:], in_=t1[:], func=AF.Sigmoid),
                 sem="S")
            sv = b.val("S")
            b.wait("vector", "S", sv)
            b.op("vector", lambda e: e.tensor_scalar(
                out=t2[:], in0=t2[:], scalar1=65536.0, scalar2=MAGIC,
                op0=ALU.mult, op1=ALU.add))
            b.op("vector", lambda e: e.tensor_scalar(
                out=t2u[:], in0=t2[:], scalar1=MAGIC, scalar2=65535.0,
                op0=ALU.subtract, op1=ALU.min))
            b.op("vector", lambda e: e.tensor_scalar(
                out=t2[:], in0=t2u[:], scalar1=1.0, scalar2=None, op0=ALU.mult))
            b.op("vector", lambda e: e.tensor_tensor(out=t1[:], in0=t1[:], in1=t2[:],
                                                     op=ALU.mult))
            b.op("vector", lambda e: e.reduce_max(out=mcol[:, 0:1], in_=t1[:], axis=AX.X,
                                                  apply_absolute_value=True))
            b.op("vector", lambda e: e.tensor_tensor(out=m4[:], in0=m4[:], in1=mcol[:, 0:1],
                                                     op=ALU.max), sem="V")
            vdn[i] = b.val("V")
            b.wait("gpsimd", "V", vdn[i])
            sdn[i] = b.dma("gpsimd", gq_d[it][:, tt * 512:(tt + 1) * 512], qb[bi][:])
        part_absmax_to_scalar(b, c, m4[:], c.psB, mrow[0:1, 4:5])
        b.op("vector", lambda e: e.tensor_scalar(out=mrow[0:1, 4:5], in0=mrow[0:1, 4:5],
                                                 scalar1=1.0 / 65536.0, scalar2=None,
                                                 op0=ALU.mult), sem="V")
        collective_max(b, c, mrow[0:1, 4:5], 1, tag + "c3", mrow[0:1, 4:5])
        make_scale_tiles(b, c, mrow[0:1, 4:5], c.psB, scl[:, 6:7], scl[:, 7:8])
        b.op("vector", lambda e: e.tensor_scalar(out=scl[:, 7:8], in0=scl[:, 7:8],
                                                 scalar1=1.0 / 65536.0, scalar2=None,
                                                 op0=ALU.mult), sem="V")

        ub = gb
        qgb = [b.sb(f"{tag}_qg{i}", [128, 512], I16, st) for i in range(2)]
        q2b = [b.sb(f"{tag}_q2b{i}", [128, 512], I16, st) for i in range(2)]
        t3 = b.sb(f"{tag}_t3", [128, 512], F32, st)
        m5 = b.sb(f"{tag}_m5", [128, 1], F32, st)
        b.op("vector", lambda e: e.memset(m5[:], 0.0), sem="V")
        ldv = {}
        ldg = {}
        vdn = {}
        sdn = {}
        sdn2 = {}
        b.wait("sync", "gl")
        b.wait("sync", "V")
        b.wait("gpsimd", "dG")
        b.wait("gpsimd", "V")
        for i in range(96):
            it, tt = divmod(i, 4)
            bi = i % 2
            if vdn.get(i - 2) is not None:
                b.wait("sync", "V", vdn[i - 2])
                b.wait("gpsimd", "V", vdn[i - 2])
            ldv[i] = b.dma("sync", ub[bi][:], up_d[it][:, tt * 512:(tt + 1) * 512], sem="gl")
            ldg[i] = b.dma("gpsimd", qgb[bi][:], gq_d[it][:, tt * 512:(tt + 1) * 512], sem="gq")
            b.wait("vector", "gl", ldv[i])
            b.wait("vector", "gq", ldg[i])
            if sdn.get(i - 2) is not None:
                b.wait("vector", "dG", sdn2[i - 2])
            b.op("vector", lambda e, bi=bi: e.tensor_scalar(
                out=t3[:], in0=ub[bi][:], scalar1=scl[:, 3:4], scalar2=MAGIC,
                op0=ALU.mult, op1=ALU.add))
            b.op("vector", lambda e, bi=bi: e.tensor_scalar(
                out=q2b[bi][:], in0=t3[:], scalar1=MAGIC, scalar2=32767.0,
                op0=ALU.subtract, op1=ALU.min))
            b.op("vector", lambda e, bi=bi: e.tensor_scalar(
                out=t1[:], in0=qgb[bi][:], scalar1=scl[:, 4:5], scalar2=None, op0=ALU.mult),
                sem="V")
            vv = b.val("V")
            b.wait("scalar", "V", vv)
            b.op("scalar", lambda e: e.activation(out=t2[:], in_=t1[:], func=AF.Sigmoid),
                 sem="S")
            sv = b.val("S")
            b.wait("vector", "S", sv)
            b.op("vector", lambda e: e.tensor_scalar(
                out=t2[:], in0=t2[:], scalar1=65536.0, scalar2=MAGIC,
                op0=ALU.mult, op1=ALU.add))
            b.op("vector", lambda e: e.tensor_scalar(
                out=t2u[:], in0=t2[:], scalar1=MAGIC, scalar2=65535.0,
                op0=ALU.subtract, op1=ALU.min))
            b.op("vector", lambda e: e.tensor_scalar(
                out=t2[:], in0=t2u[:], scalar1=1.0, scalar2=None, op0=ALU.mult))
            b.op("vector", lambda e: e.tensor_tensor(out=t1[:], in0=t1[:], in1=t2[:],
                                                     op=ALU.mult))
            b.op("vector", lambda e: e.tensor_scalar(
                out=t2[:], in0=t1[:], scalar1=scl[:, 7:8], scalar2=MAGIC,
                op0=ALU.mult, op1=ALU.add))
            b.op("vector", lambda e, bi=bi: e.tensor_scalar(
                out=qgb[bi][:], in0=t2[:], scalar1=MAGIC, scalar2=32767.0,
                op0=ALU.subtract, op1=ALU.min))
            b.op("vector", lambda e, bi=bi: e.tensor_scalar(
                out=t3[:], in0=qgb[bi][:], scalar1=1.0, scalar2=None, op0=ALU.mult))
            b.op("vector", lambda e, bi=bi: e.tensor_scalar(
                out=t2[:], in0=q2b[bi][:], scalar1=1.0, scalar2=None, op0=ALU.mult))
            b.op("vector", lambda e: e.tensor_tensor(out=t3[:], in0=t3[:], in1=t2[:],
                                                     op=ALU.mult))
            b.op("vector", lambda e: e.reduce_max(out=mcol[:, 0:1], in_=t3[:], axis=AX.X,
                                                  apply_absolute_value=True))
            b.op("vector", lambda e: e.tensor_tensor(out=m5[:], in0=m5[:], in1=mcol[:, 0:1],
                                                     op=ALU.max), sem="V")
            vdn[i] = b.val("V")
            b.wait("gpsimd", "V", vdn[i])
            sdn[i] = b.dma("gpsimd", upq_d[it][:, tt * 512:(tt + 1) * 512], q2b[bi][:])
            sdn2[i] = b.dma("gpsimd", g2q_d[it][:, tt * 512:(tt + 1) * 512], qgb[bi][:])
        part_absmax_to_scalar(b, c, m5[:], c.psB, mrow[0:1, 5:6])
        b.op("vector", lambda e: e.tensor_scalar(out=mrow[0:1, 6:7], in0=scl[0:1, 6:7],
                                                 scalar1=mrow[0:1, 5:6], scalar2=None,
                                                 op0=ALU.mult))
        b.op("vector", lambda e: e.tensor_scalar(out=mrow[0:1, 5:6], in0=mrow[0:1, 6:7],
                                                 scalar1=scl[0:1, 2:3], scalar2=None,
                                                 op0=ALU.mult), sem="V")
        collective_max(b, c, mrow[0:1, 5:6], 1, tag + "c4", mrow[0:1, 5:6])
        make_scale_tiles(b, c, mrow[0:1, 5:6], c.psB, scl2[:, 0:1], scl2[:, 1:2])
        b.op("vector", lambda e: e.tensor_scalar(out=scl2[:, 2:3], in0=scl[:, 6:7],
                                                 scalar1=scl[:, 2:3], scalar2=None,
                                                 op0=ALU.mult))
        b.op("vector", lambda e: e.tensor_tensor(out=scl2[:, 2:3], in0=scl2[:, 2:3],
                                                 in1=scl2[:, 1:2], op=ALU.mult), sem="V")

        wdb = [b.sb(f"{tag}_wdb{i}", [128, 1024], BF16, st) for i in range(2)]
        ob = [b.sb(f"{tag}_ob{i}", [128, 512], BF16, st) for i in range(2)]
        b.wait("sync", "gl")
        b.wait("sync", "V")
        b.wait("gpsimd", "dG")
        b.wait("gpsimd", "V")
        ldv = {}
        ldg = {}
        ldw = {}
        odone = {}
        for tt in range(4):
            evv = b.val("V")
            b.wait("tensor", "V", evv)
            for it in range(24):
                i = tt * 24 + it
                bi = i % 2
                b.wait("sync", "V")
                b.wait("gpsimd", "V")
                ldv[i] = b.dma("sync", q2b[bi][:], upq_d[it][:, tt * 512:(tt + 1) * 512],
                               sem="gl")
                ldg[i] = b.dma("gpsimd", qgb[bi][:], g2q_d[it][:, tt * 512:(tt + 1) * 512],
                               sem="gq")
                if odone.get(i - 2) is not None:
                    b.wait("sync", "T", odone[i - 2])
                ldw[i] = b.dma("sync", wdb[bi][:], stash_d[it], sem=f"{tag}_wd")
                b.wait("tensor", f"{tag}_wd", ldw[i])
                b.wait("vector", "gl", ldv[i])
                b.wait("vector", "gq", ldg[i])
                if odone.get(i - 2) is not None:
                    b.wait("vector", "T", odone[i - 2])
                b.op("vector", lambda e, bi=bi: e.tensor_scalar(
                    out=t1[:], in0=qgb[bi][:], scalar1=1.0, scalar2=None, op0=ALU.mult))
                b.op("vector", lambda e, bi=bi: e.tensor_scalar(
                    out=t2[:], in0=q2b[bi][:], scalar1=1.0, scalar2=None, op0=ALU.mult))
                b.op("vector", lambda e: e.tensor_tensor(out=t1[:], in0=t1[:], in1=t2[:],
                                                         op=ALU.mult))
                b.op("vector", lambda e: e.tensor_scalar(
                    out=t2[:], in0=t1[:], scalar1=scl2[:, 2:3], scalar2=MAGIC,
                    op0=ALU.mult, op1=ALU.add))
                b.op("vector", lambda e: e.tensor_scalar(
                    out=t2u[:].bitcast(I16), in0=t2[:], scalar1=MAGIC, scalar2=32767.0,
                    op0=ALU.subtract, op1=ALU.min))
                b.op("vector", lambda e, bi=bi: e.tensor_scalar(
                    out=ob[bi][:], in0=t2u[:].bitcast(I16), scalar1=scl2[:, 0:1],
                    scalar2=None, op0=ALU.mult), sem="V")
                vv = b.val("V")
                b.wait("tensor", "V", vv)
                for hh in range(8):
                    pd = c.psA[:, hh % 4 * 512:(hh % 4 + 1) * 512] if hh < 4 else                         c.psB[:, (hh - 4) * 512:(hh - 3) * 512]
                    b.op("tensor", lambda e, hh=hh, it=it, bi=bi, pd=pd: e.matmul(
                        out=pd, lhsT=wdb[bi][:, hh * 128:(hh + 1) * 128],
                        rhs=ob[bi][:], start=(it == 0), stop=(it == 23)),
                        sem="T" if hh == 7 else None)
                odone[i] = b.val("T")
            tv = b.val("T")
            b.wait("vector", "T", tv)
            for hh in range(8):
                pd = c.psA[:, hh % 4 * 512:(hh % 4 + 1) * 512] if hh < 4 else                     c.psB[:, (hh - 4) * 512:(hh - 3) * 512]
                xs = c.x[:, hh * T + tt * 512: hh * T + (tt + 1) * 512]
                b.op("vector", lambda e, pd=pd, xs=xs: e.scalar_tensor_tensor(
                    out=xs, in0=pd, scalar=0.5, in1=xs, op0=ALU.mult, op1=ALU.add),
                    sem="V")


def stage_l0ffn(inputs):
    d = {}
    for n in ("w_gate", "w_up", "w_down"):
        w = np.ascontiguousarray(np.asarray(inputs[f"L0_ffn_{n}"], dtype=np.float32))
        d[f"L0_ffn_{n}"] = w
        d[f"L0_ffn_{n}T"] = np.ascontiguousarray(w.T)
    return d





def emit_l1attn(b: Bld, c, tag):
    Wqt = precast_w(b, c, "l1aWq", b.din("L1_attn_Wqt", [8, 128, D], F32), D * D)
    Wkt = precast_w(b, c, "l1aWk", b.din("L1_attn_Wkt", [8, 128, D], F32), D * D)
    Wvr = precast_w(b, c, "l1aWv", b.din("L1_attn_Wv", [D, D], F32), D * D)
    Wor = precast_w(b, c, "l1aWo", b.din("L1_attn_Wo", [D, D], F32), D * D)
    bqc = b.din("L1_attn_bqc", [128, 8], F32)
    bkc = b.din("L1_attn_bkc", [128, 8], F32)
    bvR = b.din("L1_attn_bvR", [1, D], F32)
    boR = b.din("L1_attn_boR", [1, D], F32)
    sidx = b.din("L1_attn_sidx", [48, 1], I32)
    lntm_d = b.dtmp("scr_lntm", [T, D], BF16)
    v_d = b.dtmp("scr_v", [16, 128, D], BF16)
    r_d = b.dtmp("scr_rank", [256 * 128], F32)
    if os.environ.get("DBG_BIS"):
        b._bislog = b.dtmp("scr_bislog", [28 * 80], F32)

    ln_compute(b, c, apply_bf16=True)
    scr32 = c.scrb[:].bitcast(F32)

    with contextlib.ExitStack() as st:
        fence_loads(b)
        c.iota_row = b.sb("iota_row", [128, T], F32, st)
        c.iota_col1 = b.sb("iota_col1", [128, 1], F32, st)
        c.tri = b.sb("tri_c", [128, 128], BF16, st)
        b.dma("sync", c.iota_row[:], b.din("c_iota_row", [128, T], F32))
        b.dma("sync", c.iota_col1[:], b.din("c_iota_col1", [128, 1], F32))
        b.dma("sync", c.tri[:], b.din("c_tri", [128, 128], BF16))
        b.wait("vector", "dS")
        lnb32 = c.lnb[:].bitcast(F32)
        mvrow = b.sb(f"{tag}_mvrow", [1, D], F32, st)
        mvb = scr32[0:48, 4096:4096 + D]
        
        m_all = b.sb(f"{tag}_mall", [128, 256], F32, st)
        lo = b.sb(f"{tag}_lo", [1, 16], F32, st)
        hi = b.sb(f"{tag}_hi", [1, 16], F32, st)
        mid = b.sb(f"{tag}_mid", [1, 16], F32, st)
        sel = b.sb(f"{tag}_sel", [1, 16], I32, st)
        cnt16 = b.sb(f"{tag}_cnt", [1, 16], F32, st)
        seln = b.sb(f"{tag}_seln", [1, 16], I32, st)
        red16 = b.sb(f"{tag}_red16", [128, 16], F32, st)
        red16b = b.sb(f"{tag}_red16b", [128, 16], BF16, st)
        rowx = None
        rowy = None
        cw_sb = None
        mask01 = b.sb(f"{tag}_mask", [128, 256], BF16, st)
        tr128 = b.sb(f"{tag}_tr128", [128, 128], F32, st)
        idx48 = b.sb(f"{tag}_idx", [48, 1], I32, st)
        sid48 = b.sb(f"{tag}_sid", [48, 1], I32, st)
        wmvbo_b = c.scrb[0:1, 14848:14848 + D]
        mvcol = b.sb(f"{tag}_mvcol", [128, 8], BF16, st)
        bvb = b.sb(f"{tag}_bvb", [1, D], BF16, st)
        wsem = f"{tag}_wl"
        stage = c.scrb[:, 0:T]

        for ch in range(NCH):
            sl = slice(ch * T, (ch + 1) * T)
            b.wait("vector", "dS")
            b.op("vector", lambda e, sl=sl: e.tensor_tensor(out=c.tf32[:], in0=c.x[:, sl],
                                                            in1=c.mean[:], op=ALU.subtract))
            b.op("vector", lambda e: e.tensor_tensor(out=c.tf32[:], in0=c.tf32[:],
                                                     in1=c.rstd[:], op=ALU.mult), sem="V")
            vv = b.val("V")
            b.wait("tensor", "V", vv)
            for g in range(4):
                for j in range(4):
                    tl = g * 4 + j
                    b.op("tensor", lambda e, tl=tl, j=j: e.transpose(
                        out=c.psB[:, j * 512:j * 512 + 128],
                        in_=c.tf32[:, tl * 128:(tl + 1) * 128], identity=c.identf[:]),
                        sem="T" if j == 3 else None)
                tv = b.val("T")
                b.wait("vector", "T", tv)
                for j in range(4):
                    tl = g * 4 + j
                    b.op("vector", lambda e, tl=tl, j=j: e.tensor_copy(
                        out=stage[:, tl * 128:(tl + 1) * 128],
                        in_=c.psB[:, j * 512:j * 512 + 128]),
                        sem="V" if j == 3 else None)
                vv2 = b.val("V")
                b.wait("tensor", "V", vv2)
            vv2 = b.val("V")
            b.wait("sync", "V", vv2)
            b.dma("sync", bass.AP(lntm_d.tensor, ch * 128,
                                  [[D, 128], [128 * D, 16], [1, 128]]),
                  stage[:].rearrange("p (a j) -> p a j", j=128))
        b.wait("vector", "dS")

        wb2 = [b.sb(f"{tag}_wb{i}", [128, 1024], BF16, st) for i in range(2)]
        vst = b.sb(f"{tag}_vst", [128, 1024], BF16, st)
        ksf = b.sb(f"{tag}_ksf", [128, 8 * 48], BF16, st)
        qch = b.sb(f"{tag}_qch", [128, T], BF16, st)
        bqt = b.sb(f"{tag}_bqt", [128, 8], F32, st)
        bkt = b.sb(f"{tag}_bkt", [128, 8], F32, st)
        b.dma("sync", bqt[:], bqc)
        b.dma("sync", bkt[:], bkc)
        ln48 = c.scrb[0:48, 4096:4096 + D]
        b.dma("sync", sid48[:], sidx)
        b.wait("vector", "dS")
        b.op("vector", lambda e: e.tensor_scalar(out=sid48[:], in0=sid48[:],
                                                 scalar1=float(T - 1), scalar2=None,
                                                 op0=ALU.min), sem="V")
        vv = b.val("V")
        b.wait("gpsimd", "V", vv)
        gx_t = b.gather("gpsimd", ln48, lntm_d, sid48[:], "gx")
        b.wait("vector", "gx", gx_t)
        ln48f = scr32[0:48, 0:D]
        b.op("vector", lambda e: e.tensor_copy(out=ln48f, in_=ln48), sem="V")
        vv = b.val("V")
        b.wait("tensor", "V", vv)
        for k in range(8):
            b.op("tensor", lambda e, k=k: e.transpose(
                out=c.psB[:, k * 48:(k + 1) * 48], in_=ln48f[0:48, k * 128:(k + 1) * 128],
                identity=c.identf[0:48, 0:48]), sem="T" if k == 7 else None)
        tv = b.val("T")
        b.wait("vector", "T", tv)
        ln48b = b.sb(f"{tag}_l48b", [128, 384], BF16, st)
        b.op("vector", lambda e: e.tensor_copy(out=ln48b[:], in_=c.psB[:, 0:384]),
             sem="V")

        scrK = c.scrb
        wsem = f"{tag}_wl"
        tprev = None
        for cc in range(8):
            if tprev is not None:
                b.wait("sync", "T", tprev)
            b.wait("sync", "V")
            b.dma("sync", wb2[cc % 2][:],
                  bass.AP(Wkt.tensor, cc * 128 * D, [[D, 128], [1, D]]), sem=wsem)
            b.wait("tensor", wsem, b.val(wsem))
            b.wait("tensor", "V")
            for tt in range(4):
                for k in range(8):
                    b.op("tensor", lambda e, k=k, tt=tt, wb=wb2[cc % 2]: e.matmul(
                        out=c.psA[:, tt * 512:(tt + 1) * 512],
                        lhsT=wb[:, k * 128:(k + 1) * 128],
                        rhs=c.lnb[:, k * T + tt * 512:k * T + (tt + 1) * 512],
                        start=(k == 0), stop=(k == 7)),
                        sem="T" if k == 7 else None)
            for k in range(8):
                b.op("tensor", lambda e, k=k, wb=wb2[cc % 2]: e.matmul(
                    out=c.psB[:, 1024:1072], lhsT=wb[:, k * 128:(k + 1) * 128],
                    rhs=ln48b[:, k * 48:(k + 1) * 48],
                    start=(k == 0), stop=(k == 7)), sem="T" if k == 7 else None)
            tprev = b.val("T")
            b.wait("vector", "T", tprev)
            for tt in range(4):
                b.op("vector", lambda e, cc=cc, tt=tt: e.tensor_scalar(
                    out=scrK[:, cc * T + tt * 512:cc * T + (tt + 1) * 512],
                    in0=c.psA[:, tt * 512:(tt + 1) * 512], scalar1=bkt[:, cc:cc + 1],
                    scalar2=None, op0=ALU.add))
            b.op("vector", lambda e, cc=cc: e.tensor_scalar(
                out=ksf[:, cc * 48:(cc + 1) * 48], in0=c.psB[:, 1024:1072],
                scalar1=bkt[:, cc:cc + 1], scalar2=None, op0=ALU.add), sem="V")

        b.dma("gpsimd", bvb[:], bvR)
        b.wait("tensor", "dG")
        sprev = {}
        for tg in range(4):
            b.wait("tensor", "V")
            for tsub in range(4):
                for half in range(2):
                    ps = (c.psA, c.psB)[tsub // 2][:, (tsub % 2) * 1024 + half * 512:
                                                   (tsub % 2) * 1024 + (half + 1) * 512]
                    b.op("tensor", lambda e, ps=ps, half=half: e.matmul(
                        out=ps, lhsT=c.onesb_row[0:1, 0:128],
                        rhs=bvb[0:1, half * 512:(half + 1) * 512],
                        start=True, stop=False))
            wop = None
            b.wait("sync", "T")
            for k in range(8):
                if wop is not None and k >= 2:
                    b.wait("sync", "T", wop)
                b.dma("sync", wb2[k % 2][:],
                      bass.AP(Wvr.tensor, k * 128 * D, [[D, 128], [1, D]]), sem=wsem)
                b.wait("tensor", wsem, b.val(wsem))
                for tsub in range(4):
                    ttile = tg * 4 + tsub
                    for half in range(2):
                        ps = (c.psA, c.psB)[tsub // 2][:, (tsub % 2) * 1024 + half * 512:
                                                       (tsub % 2) * 1024 + (half + 1) * 512]
                        b.op("tensor", lambda e, k=k, half=half, ttile=ttile, ps=ps: e.matmul(
                            out=ps,
                            lhsT=c.lnb[:, k * T + ttile * 128:k * T + (ttile + 1) * 128],
                            rhs=wb2[k % 2][:, half * 512:(half + 1) * 512],
                            start=False, stop=(k == 7)),
                            sem="T" if (tsub == 3 and half == 1) else None)
                wop = b.val("T")
            tv = b.val("T")
            b.wait("vector", "T", tv)
            for tsub in range(4):
                ttile = tg * 4 + tsub
                if sprev.get(ttile - 1) is not None:
                    b.wait("vector", "vs", sprev[ttile - 1])
                b.wait("vector", "T")
                ps0 = (c.psA, c.psB)[tsub // 2]
                b.op("vector", lambda e, tsub=tsub, ps0=ps0: e.tensor_copy(
                    out=vst[:], in_=ps0[:, (tsub % 2) * 1024:(tsub % 2) * 1024 + 1024]),
                    sem="V")
                vv = b.val("V")
                b.wait("sync", "V", vv)
                sprev[ttile] = b.dma("sync", v_d[ttile], vst[:], sem="vs")
        b.wait("gpsimd", "T")
        b.wait("gpsimd", "V")
        b.wait("gpsimd", "vs")
        mprev = None
        for ttile in range(16):
            if mprev is not None:
                b.wait("gpsimd", "T", mprev)
            lv = b.dma("gpsimd", wb2[ttile % 2][:], v_d[ttile], sem="mvl")
            b.wait("tensor", "mvl", b.val("mvl"))
            if ttile == 0:
                b.wait("tensor", "V")
            for half in range(2):
                b.op("tensor", lambda e, half=half, ttile=ttile: e.matmul(
                    out=c.psA[0:1, half * 512:(half + 1) * 512],
                    lhsT=c.onesb_col[:, 0:1],
                    rhs=wb2[ttile % 2][:, half * 512:(half + 1) * 512],
                    start=(ttile == 0), stop=(ttile == 15)),
                    sem="T" if half == 1 else None)
            mprev = b.val("T")
        tv = b.val("T")
        b.wait("vector", "T", tv)
        b.op("vector", lambda e: e.tensor_scalar(out=mvrow[:], in0=c.psA[0:1, 0:1024],
                                                 scalar1=1.0 / T, scalar2=None,
                                                 op0=ALU.mult), sem="V")
        vv = b.val("V")
        b.wait("tensor", "V", vv)
        for k in range(8):
            b.op("tensor", lambda e, k=k: e.transpose(
                out=c.psB[:, 1024 + k:1024 + k + 1], in_=mvrow[0:1, k * 128:(k + 1) * 128],
                identity=c.identf[0:1, 0:1]), sem="T" if k == 7 else None)
        tv = b.val("T")
        b.wait("vector", "T", tv)
        b.op("vector", lambda e: e.tensor_copy(out=mvcol[:], in_=c.psB[:, 1024:1032]), sem="V")

        tprev = None
        for cc in range(8):
            if tprev is not None:
                b.wait("sync", "T", tprev)
            b.wait("sync", "V")
            b.dma("sync", wb2[0][:],
                  bass.AP(Wqt.tensor, cc * 128 * D, [[D, 128], [1, D]]), sem=wsem)
            b.wait("tensor", wsem, b.val(wsem))
            b.wait("tensor", "V")
            for tt in range(4):
                for k in range(8):
                    b.op("tensor", lambda e, k=k, tt=tt: e.matmul(
                        out=c.psA[:, tt * 512:(tt + 1) * 512],
                        lhsT=wb2[0][:, k * 128:(k + 1) * 128],
                        rhs=c.lnb[:, k * T + tt * 512:k * T + (tt + 1) * 512],
                        start=(k == 0), stop=(k == 7)),
                        sem="T" if k == 7 else None)
            tprev = b.val("T")
            b.wait("vector", "T", tprev)
            for tt in range(4):
                b.op("vector", lambda e, cc=cc, tt=tt: e.tensor_scalar(
                    out=qch[:, tt * 512:(tt + 1) * 512],
                    in0=c.psA[:, tt * 512:(tt + 1) * 512], scalar1=bqt[:, cc:cc + 1],
                    scalar2=None, op0=ALU.add), sem="V" if tt == 3 else None)
            vv = b.val("V")
            b.wait("tensor", "V", vv)
            for hp in range(2):
                hh = cc * 2 + hp
                pbase = hp * 64
                for ltg in range(4):
                    for j in range(4):
                        lt = ltg * 4 + j
                        b.op("tensor", lambda e, lt=lt, j=j, pbase=pbase, cc=cc: e.matmul(
                            out=c.psB[:, j * 512:j * 512 + 48],
                            lhsT=qch[pbase:pbase + 64, lt * 128:(lt + 1) * 128],
                            rhs=ksf[pbase:pbase + 64, cc * 48:cc * 48 + 48],
                            start=True, stop=True, tile_position=(pbase, 0)),
                            sem="T" if j == 3 else None)
                    tv = b.val("T")
                    b.wait("vector", "T", tv)
                    for j in range(4):
                        lt = ltg * 4 + j
                        col = hh * 16 + lt
                        b.op("vector", lambda e, j=j: e.reduce_max(
                            out=c.tf32[:, 0:1], in_=c.psB[:, j * 512:j * 512 + 45],
                            axis=AX.X))
                        b.op("vector", lambda e, j=j: e.reduce_sum(
                            out=c.tf32[:, 1:2], in_=c.psB[:, j * 512:j * 512 + 45],
                            axis=AX.X))
                        b.op("vector", lambda e, col=col: e.scalar_tensor_tensor(
                            out=m_all[:, col:col + 1], in0=c.tf32[:, 1:2],
                            scalar=-1.0 / 45.0, in1=c.tf32[:, 0:1],
                            op0=ALU.mult, op1=ALU.add), sem="V" if j == 3 else None)
                    vv = b.val("V")
                    b.wait("tensor", "V", vv)

        b.op("vector", lambda e: e.reduce_max(
            out=red16[:], in_=m_all[:].rearrange("p (h l) -> p h l", l=16), axis=AX.X))
        b.op("vector", lambda e: e.tensor_copy(out=tr128[:, 0:16], in_=red16[:]), sem="V")
        vv = b.val("V")
        b.wait("tensor", "V", vv)
        b.op("tensor", lambda e: e.transpose(out=c.psB[0:16, 0:128], in_=tr128[:, 0:16],
                                             identity=c.identf[:]), sem="T")
        tv = b.val("T")
        b.wait("vector", "T", tv)
        b.op("vector", lambda e: e.reduce_max(out=c.tf32[0:16, 0:1],
                                              in_=c.psB[0:16, 0:128], axis=AX.X), sem="V")
        vv = b.val("V")
        b.wait("tensor", "V", vv)
        b.op("tensor", lambda e: e.matmul(out=c.psB[0:1, 128:144], lhsT=c.tf32[0:16, 0:1],
                                          rhs=c.identf[0:16, 0:16], start=True, stop=True),
             sem="T")
        tv = b.val("T")
        b.wait("vector", "T", tv)
        b.op("vector", lambda e: e.tensor_scalar(out=hi[:], in0=c.psB[0:1, 128:144],
                                                 scalar1=0.1, scalar2=None, op0=ALU.add))
        b.op("vector", lambda e: e.tensor_scalar(
            out=c.tf32[:, 0:256], in0=m_all[:], scalar1=-1.0, scalar2=None, op0=ALU.mult))
        b.op("vector", lambda e: e.reduce_max(
            out=red16[:], in_=c.tf32[:, 0:256].rearrange("p (h l) -> p h l", l=16),
            axis=AX.X))
        b.op("vector", lambda e: e.tensor_copy(out=tr128[:, 0:16], in_=red16[:]), sem="V")
        vv = b.val("V")
        b.wait("tensor", "V", vv)
        b.op("tensor", lambda e: e.transpose(out=c.psB[0:16, 0:128], in_=tr128[:, 0:16],
                                             identity=c.identf[:]), sem="T")
        tv = b.val("T")
        b.wait("vector", "T", tv)
        b.op("vector", lambda e: e.reduce_max(out=c.tf32[0:16, 0:1], in_=c.psB[0:16, 0:128],
                                              axis=AX.X), sem="V")
        vv = b.val("V")
        b.wait("tensor", "V", vv)
        b.op("tensor", lambda e: e.matmul(out=c.psB[0:1, 128:144], lhsT=c.tf32[0:16, 0:1],
                                          rhs=c.identf[0:16, 0:16], start=True, stop=True),
             sem="T")
        tv = b.val("T")
        b.wait("vector", "T", tv)
        b.op("vector", lambda e: e.tensor_scalar(out=lo[:], in0=c.psB[0:1, 128:144],
                                                 scalar1=-1.0, scalar2=-1.0,
                                                 op0=ALU.mult, op1=ALU.add), sem="V")
        for itn in range(28):
            b.op("vector", lambda e: e.tensor_tensor(out=mid[:], in0=lo[:], in1=hi[:],
                                                     op=ALU.add))
            b.op("vector", lambda e: e.tensor_scalar(out=mid[:], in0=mid[:], scalar1=0.5,
                                                     scalar2=None, op0=ALU.mult), sem="V")
            vv = b.val("V")
            b.wait("tensor", "V", vv)
            b.op("tensor", lambda e: e.matmul(out=c.psB[0:128, 0:16], lhsT=c.onesf_row[:],
                                              rhs=mid[:], start=True, stop=True), sem="T")
            tv = b.val("T")
            b.wait("vector", "T", tv)
            b.op("vector", lambda e: e.tensor_tensor(
                out=mask01[:].rearrange("p (h l) -> p h l", l=16),
                in0=m_all[:].rearrange("p (h l) -> p h l", l=16),
                in1=c.psB[0:128, 0:16].unsqueeze(2).broadcast_to([128, 16, 16]),
                op=ALU.is_gt))
            b.op("vector", lambda e: e.reduce_sum(
                out=red16[:], in_=mask01[:].rearrange("p (h l) -> p h l", l=16),
                axis=AX.X))
            b.op("vector", lambda e: e.tensor_copy(out=red16b[:], in_=red16[:]), sem="V")
            vv = b.val("V")
            b.wait("tensor", "V", vv)
            b.op("tensor", lambda e: e.matmul(out=c.psB[0:1, 16:32], lhsT=c.onesb_col[:, 0:1],
                                              rhs=red16b[:], start=True, stop=True), sem="T")
            tv = b.val("T")
            b.wait("vector", "T", tv)
            b.op("vector", lambda e: e.tensor_scalar(out=sel[:], in0=c.psB[0:1, 16:32],
                                                     scalar1=39.5, scalar2=None,
                                                     op0=ALU.is_gt))
            if os.environ.get("DBG_BIS"):
                b.op("vector", lambda e: e.tensor_copy(out=cnt16[:], in_=c.psB[0:1, 16:32]),
                     sem="V")
                vv = b.val("V")
                b.wait("sync", "V", vv)
                b.dma("sync", bass.AP(b._bislog.tensor, itn * 80, [[1, 16]]), mid[:])
                b.dma("sync", bass.AP(b._bislog.tensor, itn * 80 + 16, [[1, 16]]), cnt16[:])
                b.dma("sync", bass.AP(b._bislog.tensor, itn * 80 + 32, [[1, 16]]), sel[:])
                b.dma("sync", bass.AP(b._bislog.tensor, itn * 80 + 48, [[1, 16]]), lo[:])
                b.dma("sync", bass.AP(b._bislog.tensor, itn * 80 + 64, [[1, 16]]), hi[:])
                b.wait("vector", "dS")
            b.op("vector", lambda e: e.tensor_scalar(out=seln[:], in0=c.psB[0:1, 16:32],
                                                     scalar1=39.5, scalar2=None,
                                                     op0=ALU.is_le))
            b.op("vector", lambda e: e.select(out=lo[:], mask=sel[:], on_true=mid[:],
                                              on_false=lo[:], add_drain=True))
            b.op("vector", lambda e: e.select(out=hi[:], mask=seln[:], on_true=mid[:],
                                              on_false=hi[:], add_drain=True), sem="V")
        vv = b.val("V")
        b.wait("tensor", "V", vv)
        b.op("tensor", lambda e: e.matmul(out=c.psB[0:128, 0:16], lhsT=c.onesf_row[:],
                                          rhs=lo[:], start=True, stop=True), sem="T")
        tv = b.val("T")
        b.wait("vector", "T", tv)
        b.op("vector", lambda e: e.tensor_tensor(
            out=mask01[:].rearrange("p (h l) -> p h l", l=16),
            in0=m_all[:].rearrange("p (h l) -> p h l", l=16),
            in1=c.psB[0:128, 0:16].unsqueeze(2).broadcast_to([128, 16, 16]),
            op=ALU.is_gt), sem="V")
        vv = b.val("V")
        b.wait("tensor", "V", vv)
        b.op("tensor", lambda e: e.matmul(out=c.psA[:, 0:256], lhsT=c.tri[:],
                                          rhs=mask01[:], start=True, stop=True), sem="T")
        tv = b.val("T")
        b.wait("vector", "T", tv)
        cw_sb = lnb32[0:128, 3584:3840]
        b.op("vector", lambda e, cw_sb=cw_sb: e.tensor_copy(out=cw_sb, in_=c.psA[:, 0:256]),
             sem="V")
        rowx = lnb32[0:1, 3072:3328]
        rowy = lnb32[0:1, 3328:3584]
        rowt = lnb32[0:1, 3840:4096]
        vv = b.val("V")
        b.wait("tensor", "V", vv)
        b.op("tensor", lambda e: e.matmul(out=c.psB[0:1, 256:512], lhsT=c.onesb_col[:, 0:1],
                                          rhs=mask01[:], start=True, stop=True), sem="T")
        tv = b.val("T")
        b.wait("vector", "T", tv)
        b.op("vector", lambda e, rowx=rowx: e.tensor_copy(
            out=rowx, in_=c.psB[0:1, 256:512]))
        b.op("vector", lambda e, rowt=rowt: e.tensor_copy(
            out=rowt, in_=c.psB[0:1, 256:512]), sem="V")
        cur, oth = rowx, rowy
        for k in (1, 2, 4, 8):
            b.op("vector", lambda e, cur=cur, oth=oth: e.tensor_copy(out=oth, in_=cur))
            b.op("vector", lambda e, cur=cur, oth=oth, k=k: e.tensor_tensor(
                out=oth.rearrange("p (h l) -> p h l", l=16)[:, :, k:16],
                in0=cur.rearrange("p (h l) -> p h l", l=16)[:, :, k:16],
                in1=cur.rearrange("p (h l) -> p h l", l=16)[:, :, 0:16 - k],
                op=ALU.add))
            cur, oth = oth, cur
        b.op("vector", lambda e, cur=cur, rowt=rowt: e.tensor_tensor(
            out=cur, in0=cur, in1=rowt, op=ALU.subtract), sem="V")
        excl = cur
        vv = b.val("V")
        b.wait("tensor", "V", vv)
        b.op("tensor", lambda e, excl=excl: e.matmul(out=c.psB[0:128, 0:256],
                                                     lhsT=c.onesf_row[:], rhs=excl,
                                                     start=True, stop=True), sem="T")
        tv = b.val("T")
        b.wait("vector", "T", tv)
        b.op("vector", lambda e: e.tensor_tensor(out=cw_sb, in0=cw_sb,
                                                 in1=c.psB[0:128, 0:256], op=ALU.add))
        b.op("vector", lambda e: e.tensor_tensor(out=cw_sb, in0=cw_sb, in1=mask01[:],
                                                 op=ALU.mult), sem="V")
        vv = b.val("V")
        b.wait("tensor", "V", vv)
        for half in range(2):
            b.op("tensor", lambda e, half=half: e.transpose(
                out=c.psB[:, half * 512:half * 512 + 128],
                in_=cw_sb[0:128, half * 128:(half + 1) * 128], identity=c.identf[:]),
                sem="T" if half == 1 else None)
        tv = b.val("T")
        b.wait("vector", "T", tv)
        for half in range(2):
            b.wait("vector", "dS")
            b.op("vector", lambda e, half=half: e.tensor_copy(
                out=tr128[:], in_=c.psB[:, half * 512:half * 512 + 128]), sem="V")
            vv = b.val("V")
            b.wait("sync", "V", vv)
            b.dma("sync", bass.AP(r_d.tensor, half * 128 * 128, [[128, 128], [1, 128]]),
                  tr128[:])
        b.wait("sync", "dS")

        pt_all = b.sb(f"{tag}_ptall", [128, 16 * 16 * 40], BF16, st)
        sums_all = b.sb(f"{tag}_sums", [48, 16], F32, st)
        p40 = c.lnb[0:48, 0:4096].bitcast(F32)
        qred = b.sb(f"{tag}_qred", [128, 48], BF16, st)
        l40 = b.sb(f"{tag}_l40", [48, D], BF16, st)
        mxneg = b.sb(f"{tag}_mxneg", [48, 1], F32, st)
        tprev3 = None
        for hh in range(16):
            rhbc = lnb32[0:48, 3072:3072 + T]
            b.wait("sync", "V")
            b.dma("sync", rhbc, bass.AP(r_d.tensor, hh * T, [[0, 48], [1, T]]), sem="rl")
            b.wait("vector", "rl", b.val("rl"))
            ef = lnb32[0:48, 5120:5120 + T]
            b.op("vector", lambda e, rhbc=rhbc: e.tensor_scalar(out=ef, in0=rhbc,
                                                     scalar1=c.iota_col1[0:48, :],
                                                     scalar2=None, op0=ALU.is_equal))
            b.op("vector", lambda e: e.tensor_tensor(out=ef, in0=ef,
                                                     in1=c.iota_row[0:48, :], op=ALU.mult))
            b.op("vector", lambda e: e.reduce_sum(out=lnb32[0:48, 7168:7169], in_=ef,
                                                  axis=AX.X))
            b.op("vector", lambda e: e.tensor_scalar(
                out=lnb32[0:48, 7168:7169], in0=lnb32[0:48, 7168:7169],
                scalar1=float(T - 1), scalar2=0.0, op0=ALU.min, op1=ALU.max))
            b.op("vector", lambda e: e.tensor_copy(out=idx48[:], in_=lnb32[0:48, 7168:7169]),
                 sem="V")
            vv = b.val("V")
            b.wait("gpsimd", "V", vv)
            gx_t = b.gather("gpsimd", l40[:], lntm_d, idx48[:], "gx")
            b.wait("vector", "gx", gx_t)
            l40f = lnb32[0:48, 2048:2048 + D]
            b.op("vector", lambda e, l40f=l40f: e.tensor_copy(out=l40f, in_=l40[:]), sem="V")
            vv = b.val("V")
            b.wait("tensor", "V", vv)
            for k in range(8):
                b.op("tensor", lambda e, k=k: e.transpose(
                    out=c.psB[:, k * 48:(k + 1) * 48], in_=l40f[0:48, k * 128:(k + 1) * 128],
                    identity=c.identf[0:48, 0:48]), sem="T" if k == 7 else None)
            tv = b.val("T")
            b.wait("vector", "T", tv)
            l40b = c.lnb[0:128, 6144:6144 + 384]
            b.op("vector", lambda e, l40b=l40b: e.tensor_copy(out=l40b, in_=c.psB[:, 0:384]),
                 sem="V")
            pbase = (hh % 2) * 64
            chq = hh // 2
            if hh % 2 == 0:
                if tprev3 is not None:
                    b.wait("sync", "T", tprev3)
                b.wait("sync", "V")
                b.dma("sync", wb2[0][:],
                      bass.AP(Wqt.tensor, chq * 128 * D, [[D, 128], [1, D]]), sem=wsem)
            b.wait("tensor", wsem, b.val(wsem))
            b.wait("tensor", "V")
            for k in range(8):
                b.op("tensor", lambda e, k=k, pbase=pbase, l40b=l40b: e.matmul(
                    out=c.psA[pbase:pbase + 64, 0:48],
                    lhsT=wb2[0][:, k * 128 + pbase:k * 128 + pbase + 64],
                    rhs=l40b[:, k * 48:(k + 1) * 48],
                    start=(k == 0), stop=(k == 7),
                    tile_position=(0, pbase) if pbase else None),
                    sem="T" if k == 7 else None)
            tprev3 = b.val("T")
            b.wait("vector", "T", tprev3)
            b.op("vector", lambda e, pbase=pbase, chq=chq: e.tensor_scalar(
                out=qred[pbase:pbase + 64, :], in0=c.psA[pbase:pbase + 64, 0:48],
                scalar1=bqt[pbase:pbase + 64, chq:chq + 1], scalar2=None, op0=ALU.add),
                sem="V")
            vv = b.val("V")
            b.wait("tensor", "V", vv)
            for j in range(4):
                b.op("tensor", lambda e, hh=hh, j=j, pbase=pbase, chq=chq: e.matmul(
                    out=c.psA[0:40, j * 512:(j + 1) * 512],
                    lhsT=qred[pbase:pbase + 64, 0:40],
                    rhs=c.scrb[pbase:pbase + 64, chq * T + j * 512:chq * T + (j + 1) * 512],
                    start=True, stop=True, tile_position=(pbase, 0)),
                    sem="T" if j == 3 else None)
            tv = b.val("T")
            b.wait("vector", "T", tv)
            b.op("vector", lambda e: e.reduce_max(out=mxneg[0:40, :], in_=c.psA[0:40, 0:T],
                                                  axis=AX.X))
            b.op("vector", lambda e: e.tensor_scalar(out=mxneg[0:40, :], in0=mxneg[0:40, :],
                                                     scalar1=-SCALE, scalar2=None,
                                                     op0=ALU.mult), sem="V")
            vv = b.val("V")
            b.wait("scalar", "V", vv)
            b.op("scalar", lambda e, hh=hh: e.activation(
                out=p40[0:40, 0:T], in_=c.psA[0:40, 0:T], func=AF.Exp, bias=mxneg[0:40, :],
                scale=SCALE, accum_out=sums_all[0:40, hh:hh + 1]), sem="S")
            sv = b.val("S")
            b.wait("tensor", "S", sv)
            b.wait("tensor", "V")
            for j in range(16):
                b.op("tensor", lambda e, j=j: e.transpose(
                    out=c.psB[:, (j % 8) * 48:(j % 8) * 48 + 40],
                    in_=p40[0:40, j * 128:(j + 1) * 128], identity=c.identf[0:40, 0:40]),
                    sem="T" if (j % 8 == 7) else None)
                if j % 8 == 7:
                    tv = b.val("T")
                    b.wait("vector", "T", tv)
                    for j2 in range(8):
                        jj = (j // 8) * 8 + j2
                        b.op("vector", lambda e, hh=hh, jj=jj, j2=j2: e.tensor_copy(
                            out=pt_all[:, hh * 640 + jj * 40:hh * 640 + (jj + 1) * 40],
                            in_=c.psB[:, j2 * 48:j2 * 48 + 40]),
                            sem="V" if j2 == 7 else None)
                    vv = b.val("V")
                    b.wait("tensor", "V", vv)

        vch = [c.scrb[:, 11776 + i * D:11776 + (i + 1) * D] for i in range(2)]
        dfm_g = scr32[0:128, 5120:5888].bitcast(BF16)
        rs40 = b.sb(f"{tag}_rs40", [48, 16], F32, st)
        wob2 = wb2
        b.op("vector", lambda e: e.reciprocal(out=rs40[0:40, :], in_=sums_all[0:40, :]),
             sem="V")
        vv = b.val("V")
        b.wait("tensor", "V", vv)
        for half in range(2):
            b.op("tensor", lambda e, half=half: e.matmul(
                out=c.psB[0:48, half * 512:(half + 1) * 512], lhsT=c.onesf_row[:, 0:48],
                rhs=mvrow[0:1, half * 512:(half + 1) * 512], start=True, stop=True),
                sem="T" if half == 1 else None)
        tv = b.val("T")
        b.wait("vector", "T", tv)
        b.op("vector", lambda e: e.tensor_copy(out=mvb, in_=c.psB[0:48, 0:1024]), sem="V")
        b.wait("sync", "T")
        dlt = scr32[0:48, 0:1024]
        for pss in range(2):
            vprev = None
            for rch in range(16):
                if vprev is not None:
                    b.wait("sync", "T", vprev)
                lv = b.dma("sync", vch[rch % 2], v_d[rch], sem="vl")
                b.wait("tensor", "vl", lv)
                if rch == 0:
                    b.wait("tensor", "V")
                for hb in range(8):
                    hh = pss * 8 + hb
                    ps = (c.psA, c.psB)[hb // 4][0:40, (hb % 4) * 512:(hb % 4) * 512 + 64]
                    b.op("tensor", lambda e, hh=hh, rch=rch, ps=ps: e.matmul(
                        out=ps,
                        lhsT=pt_all[:, hh * 640 + rch * 40:hh * 640 + (rch + 1) * 40],
                        rhs=vch[rch % 2][:, hh * 64:(hh + 1) * 64],
                        start=(rch == 0), stop=(rch == 15)),
                        sem="T" if hb == 7 else None)
                vprev = b.val("T")
            b.wait("vector", "T", vprev)
            for hb in range(8):
                hh = pss * 8 + hb
                ps = (c.psA, c.psB)[hb // 4][0:40, (hb % 4) * 512:(hb % 4) * 512 + 64]
                b.op("vector", lambda e, hh=hh, ps=ps: e.tensor_scalar(
                    out=dlt[0:40, hh * 64:(hh + 1) * 64], in0=ps,
                    scalar1=rs40[0:40, hh:hh + 1], scalar2=None, op0=ALU.mult))
            b.op("vector", lambda e, pss=pss: e.tensor_tensor(
                out=dlt[0:40, pss * 512:(pss + 1) * 512],
                in0=dlt[0:40, pss * 512:(pss + 1) * 512],
                in1=mvb[0:40, pss * 512:(pss + 1) * 512], op=ALU.subtract), sem="V")
            vv = b.val("V")
            b.wait("tensor", "V", vv)
        for kb in range(2):
            for k2 in range(4):
                k = kb * 4 + k2
                b.op("tensor", lambda e, k=k, k2=k2: e.transpose(
                    out=c.psB[0:128, k2 * 512:k2 * 512 + 40],
                    in_=dlt[0:40, k * 128:(k + 1) * 128],
                    identity=c.identf[0:40, 0:40]), sem="T" if k2 == 3 else None)
            tv = b.val("T")
            b.wait("vector", "T", tv)
            for k2 in range(4):
                k = kb * 4 + k2
                b.op("vector", lambda e, k=k, k2=k2: e.tensor_copy(
                    out=dfm_g[:, k * 40:(k + 1) * 40],
                    in_=c.psB[0:128, k2 * 512:k2 * 512 + 40]),
                    sem="V" if k2 == 3 else None)
            vv = b.val("V")
            b.wait("tensor", "V", vv)
        wop = None
        b.wait("sync", "T")
        for k in range(8):
            if wop is not None and k >= 2:
                b.wait("sync", "T", wop)
            b.wait("sync", "V")
            b.dma("sync", wob2[k % 2][:],
                  bass.AP(Wor.tensor, k * 128 * D, [[D, 128], [1, D]]), sem=wsem)
            b.wait("tensor", wsem, b.val(wsem))
            if k == 0:
                b.wait("tensor", "V")
            for sub in range(2):
                pso = (c.psA, c.psB)[sub]
                for half in range(2):
                    b.op("tensor", lambda e, k=k, sub=sub, half=half, pso=pso: e.matmul(
                        out=pso[0:40, half * 512:(half + 1) * 512],
                        lhsT=dfm_g[sub * 64:(sub + 1) * 64, k * 40:(k + 1) * 40],
                        rhs=wob2[k % 2][sub * 64:(sub + 1) * 64, half * 512:(half + 1) * 512],
                        start=True, stop=True,
                        tile_position=(sub * 64, 0) if sub else None),
                        sem="T" if (sub == 1 and half == 1) else None)
            wop = b.val("T")
            b.wait("vector", "T", wop)
            for sub in range(2):
                hh = 2 * k + sub
                pso = (c.psA, c.psB)[sub]
                b.op("vector", lambda e, hh=hh, pso=pso: e.tensor_copy(
                    out=c.lnb[0:40, hh * 1024:(hh + 1) * 1024], in_=pso[0:40, 0:1024]),
                    sem="V" if sub == 1 else None)
            vv = b.val("V")
            b.wait("tensor", "V", vv)

        bob = scr32[0:1, 5888:5888 + D]
        b.wait("sync", "V")
        b.dma("sync", bob, boR)
        b.wait("vector", "dS")
        wop = None
        b.wait("sync", "T")
        for k in range(8):
            if wop is not None:
                b.wait("sync", "T", wop)
            b.dma("sync", wob2[k % 2][:],
                  bass.AP(Wor.tensor, k * 128 * D, [[D, 128], [1, D]]), sem=wsem)
            b.wait("tensor", wsem, b.val(wsem))
            for half in range(2):
                b.op("tensor", lambda e, k=k, half=half: e.matmul(
                    out=c.psA[0:1, half * 512:(half + 1) * 512], lhsT=mvcol[:, k:k + 1],
                    rhs=wob2[k % 2][:, half * 512:(half + 1) * 512],
                    start=(k == 0), stop=(k == 7)),
                    sem="T" if half == 1 else None)
            wop = b.val("T")
        tv = b.val("T")
        b.wait("vector", "T", tv)
        b.op("vector", lambda e: e.tensor_tensor(out=bob, in0=bob,
                                                 in1=c.psA[0:1, 0:D], op=ALU.add))
        b.op("vector", lambda e: e.tensor_copy(out=wmvbo_b, in_=bob), sem="V")

        e_all = c.scrb[0:48, 0:16 * 512]
        for tt in range(4):
            rhbc_z = scr32[0:48, 4608:4608 + 512]
            for hh in range(16):
                vv = b.val("V")
                b.wait("sync", "V", vv)
                b.dma("sync", rhbc_z, bass.AP(r_d.tensor, hh * T + tt * 512,
                                              [[0, 48], [1, 512]]), sem="rl")
                b.wait("vector", "rl", b.val("rl"))
                b.op("vector", lambda e, hh=hh, rhbc_z=rhbc_z: e.tensor_scalar(
                    out=e_all[:, hh * 512:(hh + 1) * 512], in0=rhbc_z,
                    scalar1=c.iota_col1[0:48, :], scalar2=None, op0=ALU.is_equal), sem="V")
            vv = b.val("V")
            b.wait("tensor", "V", vv)
            for dch in range(8):
                ps = (c.psA, c.psB)[dch // 4][:, (dch % 4) * 512:(dch % 4 + 1) * 512]
                b.op("tensor", lambda e, dch=dch, ps=ps: e.matmul(
                    out=ps, lhsT=wmvbo_b[0:1, dch * 128:(dch + 1) * 128],
                    rhs=c.onesb_row[0:1, 0:512],
                    start=True, stop=False))
            for hh in range(16):
                for dch in range(8):
                    ps = (c.psA, c.psB)[dch // 4][:, (dch % 4) * 512:(dch % 4 + 1) * 512]
                    b.op("tensor", lambda e, hh=hh, dch=dch, ps=ps: e.matmul(
                        out=ps, lhsT=c.lnb[0:40, hh * 1024 + dch * 128:hh * 1024 + (dch + 1) * 128],
                        rhs=e_all[0:40, hh * 512:(hh + 1) * 512],
                        start=False, stop=(hh == 15)),
                        sem="T" if (hh == 15 and dch == 7) else None)
            tv = b.val("T")
            b.wait("vector", "T", tv)
            for dch in range(8):
                ps = (c.psA, c.psB)[dch // 4][:, (dch % 4) * 512:(dch % 4 + 1) * 512]
                xs = c.x[:, dch * T + tt * 512:dch * T + (tt + 1) * 512]
                b.op("vector", lambda e, ps=ps, xs=xs: e.tensor_tensor(
                    out=xs, in0=xs, in1=ps, op=ALU.add), sem="V")
            vv = b.val("V")
            b.wait("tensor", "V", vv)




def stage_l1attn(inputs):
    d = {}
    for n in ("Wq", "Wk"):
        d[f"L1_attn_{n}t"] = pack_pm(np.asarray(inputs[f"L1_attn_{n}"], dtype=np.float32))
    d["L1_attn_Wv"] = np.ascontiguousarray(np.asarray(inputs["L1_attn_Wv"], dtype=np.float32))
    d["L1_attn_Wo"] = np.ascontiguousarray(np.asarray(inputs["L1_attn_Wo"], dtype=np.float32))
    for n in ("bq", "bk"):
        d[f"L1_attn_{n}c"] = bias_cols(np.asarray(inputs[f"L1_attn_{n}"], dtype=np.float32))
    d["L1_attn_bvR"] = np.asarray(inputs["L1_attn_bv"], dtype=np.float32).reshape(1, D)
    d["L1_attn_boR"] = np.asarray(inputs["L1_attn_bo"], dtype=np.float32).reshape(1, D)
    si = np.asarray(inputs["L1_attn_sample_idx"], dtype=np.int32)
    d["L1_attn_sidx"] = np.ascontiguousarray(
        np.pad(si, (0, 3), mode="edge").reshape(48, 1))
    return d



SCALE0 = 0.125
PVOFF = int(os.environ.get('PVOFF', '64'))
PVSTEP = int(os.environ.get('PVSTEP', '512'))


def emit_l0attn(b: Bld, c, tag):
    Wqt = precast_w(b, c, "l0Wq", b.din("L0_attn_Wqpm", [8, 128, D], F32), D * D)
    Wkt = precast_w(b, c, "l0Wk", b.din("L0_attn_Wkpm", [8, 128, D], F32), D * D)
    WvT = precast_w(b, c, "l0Wv", b.din("L0_attn_WvT", [D, D], F32), D * D)
    WoT = precast_w(b, c, "l0Wo", b.din("L0_attn_WoT", [D, D], F32), D * D)
    bqc = b.din("L0_attn_bqc", [128, 8], F32)
    bkc = b.din("L0_attn_bkc", [128, 8], F32)
    bvR = b.din("L0_attn_bvR", [1, D], F32)
    boc = b.din("L0_attn_boc", [128, 8], F32)
    distT = b.din("L0_attn_distT", [128, 73], BF16)
    qT_d = b.dtmp(f"scr0_qT_{tag}", [D, T], BF16)
    kT_d = b.dtmp(f"scr0_kT_{tag}", [D, T], BF16)
    qd_d = b.dtmp(f"scr0_qd_{tag}", [16 * T * 512], BF16)
    ao_d = b.dtmp(f"scr0_ao_{tag}", [16, 128, D], BF16)

    ln_compute(b, c, apply_bf16=True)
    scr32 = c.scrb[:].bitcast(F32)
    lnb32 = c.lnb[:].bitcast(F32)

    with contextlib.ExitStack() as st:
        fence_loads(b)
        wsem = f"{tag}_wl"
        wb2 = [b.sb(f"{tag}_wb{i}", [128, 1024], BF16, st) for i in range(2)]
        bqt = b.sb(f"{tag}_bqt", [128, 8], F32, st)
        bkt = b.sb(f"{tag}_bkt", [128, 8], F32, st)
        bot = b.sb(f"{tag}_bot", [128, 8], F32, st)
        distb = b.sb(f"{tag}_dist", [128, 73], BF16, st)
        qd02 = b.sb(f"{tag}_qd02", [128, 512], F32, st)
        mnegall = b.sb(f"{tag}_mnegall", [128, 256], F32, st)
        mall0 = b.sb(f"{tag}_mall0", [1, 256], F32, st)
        qnl = b.sb(f"{tag}_qnl", [1, 256], F32, st)
        knl = b.sb(f"{tag}_knl", [1, 16], F32, st)
        hmF = b.sb(f"{tag}_hmF", [128, 2], F32, st)
        qde = [b.sb(f"{tag}_qde{i}", [128, 512], BF16, st) for i in range(2)]
        bvb = b.sb(f"{tag}_bvb", [1, D], BF16, st)
        acc = b.sb(f"{tag}_acc", [128, 65], F32, st)
        tmp65 = b.sb(f"{tag}_tmp65", [128, 65], F32, st)
        rcol = b.sb(f"{tag}_rcol", [128, 1], F32, st)
        e0c = b.sb(f"{tag}_e0c", [128, 1], F32, st)
        e72c = b.sb(f"{tag}_e72c", [128, 1], F32, st)
        aob = b.sb(f"{tag}_aob", [128, 64], BF16, st)
        b.dma("sync", bqt[:], bqc)
        b.dma("sync", bkt[:], bkc)
        b.dma("sync", bot[:], boc)
        b.dma("sync", distb[:], distT)
        b.dma("gpsimd", bvb[:], bvR)
        b.wait("vector", "dS")
        b.wait("vector", "dG")
        b.op("vector", lambda e: e.memset(hmF[0:64, 0:1], 1.0))
        b.op("vector", lambda e: e.memset(hmF[64:128, 0:1], 0.0))
        b.op("vector", lambda e: e.memset(hmF[0:64, 1:2], 0.0))
        b.op("vector", lambda e: e.memset(hmF[64:128, 1:2], 1.0))
        b.op("vector", lambda e: e.memset(c.tf32[:, 0:256], 0.0), sem="V")

        def proj_dimmajor(Wt, bcol, is_q):
            tprev = None
            tmm = {}
            wtg = {}

            def pissue(cc):
                if cc >= 2 and tmm.get(cc - 2) is not None:
                    b.wait("sync", "T", tmm[cc - 2])
                else:
                    b.wait("sync", "T")
                b.dma("sync", wb2[cc % 2][:],
                      bass.AP(Wt.tensor, cc * 128 * D, [[D, 128], [1, D]]), sem=wsem)
                wtg[cc] = b.val(wsem)

            for cc in range(8):
                if cc not in wtg:
                    pissue(cc)
                if cc + 1 < 8:
                    pissue(cc + 1)
                b.wait("tensor", wsem, wtg[cc])
                b.wait("tensor", "V")
                for tt in range(4):
                    for k in range(8):
                        b.op("tensor", lambda e, k=k, tt=tt, wb=wb2[cc % 2]: e.matmul(
                            out=c.psA[:, tt * 512:(tt + 1) * 512],
                            lhsT=wb[:, k * 128:(k + 1) * 128],
                            rhs=c.lnb[:, k * T + tt * 512:k * T + (tt + 1) * 512],
                            start=(k == 0), stop=(k == 7)),
                            sem="T" if k == 7 else None)
                tprev = b.val("T")
                tmm[cc] = tprev
                b.wait("vector", "T", tprev)
                for tt in range(4):
                    b.op("vector", lambda e, cc=cc, tt=tt, bcol=bcol: e.tensor_scalar(
                        out=c.scrb[:, cc * T + tt * 512:cc * T + (tt + 1) * 512],
                        in0=c.psA[:, tt * 512:(tt + 1) * 512], scalar1=bcol[:, cc:cc + 1],
                        scalar2=None, op0=ALU.add), sem="V" if tt == 3 else None)
                b.op("vector", lambda e, cc=cc: e.tensor_tensor(
                    out=c.tf32[:], in0=c.scrb[:, cc * T:(cc + 1) * T],
                    in1=c.scrb[:, cc * T:(cc + 1) * T], op=ALU.mult), sem="V")
                vv = b.val("V")
                b.wait("tensor", "V", vv)
                for half in range(2):
                    hh = cc * 2 + half
                    for tt in range(4):
                        b.op("tensor", lambda e, half=half, tt=tt: e.matmul(
                            out=c.psB[0:1, tt * 512:(tt + 1) * 512],
                            lhsT=hmF[:, half:half + 1],
                            rhs=c.tf32[:, tt * 512:(tt + 1) * 512],
                            start=True, stop=True), sem="T" if tt == 3 else None)
                    tv = b.val("T")
                    b.wait("vector", "T", tv)
                    if is_q:
                        b.op("vector", lambda e: e.reduce_max(
                            out=c.tf32[0:1, 256:272],
                            in_=c.psB[0:1, 0:T].rearrange("p (a l) -> p a l", l=128),
                            axis=AX.X), sem="V")
                        vv = b.val("V")
                        b.wait("scalar", "V", vv)
                        b.op("scalar", lambda e, hh=hh: e.activation(
                            out=qnl[0:1, hh * 16:(hh + 1) * 16], in_=c.tf32[0:1, 256:272],
                            func=AF.Ln, bias=c.eps[0:1, :]), sem="S")
                    else:
                        b.op("vector", lambda e: e.reduce_max(
                            out=c.tf32[0:1, 256:257], in_=c.psB[0:1, 0:T], axis=AX.X),
                            sem="V")
                        vv = b.val("V")
                        b.wait("scalar", "V", vv)
                        b.op("scalar", lambda e, hh=hh: e.activation(
                            out=knl[0:1, hh:hh + 1], in_=c.tf32[0:1, 256:257],
                            func=AF.Ln, bias=c.eps[0:1, :]), sem="S")
                    sv = b.val("S")
                    b.wait("tensor", "S", sv)
                    b.wait("vector", "S", sv)
            return tprev

        proj_dimmajor(Wqt, bqt, True)

        b.op("vector", lambda e: e.memset(c.tf32[:, 0:256], 0.0), sem="V")
        vv = b.val("V")
        b.wait("tensor", "V", vv)
        prevQ = {}
        for h in range(16):
            cc, pb = h // 2, (h % 2) * 64
            for lt in range(16):
                idx2 = (h * 16 + lt) * 2
                ib = lt % 2
                pcol = ib * 512
                if prevQ.get(ib) is not None:
                    b.wait("tensor", "S", prevQ[ib])
                    b.wait("tensor", "V")
                b.op("tensor", lambda e, cc=cc, pb=pb, lt=lt, pcol=pcol: e.matmul(
                    out=c.psA[:, pcol:pcol + 73],
                    lhsT=c.scrb[pb:pb + 64, cc * T + lt * 128:cc * T + (lt + 1) * 128],
                    rhs=distb[pb:pb + 64, :], start=True, stop=True,
                    tile_position=(pb, 0) if pb else None), sem="T")
                tv = b.val("T")
                b.wait("vector", "T", tv)
                b.op("vector", lambda e, idx2=idx2, pcol=pcol: e.tensor_copy(
                    out=qd02[:, idx2:idx2 + 1], in_=c.psA[:, pcol:pcol + 1]))
                b.op("vector", lambda e, idx2=idx2, pcol=pcol: e.tensor_copy(
                    out=qd02[:, idx2 + 1:idx2 + 2], in_=c.psA[:, pcol + 72:pcol + 73]),
                    sem="V")
                vvq = b.val("V")
                b.wait("scalar", "V", vvq)
                b.wait("gpsimd", "V", vvq)
                b.wait("scalar", f"qd{ib}")
                b.wait("gpsimd", f"qd{ib}")
                b.op("gpsimd", lambda e, ib=ib, idx2=idx2: e.tensor_scalar(
                    out=qde[ib][:, 0:191], in0=c.tf32[:, 0:191],
                    scalar1=qd02[:, idx2:idx2 + 1], scalar2=None, op0=ALU.add))
                b.op("gpsimd", lambda e, ib=ib, idx2=idx2: e.tensor_scalar(
                    out=qde[ib][:, 264:512], in0=c.tf32[:, 0:248],
                    scalar1=qd02[:, idx2 + 1:idx2 + 2], scalar2=None, op0=ALU.add),
                    sem="G")
                gvq = b.val("G")
                b.op("scalar", lambda e, ib=ib, pcol=pcol: e.activation(
                    out=qde[ib][:, 191:264], in_=c.psA[:, pcol:pcol + 73], func=AF.Copy),
                    sem="S")
                prevQ[ib] = b.val("S")
                b.wait("sync", "S", prevQ[ib])
                b.wait("sync", "G", gvq)
                b.dma("sync", bass.AP(qd_d.tensor, h * T * 512 + lt * 128 * 512,
                                      [[512, 128], [1, 512]]), qde[ib][:], sem=f"qd{ib}")
        b.wait("sync", "V")
        for cc in range(8):
            b.dma("sync", bass.AP(qT_d.tensor, cc * 128 * T, [[T, 128], [1, T]]),
                  c.scrb[:, cc * T:(cc + 1) * T], sem="qs")

        b.wait("vector", "qs", b.val("qs"))
        proj_dimmajor(Wkt, bkt, False)
        b.wait("sync", "V")
        for cc in range(8):
            b.dma("sync", bass.AP(kT_d.tensor, cc * 128 * T, [[T, 128], [1, T]]),
                  c.scrb[:, cc * T:(cc + 1) * T], sem="ks")

        b.op("vector", lambda e: e.tensor_tensor(
            out=mall0[:].rearrange("p (h l) -> p h l", l=16),
            in0=qnl[:].rearrange("p (h l) -> p h l", l=16),
            in1=knl[:].unsqueeze(2).broadcast_to([1, 16, 16]), op=ALU.add), sem="V")
        vv = b.val("V")
        b.wait("scalar", "V", vv)
        b.op("scalar", lambda e: e.activation(out=mall0[:], in_=mall0[:], func=AF.Exp,
                                              scale=0.5), sem="S")
        sv = b.val("S")
        b.wait("vector", "S", sv)
        b.op("vector", lambda e: e.tensor_scalar(out=mall0[:], in0=mall0[:],
                                                 scalar1=-SCALE0, scalar2=None,
                                                 op0=ALU.mult), sem="V")
        vv = b.val("V")
        b.wait("tensor", "V", vv)
        b.op("tensor", lambda e: e.matmul(out=c.psB[0:128, 0:256], lhsT=c.onesf_row[:],
                                          rhs=mall0[:], start=True, stop=True), sem="T")
        tv = b.val("T")
        b.wait("vector", "T", tv)
        b.op("vector", lambda e: e.tensor_copy(out=mnegall[:], in_=c.psB[0:128, 0:256]),
             sem="V")

        b.wait("vector", "ks", b.val("ks"))
        sprev = {}
        for tg in range(4):
            b.wait("tensor", "V")
            for tsub in range(4):
                for half in range(2):
                    ps = (c.psA, c.psB)[tsub // 2][:, (tsub % 2) * 1024 + half * 512:
                                                   (tsub % 2) * 1024 + (half + 1) * 512]
                    b.op("tensor", lambda e, ps=ps, half=half: e.matmul(
                        out=ps, lhsT=c.onesb_row[0:1, 0:128],
                        rhs=bvb[0:1, half * 512:(half + 1) * 512],
                        start=True, stop=False))
            wop = None
            b.wait("sync", "T")
            for k in range(8):
                if wop is not None and k >= 2:
                    b.wait("sync", "T", wop)
                b.dma("sync", wb2[k % 2][:],
                      bass.AP(WvT.tensor, k * 128 * D, [[D, 128], [1, D]]), sem=wsem)
                b.wait("tensor", wsem, b.val(wsem))
                for tsub in range(4):
                    ttile = tg * 4 + tsub
                    for half in range(2):
                        ps = (c.psA, c.psB)[tsub // 2][:, (tsub % 2) * 1024 + half * 512:
                                                       (tsub % 2) * 1024 + (half + 1) * 512]
                        b.op("tensor", lambda e, k=k, half=half, ttile=ttile, ps=ps: e.matmul(
                            out=ps,
                            lhsT=c.lnb[:, k * T + ttile * 128:k * T + (ttile + 1) * 128],
                            rhs=wb2[k % 2][:, half * 512:(half + 1) * 512],
                            start=False, stop=(k == 7)),
                            sem="T" if (tsub == 3 and half == 1) else None)
                wop = b.val("T")
            tv = b.val("T")
            b.wait("vector", "T", tv)
            for tsub in range(4):
                ttile = tg * 4 + tsub
                ps0 = (c.psA, c.psB)[tsub // 2]
                b.op("vector", lambda e, tsub=tsub, ttile=ttile, ps0=ps0: e.tensor_copy(
                    out=c.scrb[:, ttile * 1024:(ttile + 1) * 1024],
                    in_=ps0[:, (tsub % 2) * 1024:(tsub % 2) * 1024 + 1024]),
                    sem="V" if tsub == 3 else None)
            vv = b.val("V")
            b.wait("tensor", "V", vv)

        PtB = [c.lnb[:, 0:2048], c.lnb[:, 12288:14336]]
        PbandB = [c.lnb[:, 2048:2432], c.lnb[:, 14336:14720]]
        khb = [c.lnb[0:64, 2560 + i * 2048:4608 + i * 2048] for i in range(2)]
        qhb = [c.lnb[0:64, 6656 + i * 2048:8704 + i * 2048] for i in range(2)]
        vaug = c.lnb[:, 10752:11792]
        B_b = c.lnb[:, 11792:12176]
        aoL = c.lnb[:, 12288:13312]
        aoT_b = c.lnb[:, 13312:14336]
        wo_row = c.lnb[:, 14336:15360]
        b.wait("sync", "qs", b.val("qs"))
        b.wait("sync", "ks", b.val("ks"))
        b.wait("sync", "qd0", b.val("qd0"))
        b.wait("sync", "qd1", b.val("qd1"))
        b.wait("sync", "T")
        tend = {}
        for h in range(16):
            if h >= 2:
                b.wait("sync", "T", tend[h - 2])
            b.wait("sync", "V")
            lk = b.dma("sync", khb[h % 2], bass.AP(kT_d.tensor, h * 64 * T,
                                                   [[T, 64], [1, T]]), sem="kl")
            lq = b.dma("sync", qhb[h % 2], bass.AP(qT_d.tensor, h * 64 * T,
                                                   [[T, 64], [1, T]]), sem="kl")
            b.wait("vector", "T")
            b.op("vector", lambda e: e.memset(vaug, 1.0))
            b.op("vector", lambda e, h=h: e.tensor_copy(
                out=vaug[:].rearrange("p (a b) -> p a b", b=65)[:, :, 0:64],
                in_=c.scrb[:].rearrange("p (a b) -> p a b", b=1024)[:, :, h * 64:(h + 1) * 64]),
                sem="V")
            vv_vaug = b.val("V")
            b.wait("tensor", "kl", lq)
            b.wait("tensor", "V", vv_vaug)
            for lt in range(16):
                idx = h * 16 + lt
                band = [rt for rt in (lt - 1, lt, lt + 1) if 0 <= rt <= 15]
                left = list(range(0, lt - 1))
                right = list(range(lt + 2, 16))
                band0, W = band[0], len(band) * 128
                Pt = PtB[0]
                Pband = PbandB[0]
                b.wait("sync", "V")
                b.dma("sync", B_b[:, 0:W],
                      bass.AP(qd_d.tensor,
                              h * T * 512 + lt * 128 * 512 + band0 * 128 - lt * 128 + 255,
                              [[511, 128], [1, W]]), sem="bb")
                b.wait("tensor", "S")
                b.wait("tensor", "V")
                for rt in left + right:
                    b.op("tensor", lambda e, rt=rt, lt=lt, h=h: e.matmul(
                        out=c.psA[:, rt * 128:(rt + 1) * 128],
                        lhsT=khb[h % 2][0:64, rt * 128:(rt + 1) * 128],
                        rhs=qhb[h % 2][0:64, lt * 128:(lt + 1) * 128],
                        start=True, stop=True))
                b.op("tensor", lambda e, lt=lt, h=h, band0=band0, W=W: e.matmul(
                    out=c.psB[:, 1536:1536 + W],
                    lhsT=qhb[h % 2][0:64, lt * 128:(lt + 1) * 128],
                    rhs=khb[h % 2][0:64, band0 * 128:band0 * 128 + W],
                    start=True, stop=True), sem="T")
                tv = b.val("T")
                b.wait("scalar", "T", tv)
                segs = [s for s in (left, right) if s]
                for i, seg in enumerate(segs):
                    n = len(seg)
                    b.op("scalar", lambda e, seg=seg, n=n, idx=idx: e.activation(
                        out=Pt[:, seg[0] * 128:(seg[0] + n) * 128],
                        in_=c.psA[:, seg[0] * 128:(seg[0] + n) * 128],
                        func=AF.Exp, bias=mnegall[:, idx:idx + 1], scale=SCALE0),
                        sem="S" if i == len(segs) - 1 else None)
                b.wait("vector", "T", tv)
                b.wait("vector", "bb", b.val("bb"))
                b.op("vector", lambda e, W=W: e.tensor_copy(
                    out=c.tf32[:, 256:256 + W], in_=B_b[:, 0:W]))
                b.op("vector", lambda e, W=W: e.tensor_tensor(
                    out=c.tf32[:, 640:640 + W], in0=c.psB[:, 1536:1536 + W],
                    in1=c.tf32[:, 256:256 + W], op=ALU.add), sem="V")
                vv = b.val("V")
                b.wait("scalar", "V", vv)
                b.op("scalar", lambda e, W=W, idx=idx: e.activation(
                    out=Pband[:, 0:W], in_=c.tf32[:, 640:640 + W], func=AF.Exp,
                    bias=mnegall[:, idx:idx + 1], scale=SCALE0), sem="S")
                sv = b.val("S")
                b.wait("tensor", "S", sv)
                for i, rt in enumerate(band):
                    b.op("tensor", lambda e, i=i: e.transpose(
                        out=c.psB[:, i * 512:i * 512 + 64].bitcast(BF16),
                        in_=Pband[:, i * 128:(i + 1) * 128], identity=c.ident[:]),
                        sem="T" if i == len(band) - 1 else None)
                tv = b.val("T")
                b.wait("vector", "T", tv)
                for i, rt in enumerate(band):
                    b.op("vector", lambda e, i=i, rt=rt: e.tensor_copy(
                        out=Pt[:, rt * 128:(rt + 1) * 128],
                        in_=c.psB[:, i * 512:i * 512 + 64].bitcast(BF16)),
                        sem="V" if i == len(band) - 1 else None)
                vv = b.val("V")
                b.wait("tensor", "V", vv)
                regions = [r for r in (left, band, right) if r]
                ridx = {id(left): 0, id(band): 1, id(right): 2}
                nmm = sum(len(r) for r in regions)
                done = 0
                for reg in regions:
                    ri = ridx[id(reg)]
                    for j, rt in enumerate(reg):
                        done += 1
                        b.op("tensor", lambda e, rt=rt, ri=ri, j=j, reg=reg: e.matmul(
                            out=c.psB[0:128, 1536 + ri * 128:1536 + ri * 128 + 65],
                            lhsT=Pt[:, rt * 128:(rt + 1) * 128],
                            rhs=vaug[:, rt * 65:(rt + 1) * 65],
                            start=(j == 0), stop=(j == len(reg) - 1)),
                            sem="T" if done == nmm else None)
                tv = b.val("T")
                b.wait("vector", "T", tv)
                b.wait("scalar", "V")
                idx2 = idx * 2
                acts = []
                if left:
                    acts.append((e0c, idx2))
                if right:
                    acts.append((e72c, idx2 + 1))
                for i, (tgt, col) in enumerate(acts):
                    b.op("scalar", lambda e, tgt=tgt, col=col: e.activation(
                        out=tgt[:], in_=qd02[:, col:col + 1], func=AF.Exp,
                        scale=SCALE0), sem="S" if i == len(acts) - 1 else None)
                sv = b.val("S")
                b.wait("vector", "S", sv)
                b.op("vector", lambda e: e.tensor_copy(
                    out=acc[:], in_=c.psB[:, 1664:1729]))
                if left:
                    b.op("vector", lambda e: e.scalar_tensor_tensor(
                        out=acc[:], in0=c.psB[:, 1536:1601], scalar=e0c[:], in1=acc[:],
                        op0=ALU.mult, op1=ALU.add))
                if right:
                    b.op("vector", lambda e: e.scalar_tensor_tensor(
                        out=acc[:], in0=c.psB[:, 1792:1857], scalar=e72c[:], in1=acc[:],
                        op0=ALU.mult, op1=ALU.add))
                b.op("vector", lambda e: e.reciprocal(out=rcol[:], in_=acc[:, 64:65]))
                b.wait("vector", "ao")
                b.op("vector", lambda e: e.tensor_scalar(
                    out=aob[:], in0=acc[:, 0:64], scalar1=rcol[:], scalar2=None,
                    op0=ALU.mult), sem="V")
                vv = b.val("V")
                b.wait("sync", "V", vv)
                b.dma("sync", bass.AP(ao_d.tensor, lt * 128 * D + h * 64,
                                      [[D, 128], [1, 64]]), aob[:], sem="ao")
            tend[h] = b.val("T")

        b.wait("sync", "ao", b.val("ao"))
        wop = None
        b.wait("sync", "T")
        for lt in range(16):
            b.wait("sync", "V")
            b.wait("sync", "T")
            la = b.dma("sync", aoL, ao_d[lt], sem="aol")
            b.wait("tensor", "aol", la)
            b.wait("tensor", "V")
            for k in range(8):
                b.op("tensor", lambda e, k=k: e.transpose(
                    out=c.psB[:, (k % 4) * 512:(k % 4) * 512 + 64].bitcast(BF16),
                    in_=aoL[:, k * 128:(k + 1) * 128], identity=c.ident[:]),
                    sem="T" if k % 4 == 3 else None)
                if k % 4 == 3:
                    tv = b.val("T")
                    b.wait("vector", "T", tv)
                    for k2 in range(4):
                        kk = (k // 4) * 4 + k2
                        b.op("vector", lambda e, kk=kk, k2=k2: e.tensor_copy(
                            out=aoT_b[:, kk * 128:(kk + 1) * 128],
                            in_=c.psB[:, k2 * 512:k2 * 512 + 64].bitcast(BF16)),
                            sem="V" if k2 == 3 else None)
                    vv = b.val("V")
                    b.wait("tensor", "V", vv)
            for k in range(8):
                if wop is not None and k >= 2:
                    b.wait("sync", "T", wop)
                b.wait("sync", "V")
                b.dma("sync", wb2[k % 2][:],
                      bass.AP(WoT.tensor, k * 128 * D, [[D, 128], [1, D]]), sem=wsem)
                b.wait("tensor", wsem, b.val(wsem))
                for half in range(2):
                    b.op("tensor", lambda e, k=k, half=half: e.matmul(
                        out=c.psA[:, half * 512:(half + 1) * 512],
                        lhsT=aoT_b[:, k * 128:(k + 1) * 128],
                        rhs=wb2[k % 2][:, half * 512:(half + 1) * 512],
                        start=(k == 0), stop=(k == 7)),
                        sem="T" if half == 1 else None)
                wop = b.val("T")
            tv = b.val("T")
            b.wait("vector", "T", tv)
            b.op("vector", lambda e: e.tensor_copy(out=wo_row[:], in_=c.psA[:, 0:1024]),
                 sem="V")
            vv = b.val("V")
            b.wait("tensor", "V", vv)
            for k in range(8):
                b.op("tensor", lambda e, k=k: e.transpose(
                    out=c.psB[:, (k % 4) * 512:(k % 4) * 512 + 64].bitcast(BF16),
                    in_=wo_row[:, k * 128:(k + 1) * 128], identity=c.ident[:]),
                    sem="T" if k % 4 == 3 else None)
                if k % 4 == 3:
                    tv = b.val("T")
                    b.wait("vector", "T", tv)
                    for k2 in range(4):
                        dch = (k // 4) * 4 + k2
                        xs = c.x[:, dch * T + lt * 128:dch * T + (lt + 1) * 128]
                        b.op("vector", lambda e, dch=dch, k2=k2, xs=xs: e.scalar_tensor_tensor(
                            out=xs, in0=c.psB[:, k2 * 512:k2 * 512 + 64].bitcast(BF16),
                            scalar=bot[:, dch:dch + 1], in1=xs,
                            op0=ALU.add, op1=ALU.add),
                            sem="V" if k2 == 3 else None)
                    vv = b.val("V")
                    b.wait("tensor", "V", vv)


def stage_l0attn(inputs):
    import ml_dtypes
    d = {}
    for n in ("Wq", "Wk"):
        w = np.asarray(inputs[f"L0_attn_{n}"], dtype=np.float32)
        d[f"L0_attn_{n}pm"] = pack_pm(np.ascontiguousarray(w.T))
    d["L0_attn_WvT"] = np.ascontiguousarray(
        np.asarray(inputs["L0_attn_Wv"], dtype=np.float32).T)
    d["L0_attn_WoT"] = np.ascontiguousarray(
        np.asarray(inputs["L0_attn_Wo"], dtype=np.float32).T)
    for n in ("bq", "bk", "bo"):
        d[f"L0_attn_{n}c"] = bias_cols(np.asarray(inputs[f"L0_attn_{n}"], dtype=np.float32))
    d["L0_attn_bvR"] = np.asarray(inputs["L0_attn_bv"], dtype=np.float32).reshape(1, D)
    de = np.asarray(inputs["L0_attn_dist_emb"], dtype=np.float32)
    assert de.shape == (73, 64), de.shape
    dT = np.zeros((128, 73), dtype=ml_dtypes.bfloat16)
    dT[0:64] = de.T.astype(ml_dtypes.bfloat16)
    dT[64:128] = de.T.astype(ml_dtypes.bfloat16)
    d["L0_attn_distT"] = dT
    return d



def pack_pm(w):
    I, O = w.shape
    return np.ascontiguousarray(
        w.reshape(I // 128, 128, O // 128, 128).transpose(2, 1, 0, 3).reshape(O // 128, 128, I))


def retile(w):
    I, O = w.shape
    return np.ascontiguousarray(
        w.reshape(I // 128, 128, O // 128, 128).transpose(2, 0, 1, 3))


def bias_cols(v):
    n = v.shape[0] // 128
    return np.ascontiguousarray(v.reshape(n, 128).T)


_CONSTS = None


def host_consts():
    global _CONSTS
    if _CONSTS is None:
        import ml_dtypes
        cs = {}
        cs["c_ident"] = np.eye(128).astype(ml_dtypes.bfloat16)
        cs["c_identf"] = np.eye(128, dtype=np.float32)
        cs["c_iota_row"] = np.broadcast_to(np.arange(T, dtype=np.float32), (128, T)).copy()
        cs["c_iota_col1"] = (np.arange(128, dtype=np.float32) + 1).reshape(128, 1)
        ss = np.zeros((24, 24), np.float32)
        for e in range(NE):
            ss[e * NR, :] = 1.0
        cs["c_sumsel24"] = ss.astype(ml_dtypes.bfloat16)
        cs["c_tri"] = np.triu(np.ones((128, 128), np.float32)).astype(ml_dtypes.bfloat16)
        _CONSTS = cs
    return _CONSTS


def stage_l1ffn(inputs):
    d = {}
    d["L1_ffn_W1t"] = pack_pm(np.asarray(inputs["L1_ffn_W1"], dtype=np.float32))
    d["L1_ffn_A1p"] = np.ascontiguousarray(
        np.asarray(inputs["L1_ffn_A1"]).transpose(1, 0, 2).reshape(D, 24))
    d["L1_ffn_B1p"] = np.asarray(inputs["L1_ffn_B1"]).reshape(24, HFF1)
    d["L1_ffn_R1x"] = np.repeat(np.asarray(inputs["L1_ffn_R1"]), NR, axis=1)
    d["L1_ffn_b1c"] = bias_cols(np.asarray(inputs["L1_ffn_b1"]))
    d["L1_ffn_W2t"] = pack_pm(np.asarray(inputs["L1_ffn_W2"], dtype=np.float32))
    d["L1_ffn_A2p"] = np.ascontiguousarray(
        np.asarray(inputs["L1_ffn_A2"]).transpose(1, 0, 2).reshape(HFF1, 24))
    d["L1_ffn_B2p"] = np.asarray(inputs["L1_ffn_B2"]).reshape(24, D)
    d["L1_ffn_R2x"] = np.repeat(np.asarray(inputs["L1_ffn_R2"]), NR, axis=1)
    d["L1_ffn_b2c"] = bias_cols(np.asarray(inputs["L1_ffn_b2"]))
    return d



_PROGRAMS = {}


def build_single(emitter, tag):
    b = Bld()
    c = setup_common(b)
    emitter(b, c, tag)
    finish_store(b, c)
    nc = b.build()
    return nc, list(b.inputs)


def run_block(key, emitter, tag, x_percore, staged):
    if key not in _PROGRAMS:
        _PROGRAMS[key] = build_single(emitter, tag)
    nc, names = _PROGRAMS[key]
    cs = host_consts()
    in_maps = []
    for ci in range(NCORES):
        m = {}
        for n in names:
            if n == "xio_in":
                m[n] = x_percore[ci]
            elif n in cs:
                m[n] = cs[n]
            else:
                m[n] = staged[n]
        in_maps.append(m)
    res = run_bass_kernel_spmd(nc, in_maps, core_ids=list(range(NCORES)))
    return [r["xio_out"] for r in res.results]


def x_to_percore(x):
    out = []
    for ci in range(NCORES):
        xc = np.ascontiguousarray(np.asarray(x)[ci]).T
        out.append(np.ascontiguousarray(xc.reshape(NCH, 128, T), dtype=np.float32))
    return out


def percore_to_x(lst):
    outs = []
    for ci in range(NCORES):
        xf = lst[ci].reshape(D, T)
        outs.append(xf.T)
    return np.stack(outs).astype(np.float32)


def block_barrier(b):
    for eng in ENGINES:
        for s in ("T", "V", "S", "G"):
            if b.cnt.get(s, 0):
                b.wait(eng, s)
        for s in sorted(b.dma_sems):
            if b.cnt.get(s, 0):
                b.wait(eng, s)


FUSED_SEQ = ((emit_l0ffn, "f0A"), (emit_l0attn, "a0"), (emit_l0ffn, "f0B"),
             (emit_l1ffn, "f1A"), (emit_l1attn, "a1"), (emit_l1ffn, "f1B"))


def build_fused():
    b = Bld()
    c = setup_common(b)
    for i, (em, tg) in enumerate(FUSED_SEQ):
        if i:
            block_barrier(b)
        em(b, c, tg)
    finish_store(b, c)
    nc = b.build()
    return nc, list(b.inputs)


def kernel(**inputs):
    staged = {}
    staged.update(stage_l0ffn(inputs))
    staged.update(stage_l0attn(inputs))
    staged.update(stage_l1ffn(inputs))
    staged.update(stage_l1attn(inputs))
    x = x_to_percore(inputs["L0_attn_hidden_states"])
    if os.environ.get("UNFUSED") == "1":
        x = run_block("l0ffn", emit_l0ffn, "f0", x, staged)
        x = run_block("l0attn", emit_l0attn, "a0", x, staged)
        x = run_block("l0ffn", emit_l0ffn, "f0", x, staged)
        x = run_block("l1ffn", emit_l1ffn, "f1", x, staged)
        x = run_block("l1attn", emit_l1attn, "a1", x, staged)
        x = run_block("l1ffn", emit_l1ffn, "f1", x, staged)
        return percore_to_x(x)
    if "fused" not in _PROGRAMS:
        _PROGRAMS["fused"] = build_fused()
    nc, names = _PROGRAMS["fused"]
    cs = host_consts()
    in_maps = []
    for ci in range(NCORES):
        m = {}
        for n in names:
            if n == "xio_in":
                m[n] = x[ci]
            elif n in cs:
                m[n] = cs[n]
            else:
                m[n] = staged[n]
        in_maps.append(m)
    res = run_bass_kernel_spmd(nc, in_maps, core_ids=list(range(NCORES)))
    return percore_to_x([r["xio_out"] for r in res.results])
